# version 07
# baseline (speedup 1.0000x reference)
import numpy as np
import ml_dtypes
import concourse.bass as bass
import concourse.mybir as mybir
from concourse.bass_utils import run_bass_kernel_spmd
from contextlib import ExitStack

F32 = mybir.dt.float32
F32R = mybir.dt.float32r
BF16 = mybir.dt.bfloat16
AF = mybir.ActivationFunctionType
AX = mybir.AxisListType
ALU = mybir.AluOpType
bfloat16 = ml_dtypes.bfloat16

NCORES = 8
BPC = 4
S = 2048
E = 1024
H = 4096
D = 64
KC = 8
C32 = 32
NTT = S // 128
TW = 128
NTW = S // TW
FW = 256
SCALE_QK = float(64 ** -0.25)


def _stage_lo_T(wstages):
    M = np.zeros((32, 7, 128, 128), np.float32)
    for st in range(7):
        s = 1 << st
        w = wstages[st]
        ig = np.arange(4096)
        ig = ig[(ig & s) == 0]
        widx = (ig // (2 * s)) * s + (ig % s)
        c, i = np.divmod(ig, 128)
        i2 = i + s
        M[c, st, i, i] = w[widx, 0, 0]
        M[c, st, i2, i] = w[widx, 0, 1]
        M[c, st, i, i2] = w[widx, 1, 0]
        M[c, st, i2, i2] = w[widx, 1, 1]
    return np.ascontiguousarray(M.transpose(0, 2, 1, 3).reshape(32, 128, 7 * 128))


def _stage_hi_T(wstages):
    A = np.zeros((128, 5, 32, 32), np.float32)
    for st in range(7, 12):
        s = 1 << st
        sc = s // 128
        w = wstages[st]
        ig = np.arange(4096)
        ig = ig[(ig & s) == 0]
        widx = (ig // (2 * s)) * s + (ig % s)
        ci, p = np.divmod(ig, 128)
        ci2 = ci + sc
        j, pg = np.divmod(p, 4)
        st5 = st - 7
        A[32 * pg + ci, st5, j, ci] = w[widx, 0, 0]
        A[32 * pg + ci2, st5, j, ci] = w[widx, 0, 1]
        A[32 * pg + ci, st5, j, ci2] = w[widx, 1, 0]
        A[32 * pg + ci2, st5, j, ci2] = w[widx, 1, 1]
    return A


def _host_consts():
    r = np.arange(128)[:, None]
    u = np.arange(896)[None, :]
    mask = (u <= r + 384).astype(bfloat16)
    identf = np.eye(128, dtype=np.float32)
    identb = np.eye(128, dtype=bfloat16)
    i32q = np.zeros((128, 32), np.float32)
    for pg in range(4):
        i32q[32 * pg:32 * (pg + 1)] = np.eye(32, dtype=np.float32)
    ones_r = np.ones((1, 128), np.float32)
    return mask, identf, identb, i32q, ones_r


class Plan:
    def __init__(self, nc, ctx):
        self.nc = nc
        self.ctx = ctx
        self.ops = []
        self.semh = {}
        self.cnt = {}
        self.on = True

    def sem(self, name):
        self.semh[name] = self.ctx.enter_context(self.nc.semaphore(name))
        self.cnt[name] = 0
        return name

    def op(self, eng, f):
        if not self.on:
            return
        self.ops.append((eng, f))

    def opi(self, eng, f, sem, k=1):
        if not self.on:
            return self.cnt[sem]
        h = self.semh[sem]
        self.cnt[sem] += k
        self.ops.append((eng, lambda e, f=f, h=h, k=k: f(e).then_inc(h, k)))
        return self.cnt[sem]

    def wait(self, eng, sem, val):
        if val <= 0 or not self.on:
            return
        h = self.semh[sem]
        self.ops.append((eng, lambda e, h=h, v=val: e.wait_ge(h, v)))

    def waitc(self, eng, sem):
        self.wait(eng, sem, self.cnt[sem])

    def replay(self):
        nc = self.nc
        ops = self.ops
        with nc.Block() as block:
            @block.tensor
            def _(t):
                for e, f in ops:
                    if e == 'T':
                        f(t)

            @block.vector
            def _(v):
                for e, f in ops:
                    if e == 'V':
                        f(v)

            @block.scalar
            def _(c):
                for e, f in ops:
                    if e == 'C':
                        f(c)

            @block.sync
            def _(s):
                for e, f in ops:
                    if e == 'S':
                        f(s)

            @block.gpsimd
            def _(g):
                for e, f in ops:
                    if e == 'G':
                        f(g)


_PH = {'loads': 0, 'bhichain': 1, 'blochain': 2, 'upfold': 3, 'downfold': 4,
       'build': 5, 'ln': 6, 'proj': 7, 'attn': 8, 'full': 9, 'dbg': 9}


def build_nc(limit='full'):
    _rank = _PH[limit]
    nc = bass.Bass("TRN2", target_bir_lowering=False, dynamic_dma_scratch_size=2048)
    ctx = ExitStack()

    dt_ = nc.dram_tensor
    x_d = dt_("x", [BPC, S, E], F32, kind="ExternalInput").ap()
    out_d = dt_("out", [BPC, S, E], F32, kind="ExternalOutput").ap()
    wq_d = dt_("wq", [128, KC * D], F32, kind="ExternalInput").ap()
    wk_d = dt_("wk", [128, KC * D], F32, kind="ExternalInput").ap()
    wv1_d = dt_("wv1", [128, KC * D], F32, kind="ExternalInput").ap()
    wv2_d = dt_("wv2", [D, E], F32, kind="ExternalInput").ap()
    wexp_d = dt_("wexp", [E, H], F32, kind="ExternalInput").ap()
    wcon_d = dt_("wcon", [H, E], F32, kind="ExternalInput").ap()
    bexp_d = dt_("bexp", [1, H], F32, kind="ExternalInput").ap()
    ubias_d = dt_("ubias", [1, H], F32, kind="ExternalInput").ap()
    bcon_d = dt_("bcon", [1, E], F32, kind="ExternalInput").ap()
    dbias_d = dt_("dbias", [1, E], F32, kind="ExternalInput").ap()
    mloU_d = dt_("mloU", [32, 128, 7 * 128], F32, kind="ExternalInput").ap()
    mloD_d = dt_("mloD", [32, 128, 7 * 128], F32, kind="ExternalInput").ap()
    mhiU_d = dt_("mhiU", [128, 5 * 32 * 32], F32, kind="ExternalInput").ap()
    mhiD_d = dt_("mhiD", [128, 5 * 32 * 32], F32, kind="ExternalInput").ap()
    mask_d = dt_("maskc", [128, 896], BF16, kind="ExternalInput").ap()
    identf_d = dt_("identf", [128, 128], F32, kind="ExternalInput").ap()
    identb_d = dt_("identb", [128, 128], BF16, kind="ExternalInput").ap()
    i32q_d = dt_("i32q", [128, 32], F32, kind="ExternalInput").ap()
    ones_d = dt_("onesr", [1, 128], F32, kind="ExternalInput").ap()
    swz_d = dt_("swz", [H, FW], F32R).ap()
    swzb_d = dt_("swzb", [H, 2], F32R).ap()
    wcpg_d = dt_("wcpg", [H, E], F32R).ap()
    v2h_d = dt_("v2h", [1, H], F32).ap()
    dbg_xn = dt_("dbg_xn", [128, KC * S], BF16, kind="ExternalOutput").ap()
    dbg_q = dt_("dbg_q", [64, S], BF16, kind="ExternalOutput").ap()
    dbg_k = dt_("dbg_k", [64, S], BF16, kind="ExternalOutput").ap()
    dbg_vt = dt_("dbg_vt", [128, NTT * D], BF16, kind="ExternalOutput").ap()
    dbg_o = dt_("dbg_o", [64, S], BF16, kind="ExternalOutput").ap()
    dbg_p = dt_("dbg_p", [128, S], BF16, kind="ExternalOutput").ap()
    dbg_a = dt_("dbg_a", [128, C32 * TW], BF16, kind="ExternalOutput").ap()
    blod_sp = dt_("blod_sp", [128, C32 * 128], F32R).ap()
    bhid_sp = dt_("bhid_sp", [128, C32 * 128], F32R).ap()

    sbm = lambda cx, name, shape, dtype: cx.enter_context(nc.sbuf_tensor(name, shape, dtype))
    sbr = lambda cx, name, shape, dtype: cx.enter_context(nc.sbuf_tensor(name, shape, dtype, side="right"))
    psm = lambda cx, name, shape, dtype: cx.enter_context(nc.psum_tensor(name, shape, dtype))

    wup = sbm(ctx, "wup", [128, KC * H], BF16)
    maskb = sbm(ctx, "maskb", [128, 896], BF16)
    idb = sbm(ctx, "idb", [128, 128], BF16)
    biasp = sbm(ctx, "biasp", [128, 32], F32)
    bsbc = sbm(ctx, "bsbc", [128, E], F32)
    wqsb = sbm(ctx, "wqsb", [128, KC * D], BF16)
    wksb = sbm(ctx, "wksb", [128, KC * D], BF16)
    wv1sb = sbm(ctx, "wv1sb", [128, KC * D], BF16)
    wv2sb = sbm(ctx, "wv2sb", [64, E], BF16)
    diagq = sbm(ctx, "diagq", [128, 128], BF16)
    lnv = sbm(ctx, "lnv", [128, 16], F32)
    attv = sbm(ctx, "attv", [128, 16], F32)

    pb = [psm(ctx, f"pb{i}", [128, 512], F32) for i in range(6)]
    pT = [psm(ctx, f"pT{i}", [128, 1024], BF16) for i in range(2)]
    pTf = [b[:].bitcast(F32) for b in pT]

    P = Plan(nc, ctx)
    for s_ in ["bld", "bmm", "bev", "bswz", "bg",
               "sx", "ssq", "sst", "sxn", "stt",
               "spj", "ssc", "smx", "sex", "sdg", "spt", "spte", "sot",
               "sup", "sgl", "sdn", "soev", "sxr", "sost"]:
        P.sem(s_)

    P.opi('S', lambda e: e.dma_start(out=idf[:], in_=identf_d[:]), "bld", 16)
    P.opi('S', lambda e: e.dma_start(out=idb[:], in_=identb_d[:]), "bld", 16)
    P.opi('S', lambda e: e.dma_start(out=maskb[:], in_=mask_d[:]), "bld", 16)
    P.opi('S', lambda e: e.dma_start(out=onesr[:], in_=ones_d[:]), "bld", 16)
    P.opi('G', lambda e: e.dma_start(out=wqsb[:], in_=wq_d[:], max_dma_last_dim=4096), "bg", 16)
    P.opi('G', lambda e: e.dma_start(out=wksb[:], in_=wk_d[:], max_dma_last_dim=4096), "bg", 16)
    P.opi('G', lambda e: e.dma_start(out=wv1sb[:], in_=wv1_d[:], max_dma_last_dim=4096), "bg", 16)
    P.opi('G', lambda e: e.dma_start(out=wv2sb[:], in_=wv2_d[:], max_dma_last_dim=4096), "bg", 16)
    for pg in range(4):
        for jh in range(2):
            src = bass.AP(wcon_d.tensor, (pg + 4 * 16 * jh) * E, [[4 * E, 16], [128 * E, 32], [1, E]])
            dst = bass.AP(wcpg_d.tensor, (32 * pg) * E + (16 * jh) * 128 * E, [[128 * E, 16], [E, 32], [1, E]])
            P.opi('G', lambda e, s=src, d=dst: e.dma_start(out=d, in_=s, max_dma_last_dim=4096), "bg", 16)

    cxA = ExitStack()
    idf = sbr(cxA, "idf", [128, 128], F32)
    onesr = sbr(cxA, "onesr_sb", [1, 128], F32)
    bloUT = sbr(cxA, "bloUT", [128, C32 * 128], F32R)
    bhiUblk = sbr(cxA, "bhiUblk", [128, C32 * 128], F32R)
    cx1c = ExitStack()
    bloD = sbr(cx1c, "bloD", [128, C32 * 128], F32R)
    bhiDblk = sbr(cx1c, "bhiDblk", [128, C32 * 128], F32R)
    mhisb = sbr(cx1c, "mhisb", [128, 5 * 32 * 32], F32)
    mhisb2 = sbr(cx1c, "mhisb2", [128, 5 * 32 * 32], F32)
    i32qsb = sbr(cx1c, "i32qsb", [128, 32], F32)
    mlo2 = [sbr(cx1c, f"mlo2_{i}", [128, 7 * 128], F32) for i in range(2)]
    chb = [sbr(cx1c, f"chb{i}", [128, 128], F32) for i in range(2)]
    cqb = sbr(cx1c, "cqb", [128, 32], F32)
    cqT = sbr(cx1c, "cqT", [128, 32], F32)

    P.opi('S', lambda e: e.dma_start(out=mhisb[:], in_=mhiU_d[:]), "bld", 16)
    P.opi('S', lambda e: e.dma_start(out=mhisb2[:], in_=mhiD_d[:]), "bld", 16)
    P.opi('S', lambda e: e.dma_start(out=i32qsb[:], in_=i32q_d[:]), "bld", 16)
    P.opi('V', lambda e: e.memset(lnv[:, 15:16], 1e-5), "bev", 1)
    P.opi('V', lambda e: e.memset(bhiUblk[:].bitcast(F32), 0.0), "bev", 1)
    P.opi('V', lambda e: e.memset(bhiDblk[:].bitcast(F32), 0.0), "bev", 1)

    P.on = _rank >= 1
    P.waitc('T', "bld")
    for bt in range(2):
        msrc = mhisb if bt == 0 else mhisb2
        dblk = bhiUblk if bt == 0 else bhiDblk
        for j in range(C32):
            P.waitc('V', "bmm")
            P.opi('V', lambda e: e.tensor_copy(cqb[:], i32qsb[:]), "bev", 1)
            P.waitc('T', "bev")
            for st in range(5):
                dst_ps = pb[st % 2]
                for pg in range(4):
                    off = st * 32 * 32 + j * 32
                    P.opi('T', lambda e, d=dst_ps, m=msrc, o=off, pg=pg: e.matmul(
                        d[32 * pg:32 * (pg + 1), 0:32],
                        bass.AP(m, (32 * pg) * (5 * 32 * 32) + o, [[5 * 32 * 32, 32], [1, 32]]),
                        cqb[32 * pg:32 * (pg + 1), :],
                        start=True, stop=True,
                        tile_position=(32 * pg, 32 * pg)), "bmm", 1)
                P.waitc('V', "bmm")
                P.opi('V', lambda e, d=dst_ps: e.tensor_copy(cqb[:], d[:, 0:32]), "bev", 1)
                P.waitc('T', "bev")
            if bt == 0:
                P.opi('V', lambda e: e.transpose(cqT[:], cqb[:]), "bev", 1)
                P.waitc('V', "bev")
                for pg in range(4):
                    P.opi('V', lambda e, j=j, pg=pg, d=dblk: e.tensor_copy(
                        d[32 * pg:32 * (pg + 1), 128 * j + 32 * pg:128 * j + 32 * (pg + 1)],
                        cqT[32 * pg:32 * (pg + 1), :]), "bev", 1)
            else:
                for pg in range(4):
                    P.opi('V', lambda e, j=j, pg=pg, d=dblk: e.tensor_copy(
                        d[32 * pg:32 * (pg + 1), 128 * j + 32 * pg:128 * j + 32 * (pg + 1)],
                        cqb[32 * pg:32 * (pg + 1), :]), "bev", 1)

    import os as _os
    _nblo = int(_os.environ.get("KBLO", "99"))
    _bts = [int(x) for x in _os.environ.get("KBT", "01")]
    P.on = _rank >= 2
    for bt in _bts:
        mlo_src = mloU_d if bt == 0 else mloD_d
        for c in range(C32):
            if c >= _nblo:
                continue
            buf = mlo2[c % 2]
            idx = bt * C32 + c
            if idx >= 2:
                P.wait('S', "bmm", P.cnt["bmm"] - 7)
            P.opi('S', lambda e, b=buf, src=mlo_src, c=c: e.dma_start(
                out=b[:], in_=src[c]), "bld", 16)
            P.waitc('T', "bld")
            for st in range(7):
                dst_ps = pb[2 + st % 2]
                rhs = idf[:] if st == 0 else chb[(st - 1) % 2][:]
                P.opi('T', lambda e, d=dst_ps, b=buf, st=st, r=rhs: e.matmul(
                    d[:, 0:128], b[:, 128 * st:128 * (st + 1)], r, start=True, stop=True), "bmm", 1)
                if st < 6:
                    P.waitc('V', "bmm")
                    P.opi('V', lambda e, d=chb[st % 2], s=dst_ps: e.tensor_copy(d[:], s[:, 0:128]), "bev", 1)
                    P.waitc('T', "bev")
            P.waitc('V', "bmm")
            if bt == 1:
                P.opi('V', lambda e, c=c, s=pb[2]: e.tensor_copy(bloD[:, 128 * c:128 * (c + 1)], s[:, 0:128]), "bev", 1)
            else:
                P.opi('V', lambda e, s=pb[2]: e.tensor_copy(chb[0][:], s[:, 0:128]), "bev", 1)
                P.waitc('T', "bev")
                P.opi('T', lambda e: e.transpose(pb[4][:, 0:128], chb[0][:], idf[:]), "bmm", 1)
                P.waitc('V', "bmm")
                P.opi('V', lambda e, c=c: e.tensor_copy(bloUT[:, 128 * c:128 * (c + 1)], pb[4][:, 0:128]), "bev", 1)
            P.waitc('T', "bev")

    P.waitc('S', "bev")
    P.opi('S', lambda e: e.dma_start(out=blod_sp[:], in_=bloD[:]), "bswz", 16)
    P.opi('S', lambda e: e.dma_start(out=bhid_sp[:], in_=bhiDblk[:]), "bswz", 16)
    P.waitc('V', "bswz")
    P.waitc('T', "bswz")
    P.waitc('S', "bswz")
    cx1c.close()

    P.on = _rank >= 3
    cx2 = ExitStack()
    wx = sbr(cx2, "wx", [128, H], F32)
    wtb = [sbr(cx2, f"wtb{i}", [128, FW], F32R) for i in range(2)]
    p1t = sbr(cx2, "p1t", [128, C32 * FW], F32R)
    p1s = sbr(cx2, "p1s", [128, C32 * FW], F32R)
    p2tb = [sbr(cx2, f"p2tb{i}", [128, FW], F32R) for i in range(2)]
    rowsb = sbr(cx2, "rowsb", [1, H], F32)
    bvec = sbr(cx2, "bvec", [128, 2 * C32], F32R)
    p1tbia = sbr(cx2, "p1tbia", [128, C32 * 2], F32R)
    p1sbia = sbr(cx2, "p1sbia", [128, C32 * 2], F32R)

    P.opi('S', lambda e: e.dma_start(out=rowsb[:], in_=bexp_d[:]), "bld", 16)
    P.waitc('T', "bld")
    for c in range(C32):
        P.opi('T', lambda e, c=c: e.matmul(
            pb[3][:, c:c + 1], rowsb[0:1, 128 * c:128 * (c + 1)], onesr[0:1, 0:1],
            start=True, stop=True), "bmm", 1)
    P.waitc('V', "bmm")
    P.opi('V', lambda e: e.memset(bvec[:].bitcast(F32), 0.0), "bev", 1)
    P.waitc('V', "bev")
    P.opi('V', lambda e: e.tensor_copy(
        bass.AP(bvec, 0, [[2 * C32, 128], [2, 32]]), pb[3][:, 0:32]), "bev", 1)
    P.waitc('T', "bev")
    for c in range(C32):
        P.opi('T', lambda e, c=c: e.matmul(
            pb[4][:, 2 * c:2 * (c + 1)], bloUT[:, 128 * c:128 * (c + 1)], bvec[:, 2 * c:2 * (c + 1)],
            start=True, stop=True), "bmm", 1)
    P.waitc('V', "bmm")
    P.opi('V', lambda e: e.tensor_copy(p1tbia[:], pb[4][:, 0:64]), "bev", 1)
    P.waitc('S', "bev")
    P.opi('S', lambda e: e.dma_start(
        out=bass.AP(swzb_d.tensor, 0, [[2, 128], [128 * 2, 32], [1, 2]]),
        in_=bass.AP(p1tbia, 0, [[C32 * 2, 128], [2, 32], [1, 2]])), "bswz", 16)
    P.waitc('S', "bswz")
    for pg in range(4):
        P.opi('S', lambda e, pg=pg: e.dma_start(
            out=bass.AP(p1sbia, (32 * pg) * (C32 * 2), [[C32 * 2, 32], [2, 32], [1, 2]]),
            in_=bass.AP(swzb_d.tensor, pg * 2, [[128 * 2, 32], [4 * 2, 32], [1, 2]])), "bswz", 16)
    P.waitc('T', "bswz")
    for j in range(C32):
        P.opi('T', lambda e, j=j: e.matmul(
            pb[5][:, 2 * j:2 * (j + 1)], bhiUblk[:, 128 * j:128 * (j + 1)],
            p1sbia[:, 2 * j:2 * (j + 1)], start=True, stop=True), "bmm", 1)
    P.waitc('V', "bmm")
    P.opi('V', lambda e: e.tensor_copy(p1sbia[:], pb[5][:, 0:64]), "bev", 1)
    P.waitc('S', "bev")
    for pg in range(4):
        P.opi('S', lambda e, pg=pg: e.dma_start(
            out=bass.AP(v2h_d.tensor, pg, [[128, 32], [4, 32], [1, 1]]),
            in_=bass.AP(p1sbia, (32 * pg) * 64, [[64, 32], [2, 32], [1, 1]]).bitcast(F32),
            allow_slow_non_contiguous=True), "bswz", 16)

    for w in range(4):
        for ki, kappa in enumerate([2 * w, 2 * w + 1]):
            P.wait('S', "bmm", P.cnt["bmm"])
            P.opi('S', lambda e, kappa=kappa: e.dma_start(out=wx[:], in_=wexp_d[128 * kappa:128 * (kappa + 1), :]), "bld", 16)
            P.waitc('T', "bld")
            for c in range(C32):
                P.opi('T', lambda e, ki=ki, c=c: e.transpose(
                    pTf[0][:, 128 * (c % 4):128 * (c % 4 + 1)], wx[:, 128 * c:128 * (c + 1)], idf[:]), "bmm", 1)
                if c % 4 == 3:
                    P.waitc('V', "bmm")
                    c0 = c - 3
                    P.opi('V', lambda e, ki=ki, c0=c0: e.tensor_copy(
                        bass.AP(p1t, FW * c0 + 128 * ki, [[C32 * FW, 128], [FW, 4], [1, 128]]),
                        pTf[0][:, 0:512]), "bev", 1)
                    P.waitc('T', "bev")
        for c in range(C32):
            P.waitc('T', "bev")
            P.opi('T', lambda e, c=c: e.matmul(
                pb[4][:, 0:FW], bloUT[:, 128 * c:128 * (c + 1)], p1t[:, FW * c:FW * (c + 1)],
                start=True, stop=True), "bmm", 1)
            P.waitc('V', "bmm")
            P.opi('V', lambda e, c=c: e.tensor_copy(p1t[:, FW * c:FW * (c + 1)], pb[4][:, 0:FW]), "bev", 1)
        P.waitc('S', "bev")
        P.opi('S', lambda e: e.dma_start(
            out=bass.AP(swz_d.tensor, 0, [[FW, 128], [128 * FW, 32], [1, FW]]),
            in_=p1t[:]), "bswz", 16)
        P.waitc('S', "bswz")
        P.wait('S', "bmm", P.cnt["bmm"])
        for pg in range(4):
            P.opi('S', lambda e, pg=pg: e.dma_start(
                out=bass.AP(p1s, (32 * pg) * (C32 * FW), [[C32 * FW, 32], [FW, 32], [1, FW]]),
            in_=bass.AP(swz_d.tensor, pg * FW, [[128 * FW, 32], [4 * FW, 32], [1, FW]])), "bswz", 16)
        P.waitc('T', "bswz")
        for j in range(C32):
            P.opi('T', lambda e, j=j: e.matmul(
                pb[5][:, 0:FW], bhiUblk[:, 128 * j:128 * (j + 1)], p1s[:, FW * j:FW * (j + 1)],
                start=True, stop=True), "bmm", 1)
            P.waitc('V', "bmm")
            p2b = p2tb[j % 2]
            if j >= 2:
                P.wait('V', "bmm", P.cnt["bmm"])
            P.opi('V', lambda e, d=p2b: e.tensor_copy(d[:], pb[5][:, 0:FW]), "bev", 1)
            P.waitc('T', "bev")
            for sub in range(2):
                P.opi('T', lambda e, d=p2b, sub=sub: e.transpose(
                    pTf[1][:, 128 * sub:128 * (sub + 1)], d[:].bitcast(F32)[:, 128 * sub:128 * (sub + 1)], idf[:]), "bmm", 1)
            P.waitc('V', "bmm")
            P.opi('V', lambda e, w=w, j=j: e.tensor_copy(
                bass.AP(wup, (2 * w) * H + 128 * j, [[KC * H, 128], [H, 2], [1, 128]]),
                pTf[1][:, 0:256]), "bev", 1)
            P.waitc('T', "bev")

    P.waitc('S', "bswz")
    P.wait('S', "bmm", P.cnt["bmm"])
    P.opi('S', lambda e: e.dma_start(out=rowsb[:], in_=v2h_d[:]), "bld", 16)
    P.waitc('T', "bld")
    for c in range(C32):
        P.opi('T', lambda e, c=c: e.matmul(
            pb[3][:, c:c + 1], rowsb[0:1, 128 * c:128 * (c + 1)], onesr[0:1, 0:1],
            start=True, stop=True), "bmm", 1)
    P.wait('S', "bmm", P.cnt["bmm"])
    P.opi('S', lambda e: e.dma_start(out=rowsb[:], in_=ubias_d[:]), "bld", 16)
    P.waitc('T', "bld")
    for c in range(C32):
        P.opi('T', lambda e, c=c: e.matmul(
            pb[4][:, c:c + 1], rowsb[0:1, 128 * c:128 * (c + 1)], onesr[0:1, 0:1],
            start=True, stop=True), "bmm", 1)
    P.waitc('V', "bmm")
    P.opi('V', lambda e: e.tensor_copy(biasp[:], pb[3][:, 0:32]), "bev", 1)
    P.waitc('V', "bev")
    P.opi('V', lambda e: e.tensor_add(biasp[:], biasp[:], pb[4][:, 0:32]), "bev", 1)
    P.wait('S', "bev", P.cnt["bev"])
    P.opi('S', lambda e: e.dma_start(out=rowsb[0:1, 0:E], in_=bcon_d[:]), "bld", 16)
    P.opi('S', lambda e: e.dma_start(out=rowsb[0:1, E:2 * E], in_=dbias_d[:]), "bld", 16)
    P.waitc('V', "bld")
    P.opi('V', lambda e: e.tensor_add(rowsb[0:1, 0:E], rowsb[0:1, 0:E], rowsb[0:1, E:2 * E]), "bev", 1)
    P.waitc('T', "bev")
    for hf in range(2):
        P.opi('T', lambda e, hf=hf: e.matmul(
            pb[5][:, 0:512], onesr[0:1, 0:128], rowsb[0:1, 512 * hf:512 * (hf + 1)],
            start=True, stop=True), "bmm", 1)
        P.waitc('V', "bmm")
        P.opi('V', lambda e, hf=hf: e.tensor_copy(bsbc[:, 512 * hf:512 * (hf + 1)], pb[5][:, 0:512]), "bev", 1)
        P.waitc('T', "bev")
    P.waitc('S', "bev")
    P.waitc('V', "bswz")
    cx2.close()
    cxA.close()

    P.on = _rank >= 4
    ldn = sbr(ctx, "ldn", [128, C32 * E], BF16)
    cx3 = ExitStack()
    bloD2 = sbr(cx3, "bloD2", [128, C32 * 128], F32R)
    bhiD2 = sbr(cx3, "bhiD2", [128, C32 * 128], F32R)
    upg = sbr(cx3, "upg", [128, C32 * FW], F32R)
    ucmh = sbr(cx3, "ucmh", [128, 16 * FW], F32R)
    wct = [sbr(cx3, f"wct{i}", [128, FW], F32R) for i in range(2)]

    P.opi('S', lambda e: e.dma_start(out=bloD2[:], in_=blod_sp[:]), "bld", 16)
    P.opi('S', lambda e: e.dma_start(out=bhiD2[:], in_=bhid_sp[:]), "bld", 16)
    P.waitc('S', "bg")
    for w in range(4):
        for j in range(C32):
            b = wct[j % 2]
            if w + j > 0:
                P.wait('S', "bmm", P.cnt["bmm"] - 1)
            P.opi('S', lambda e, b=b, j=j, w=w: e.dma_start(
                out=b[:], in_=bass.AP(wcpg_d.tensor, (128 * j) * E + FW * w, [[E, 128], [1, FW]])), "bld", 16)
            P.waitc('T', "bld")
            P.opi('T', lambda e, j=j, b=b: e.matmul(
                pb[4][:, 0:FW], bhiD2[:, 128 * j:128 * (j + 1)], b[:],
                start=True, stop=True), "bmm", 1)
            P.waitc('V', "bmm")
            P.opi('V', lambda e, j=j: e.tensor_copy(upg[:, FW * j:FW * (j + 1)], pb[4][:, 0:FW]), "bev", 1)
            P.waitc('T', "bev")
        P.waitc('S', "bev")
        P.wait('S', "bld", P.cnt["bld"])
        for pg in range(4):
            P.opi('S', lambda e, pg=pg: e.dma_start(
                out=bass.AP(swz_d.tensor, pg * FW, [[128 * FW, 32], [4 * FW, 32], [1, FW]]),
                in_=bass.AP(upg, (32 * pg) * (C32 * FW), [[C32 * FW, 32], [FW, 32], [1, FW]])), "bswz", 16)
        P.waitc('S', "bswz")
        for ch in range(2):
            P.wait('S', "bmm", P.cnt["bmm"])
            P.opi('S', lambda e, ch=ch: e.dma_start(
                out=bass.AP(ucmh, 0, [[16 * FW, 128], [FW, 16], [1, FW]]),
                in_=bass.AP(swz_d.tensor, (16 * ch) * 128 * FW, [[FW, 128], [128 * FW, 16], [1, FW]])), "bld", 16)
            P.waitc('T', "bld")
            for cc in range(16):
                c = 16 * ch + cc
                P.opi('T', lambda e, c=c, cc=cc: e.matmul(
                    pb[5][:, 0:FW], bloD2[:, 128 * c:128 * (c + 1)], ucmh[:, FW * cc:FW * (cc + 1)],
                    start=True, stop=True), "bmm", 1)
                P.waitc('V', "bmm")
                P.opi('V', lambda e, c=c, w=w: e.tensor_copy(
                    ldn[:, E * c + FW * w:E * c + FW * (w + 1)], pb[5][:, 0:FW]), "bev", 1)
                P.waitc('T', "bev")
    P.waitc('S', "bev")
    P.waitc('V', "bswz")
    cx3.close()

    xn = sbr(ctx, "xn", [128, KC * S], BF16)
    abuf = sbr(ctx, "abuf", [128, C32 * TW], BF16)
    qbuf = sbr(ctx, "qbuf", [64, S], BF16)
    kbuf = sbr(ctx, "kbuf", [64, S], BF16)
    o64t = sbr(ctx, "o64t", [64, S], BF16)
    vt = sbr(ctx, "vt", [128, NTT * D], BF16)
    pbuf = sbr(ctx, "pbuf", [128, S], BF16)
    ptb = sbr(ctx, "ptb", [128, 2 * 128], BF16)
    xtile = [sbr(ctx, f"xtile{i}", [128, E], F32) for i in range(2)]
    xnt = [sbr(ctx, f"xnt{i}", [128, E], BF16) for i in range(2)]
    ost = [sbr(ctx, f"ost{i}", [128, E], F32) for i in range(2)]

    P.on = True
    P.waitc('T', "bev")
    P.waitc('C', "bev")
    P.waitc('V', "bmm")
    P.waitc('C', "bld")
    P.waitc('C', "bg")
    P.waitc('T', "bg")

    nseq = 0 if _rank <= 5 else (1 if limit in ('ln', 'proj', 'attn', 'ffn1') else BPC)
    for b in range(nseq):
        P.wait('C', "sost", P.cnt["sost"])
        for i in range(NTT):
            sl = i % 2
            xt = xtile[sl]
            gi = b * NTT + i
            if gi >= 2:
                P.wait('S', "sxn", 2 * (gi - 2) + 1)
            if gi >= 2:
                P.wait('S', "ssq", P.cnt["ssq"] - 3)
            P.opi('S', lambda e, xt=xt, b=b, i=i: e.dma_start(
                out=xt[:], in_=x_d[b, 128 * i:128 * (i + 1), :]), "sx", 16)
            P.waitc('V', "sx")
            P.waitc('C', "sx")
            c0, c1, c2, c3, c5 = sl * 8, sl * 8 + 1, sl * 8 + 2, sl * 8 + 3, sl * 8 + 5
            P.opi('V', lambda e, xt=xt, c0=c0: e.reduce_sum(lnv[:, c0:c0 + 1], xt[:], axis=AX.X), "sst", 1)
            P.waitc('V', "sst")
            if gi >= 2:
                P.wait('C', "stt", P.cnt["stt"] - 8)
            P.opi('C', lambda e, xt=xt, sq=xnt[sl], c1=c1: e.activation(
                sq[:], xt[:], AF.Square, accum_out=lnv[:, c1:c1 + 1]), "ssq", 1)
            P.waitc('V', "ssq")
            P.opi('V', lambda e, c0=c0, c2=c2: e.tensor_scalar_mul(lnv[:, c2:c2 + 1], lnv[:, c0:c0 + 1], -1.0 / E), "sst", 1)
            P.opi('V', lambda e, c1=c1, c3=c3: e.tensor_scalar_mul(lnv[:, c3:c3 + 1], lnv[:, c1:c1 + 1], 1.0 / E), "sst", 1)
            P.waitc('V', "sst")
            c4 = sl * 8 + 4
            P.opi('V', lambda e, c2=c2, c4=c4: e.tensor_mul(lnv[:, c4:c4 + 1], lnv[:, c2:c2 + 1], lnv[:, c2:c2 + 1]), "sst", 1)
            P.waitc('V', "sst")
            P.opi('V', lambda e, c3=c3, c4=c4: e.tensor_sub(lnv[:, c3:c3 + 1], lnv[:, c3:c3 + 1], lnv[:, c4:c4 + 1]), "sst", 1)
            P.waitc('V', "sst")
            P.waitc('C', "sst")
            P.opi('C', lambda e, c3=c3, c5=c5: e.activation(lnv[:, c5:c5 + 1], lnv[:, c3:c3 + 1], AF.Sqrt, bias=lnv[:, 15:16]), "ssq", 1)
            P.waitc('V', "ssq")
            P.opi('V', lambda e, c5=c5: e.reciprocal(lnv[:, c5:c5 + 1], lnv[:, c5:c5 + 1]), "sst", 1)
            P.waitc('V', "sst")
            P.opi('V', lambda e, xt=xt, xo=xnt[sl], c2=c2, c5=c5: e.tensor_scalar(
                xo[:], xt[:], lnv[:, c2:c2 + 1], lnv[:, c5:c5 + 1], ALU.add, ALU.mult), "sxn", 1)
            P.waitc('T', "sxn")
            tb = pT[sl]
            for kappa in range(KC):
                P.opi('T', lambda e, tb=tb, xo=xnt[sl], kappa=kappa: e.transpose(
                    tb[:, 128 * kappa:128 * (kappa + 1)], xo[:, 128 * kappa:128 * (kappa + 1)], idb[:]), "stt", 1)
            P.waitc('V', "stt")
            P.opi('V', lambda e, tb=tb, i=i: e.tensor_copy(
                bass.AP(xn, 128 * i, [[KC * S, 128], [S, 8], [1, 128]]), tb[:]), "sxn", 1)
        if limit == 'ln':
            continue
        P.waitc('T', "sxn")
        for tw in range(4):
            for which, wsb_, dst in ((0, wqsb, qbuf), (1, wksb, kbuf), (2, wv1sb, None)):
                bank = pb[which]
                for kappa in range(KC):
                    P.opi('T', lambda e, bank=bank, w_=wsb_, kappa=kappa, tw=tw: e.matmul(
                        bank[0:64, 0:512], w_[:, D * kappa:D * (kappa + 1)],
                        bass.AP(xn, S * kappa + 512 * tw, [[KC * S, 128], [1, 512]]),
                        start=(kappa == 0), stop=(kappa == KC - 1)), "ssc", 1)
                P.waitc('C', "ssc")
                if which < 2:
                    P.opi('C', lambda e, bank=bank, dst=dst, tw=tw: e.activation(
                        dst[0:64, 512 * tw:512 * (tw + 1)], bank[0:64, 0:512], AF.Copy, scale=SCALE_QK), "spj", 1)
                else:
                    P.opi('C', lambda e, bank=bank: e.activation(
                        pbuf[0:64, 0:512], bank[0:64, 0:512], AF.Copy), "spj", 1)
            P.waitc('T', "spj")
            for s4 in range(4):
                P.opi('T', lambda e, s4=s4: e.transpose(
                    pT[0][:, 64 * s4:64 * (s4 + 1)], pbuf[0:64, 128 * s4:128 * (s4 + 1)], idb[0:64, 0:64]), "stt", 1)
            P.waitc('V', "stt")
            P.wait('V', "sot", P.cnt["sot"])
            P.opi('V', lambda e, tw=tw: e.tensor_copy(vt[:, 256 * tw:256 * (tw + 1)], pT[0][:, 0:256]), "spj", 1)
            P.waitc('T', "spj")
        if limit == 'proj':
            continue
        for qi in range(NTT):
            bd = qi // 4
            o = 128 * (qi % 4)
            nbk = bd + 1
            P.waitc('T', "sex")
            P.waitc('V', "sex")
            P.waitc('V', "spt")
            for blk in range(nbk):
                bank = pb[blk]
                P.opi('T', lambda e, bank=bank, qi=qi, blk=blk: e.matmul(
                    bank[:, 0:512], qbuf[0:64, 128 * qi:128 * (qi + 1)], kbuf[0:64, 512 * blk:512 * (blk + 1)],
                    start=True, stop=True), "ssc", 1)
                P.waitc('V', "ssc")
                P.opi('V', lambda e, bank=bank, blk=blk: e.reduce_max(
                    attv[:, blk:blk + 1], bank[:, 0:512], axis=AX.X), "smx", 1)
            P.waitc('V', "smx")
            P.opi('V', lambda e, nbk=nbk: e.reduce_max(
                attv[:, 4:5], attv[:, 0:nbk], axis=AX.X, negate=True), "smx", 1)
            P.waitc('V', "smx")
            P.waitc('C', "smx")
            P.waitc('C', "spt")
            for blk in range(nbk):
                bank = pb[blk]
                P.opi('C', lambda e, bank=bank, blk=blk: e.activation(
                    pbuf[:, 512 * blk:512 * (blk + 1)], bank[:, 0:512], AF.Exp,
                    bias=attv[:, 4:5], accum_out=attv[:, 8 + blk:9 + blk]), "sex", 1)
            P.waitc('V', "sex")
            P.opi('V', lambda e, bd=bd, o=o: e.tensor_mul(
                pbuf[:, 512 * bd:512 * (bd + 1)], pbuf[:, 512 * bd:512 * (bd + 1)],
                maskb[:, 384 - o:896 - o]), "sdg", 1)
            P.waitc('V', "sdg")
            P.opi('V', lambda e, bd=bd: e.reduce_sum(
                attv[:, 8 + bd:9 + bd], pbuf[:, 512 * bd:512 * (bd + 1)], axis=AX.X), "sdg", 1)
            P.waitc('V', "sdg")
            P.opi('V', lambda e, nbk=nbk: e.reduce_sum(attv[:, 13:14], attv[:, 8:8 + nbk], axis=AX.X), "sdg", 1)
            P.waitc('V', "sdg")
            P.opi('V', lambda e: e.reciprocal(attv[:, 13:14], attv[:, 13:14]), "sdg", 1)
            P.waitc('V', "sdg")
            P.opi('V', lambda e: e.tensor_scalar_mul(diagq[:], idb[:], attv[:, 13:14]), "sdg", 1)
            P.waitc('T', "sdg")
            for m in range(qi + 1):
                ptbank = pTf[m % 2]
                P.wait('T', "spte", P.cnt["spte"] - 1)
                P.opi('T', lambda e, ptbank=ptbank, m=m: e.matmul(
                    ptbank[:, 0:128], pbuf[:, 128 * m:128 * (m + 1)], diagq[:],
                    start=True, stop=True), "spt", 1)
                P.waitc('V', "spt")
                P.wait('V', "sot", P.cnt["sot"] - 1)
                P.opi('V', lambda e, ptbank=ptbank, m=m: e.tensor_copy(
                    ptb[:, 128 * (m % 2):128 * (m % 2 + 1)], ptbank[:, 0:128]), "spte", 1)
                P.waitc('T', "spte")
                P.opi('T', lambda e, m=m, qi=qi: e.matmul(
                    pb[5][0:64, 0:128], vt[:, 64 * m:64 * (m + 1)], ptb[:, 128 * (m % 2):128 * (m % 2 + 1)],
                    start=(m == 0), stop=(m == qi)), "sot", 1)
            P.waitc('C', "sot")
            P.opi('C', lambda e, qi=qi: e.activation(
                o64t[0:64, 128 * qi:128 * (qi + 1)], pb[5][0:64, 0:128], AF.Copy), "sot", 1)
            P.waitc('T', "sot")
        if b == 0 and limit == 'dbg':
            P.waitc('S', "sot")
            P.opi('S', lambda e: e.dma_start(out=dbg_xn[:], in_=xn[:]), "sost", 16)
            P.opi('S', lambda e: e.dma_start(out=dbg_q[:], in_=qbuf[:]), "sost", 16)
            P.opi('S', lambda e: e.dma_start(out=dbg_k[:], in_=kbuf[:]), "sost", 16)
            P.opi('S', lambda e: e.dma_start(out=dbg_vt[:], in_=vt[:]), "sost", 16)
            P.opi('S', lambda e: e.dma_start(out=dbg_o[:], in_=o64t[:]), "sost", 16)
            P.opi('S', lambda e: e.dma_start(out=dbg_p[:], in_=pbuf[:]), "sost", 16)
        if limit == 'attn':
            continue
        P.waitc('T', "sex")
        P.waitc('T', "sot")
        for tw in range(NTW):
            t_id = tw
            slx = t_id % 2
            upbanks = (pb[0], pb[1])
            for c in range(C32):
                bank = upbanks[c % 2]
                P.wait('T', "sgl", P.cnt["sgl"] - 1)
                for kappa in range(KC):
                    P.opi('T', lambda e, bank=bank, c=c, kappa=kappa, tw=tw: e.matmul(
                        bank[:, 0:TW],
                        bass.AP(wup, H * kappa + c, [[KC * H, 128], [128, 32], [32, 4]]),
                        bass.AP(xn, S * kappa + TW * tw, [[KC * S, 128], [1, TW]]),
                        start=(kappa == 0), stop=(kappa == KC - 1)), "sup", 1)
                P.waitc('C', "sup")
                P.opi('C', lambda e, bank=bank, c=c: e.activation(
                    abuf[:, TW * c:TW * (c + 1)], bank[:, 0:TW], AF.Gelu, bias=biasp[:, c:c + 1]), "sgl", 1)
            P.waitc('T', "sgl")
            if tw == 0:
                P.waitc('S', "sxn")
            P.wait('S', "soev", P.cnt["soev"] - 2)
            P.opi('S', lambda e, b=b, t_id=t_id, slx=slx: e.dma_start(
                out=xtile[slx][:], in_=x_d[b, 128 * t_id:128 * (t_id + 1), :]), "sxr", 16)
            dnbanks = (pb[2], pb[3])
            for fo in range(2):
                bank = dnbanks[fo]
                P.wait('T', "soev", P.cnt["soev"] - 3)
                P.opi('T', lambda e, bank=bank, t_id=t_id, fo=fo: e.matmul(
                    bank[:, 0:512], o64t[0:64, 128 * t_id:128 * (t_id + 1)], wv2sb[0:64, 512 * fo:512 * (fo + 1)],
                    start=True, stop=False), "sdn", 1)
                for c in range(C32):
                    P.opi('T', lambda e, bank=bank, c=c, tw=tw, fo=fo: e.matmul(
                        bank[:, 0:512], abuf[:, TW * c:TW * (c + 1)],
                        ldn[:, E * c + 512 * fo:E * c + 512 * (fo + 1)],
                        start=False, stop=(c == C32 - 1)), "sdn", 1)
                P.waitc('V', "sdn")
                P.waitc('V', "sxr")
                P.wait('V', "sost", 16 * (P.cnt["sost"] // 16 - 1))
                P.opi('V', lambda e, bank=bank, slx=slx, fo=fo: e.tensor_add(
                    ost[slx][:, 512 * fo:512 * (fo + 1)], bank[:, 0:512], xtile[slx][:, 512 * fo:512 * (fo + 1)]), "soev", 1)
                P.waitc('V', "soev")
                P.opi('V', lambda e, slx=slx, fo=fo: e.tensor_add(
                    ost[slx][:, 512 * fo:512 * (fo + 1)], ost[slx][:, 512 * fo:512 * (fo + 1)],
                    bsbc[:, 512 * fo:512 * (fo + 1)]), "soev", 1)
            P.waitc('S', "soev")
            P.opi('S', lambda e, b=b, t_id=t_id, slx=slx: e.dma_start(
                out=out_d[b, 128 * t_id:128 * (t_id + 1), :], in_=ost[slx][:]), "sost", 16)
            if b == 0 and tw == 0 and limit == 'dbg':
                P.waitc('S', "sgl")
                P.opi('S', lambda e: e.dma_start(out=dbg_a[:], in_=abuf[:]), "sost", 16)

    P.wait('S', "sost", P.cnt["sost"])
    P.replay()
    ctx.close()
    return nc


_NC_CACHE = {}


def _chunk_rows(w):
    w = np.asarray(w, np.float32).reshape(KC, 128, D).transpose(1, 0, 2).reshape(128, KC * D)
    return np.ascontiguousarray(w)


def kernel(L0_attn_x, L0_attn_w_q, L0_attn_w_k, L0_attn_w_v1, L0_attn_w_v2,
           L0_ffn_W_exp, L0_ffn_b_exp, L0_ffn_up_weights, L0_ffn_up_bias,
           L0_ffn_down_weights, L0_ffn_W_con, L0_ffn_b_con, L0_ffn_down_bias):
    x = np.ascontiguousarray(np.asarray(L0_attn_x, dtype=np.float32))
    mask, identf, identb, i32q, ones_r = _host_consts()
    mloU = _stage_lo_T(np.asarray(L0_ffn_up_weights, np.float32))
    mloD = _stage_lo_T(np.asarray(L0_ffn_down_weights, np.float32))
    mhiU = np.ascontiguousarray(_stage_hi_T(np.asarray(L0_ffn_up_weights, np.float32)).reshape(128, -1))
    mhiD = np.ascontiguousarray(_stage_hi_T(np.asarray(L0_ffn_down_weights, np.float32)).reshape(128, -1))

    common = {
        "wq": _chunk_rows(L0_attn_w_q),
        "wk": _chunk_rows(L0_attn_w_k),
        "wv1": _chunk_rows(L0_attn_w_v1),
        "wv2": np.ascontiguousarray(L0_attn_w_v2, np.float32),
        "wexp": np.ascontiguousarray(L0_ffn_W_exp, np.float32),
        "wcon": np.ascontiguousarray(L0_ffn_W_con, np.float32),
        "bexp": np.ascontiguousarray(np.asarray(L0_ffn_b_exp, np.float32).reshape(1, H)),
        "ubias": np.ascontiguousarray(np.asarray(L0_ffn_up_bias, np.float32).reshape(1, H)),
        "bcon": np.ascontiguousarray(np.asarray(L0_ffn_b_con, np.float32).reshape(1, E)),
        "dbias": np.ascontiguousarray(np.asarray(L0_ffn_down_bias, np.float32).reshape(1, E)),
        "mloU": mloU, "mloD": mloD, "mhiU": mhiU, "mhiD": mhiD,
        "maskc": mask, "identf": identf, "identb": identb,
        "i32q": i32q, "onesr": ones_r,
    }
    in_maps = []
    for core in range(NCORES):
        m = dict(common)
        m["x"] = np.ascontiguousarray(x[BPC * core:BPC * (core + 1)])
        in_maps.append(m)

    if "nc" not in _NC_CACHE:
        _NC_CACHE["nc"] = build_nc()
    nc = _NC_CACHE["nc"]
    res = run_bass_kernel_spmd(nc, in_maps, core_ids=list(range(NCORES)))
    return np.concatenate([res.results[i]["out"] for i in range(NCORES)], axis=0)




# version 22
# speedup vs baseline: 1.0617x; 1.0617x over previous
import numpy as np
import ml_dtypes
import concourse.bass as bass
import concourse.mybir as mybir
from concourse.bass_utils import run_bass_kernel_spmd
from contextlib import ExitStack

F32 = mybir.dt.float32
F32R = mybir.dt.float32r
BF16 = mybir.dt.bfloat16
AF = mybir.ActivationFunctionType
AX = mybir.AxisListType
ALU = mybir.AluOpType
bfloat16 = ml_dtypes.bfloat16

NCORES = 8
BPC = 4
S = 2048
E = 1024
H = 4096
D = 64
KC = 8
C32 = 32
NTT = S // 128
TW = 128
NTW = S // TW
FW = 256
SCALE_QK = float(64 ** -0.25)


def _stage_lo_T(wstages):
    M = np.zeros((32, 7, 128, 128), np.float32)
    for st in range(7):
        s = 1 << st
        w = wstages[st]
        ig = np.arange(4096)
        ig = ig[(ig & s) == 0]
        widx = (ig // (2 * s)) * s + (ig % s)
        c, i = np.divmod(ig, 128)
        i2 = i + s
        M[c, st, i, i] = w[widx, 0, 0]
        M[c, st, i2, i] = w[widx, 0, 1]
        M[c, st, i, i2] = w[widx, 1, 0]
        M[c, st, i2, i2] = w[widx, 1, 1]
    return np.ascontiguousarray(M.transpose(0, 2, 1, 3).reshape(32, 128, 7 * 128))


def _stage_hi_T(wstages):
    A = np.zeros((5, 32, 128, 128), np.float32)
    for st in range(7, 12):
        s = 1 << st
        sc = s // 128
        w = wstages[st]
        ig = np.arange(4096)
        ig = ig[(ig & s) == 0]
        widx = (ig // (2 * s)) * s + (ig % s)
        ci, p = np.divmod(ig, 128)
        ci2 = ci + sc
        j, pg = np.divmod(p, 4)
        st5 = st - 7
        A[st5, j, 32 * pg + ci, 32 * pg + ci] = w[widx, 0, 0]
        A[st5, j, 32 * pg + ci2, 32 * pg + ci] = w[widx, 0, 1]
        A[st5, j, 32 * pg + ci, 32 * pg + ci2] = w[widx, 1, 0]
        A[st5, j, 32 * pg + ci2, 32 * pg + ci2] = w[widx, 1, 1]
    return np.ascontiguousarray(A.transpose(0, 2, 1, 3).reshape(5, 128, 32 * 128))


def _host_consts():
    r = np.arange(128)[:, None]
    u = np.arange(896)[None, :]
    mask = (u <= r + 384).astype(bfloat16)
    identf = np.eye(128, dtype=np.float32)
    identb = np.eye(128, dtype=bfloat16)
    i32q = np.zeros((128, 32 * 32), np.float32)
    for j in range(32):
        for pg in range(4):
            i32q[32 * pg:32 * (pg + 1), 32 * j:32 * (j + 1)] = np.eye(32, dtype=np.float32)
    ones_r = np.ones((1, 128), np.float32)
    return mask, identf, identb, i32q, ones_r


class Plan:
    def __init__(self, nc, ctx):
        self.nc = nc
        self.ctx = ctx
        self.ops = []
        self.semh = {}
        self.cnt = {}
        self.on = True

    def sem(self, name):
        self.semh[name] = self.ctx.enter_context(self.nc.semaphore(name))
        self.cnt[name] = 0
        return name

    def op(self, eng, f):
        if not self.on:
            return
        self.ops.append((eng, f))

    def opi(self, eng, f, sem, k=1):
        if not self.on:
            return self.cnt[sem]
        h = self.semh[sem]
        self.cnt[sem] += k
        self.ops.append((eng, lambda e, f=f, h=h, k=k: f(e).then_inc(h, k)))
        return self.cnt[sem]

    def wait(self, eng, sem, val):
        if val <= 0 or not self.on:
            return
        h = self.semh[sem]
        self.ops.append((eng, lambda e, h=h, v=val: e.wait_ge(h, v)))

    def waitc(self, eng, sem):
        self.wait(eng, sem, self.cnt[sem])

    def replay(self):
        nc = self.nc
        ops = self.ops
        with nc.Block() as block:
            @block.tensor
            def _(t):
                for e, f in ops:
                    if e == 'T':
                        f(t)

            @block.vector
            def _(v):
                for e, f in ops:
                    if e == 'V':
                        f(v)

            @block.scalar
            def _(c):
                for e, f in ops:
                    if e == 'C':
                        f(c)

            @block.sync
            def _(s):
                for e, f in ops:
                    if e == 'S':
                        f(s)

            @block.gpsimd
            def _(g):
                for e, f in ops:
                    if e == 'G':
                        f(g)


_PH = {'loads': 0, 'bhichain': 1, 'blochain': 2, 'upfold': 3, 'downfold': 4,
       'build': 5, 'ln': 6, 'proj': 7, 'attn': 8, 'full': 9, 'dbg': 9}


def build_nc(limit='full'):
    _rank = _PH[limit]
    nc = bass.Bass("TRN2", target_bir_lowering=False, dynamic_dma_scratch_size=2048)
    ctx = ExitStack()

    dt_ = nc.dram_tensor
    x_d = dt_("x", [BPC, S, E], F32, kind="ExternalInput").ap()
    out_d = dt_("out", [BPC, S, E], F32, kind="ExternalOutput").ap()
    wq_d = dt_("wq", [128, KC * D], F32, kind="ExternalInput").ap()
    wk_d = dt_("wk", [128, KC * D], F32, kind="ExternalInput").ap()
    wv1_d = dt_("wv1", [128, KC * D], F32, kind="ExternalInput").ap()
    wv2_d = dt_("wv2", [D, E], F32, kind="ExternalInput").ap()
    wexp_d = dt_("wexp", [E, H], F32, kind="ExternalInput").ap()
    wcon_d = dt_("wcon", [H, E], F32, kind="ExternalInput").ap()
    bexp_d = dt_("bexp", [1, H], F32, kind="ExternalInput").ap()
    ubias_d = dt_("ubias", [1, H], F32, kind="ExternalInput").ap()
    bcon_d = dt_("bcon", [1, E], F32, kind="ExternalInput").ap()
    dbias_d = dt_("dbias", [1, E], F32, kind="ExternalInput").ap()
    mloU_d = dt_("mloU", [32, 128, 7 * 128], F32, kind="ExternalInput").ap()
    mloD_d = dt_("mloD", [32, 128, 7 * 128], F32, kind="ExternalInput").ap()
    mhiU_d = dt_("mhiU", [5, 128, 32 * 128], F32, kind="ExternalInput").ap()
    mhiD_d = dt_("mhiD", [5, 128, 32 * 128], F32, kind="ExternalInput").ap()
    mask_d = dt_("maskc", [128, 896], BF16, kind="ExternalInput").ap()
    identf_d = dt_("identf", [128, 128], F32, kind="ExternalInput").ap()
    identb_d = dt_("identb", [128, 128], BF16, kind="ExternalInput").ap()
    i32q_d = dt_("i32q", [128, 32 * 32], F32, kind="ExternalInput").ap()
    ones_d = dt_("onesr", [1, 128], F32, kind="ExternalInput").ap()
    swz_d = dt_("swz", [H, FW], F32R).ap()
    swzb_d = dt_("swzb", [H, 2], F32R).ap()
    wcpg_d = dt_("wcpg", [H, E], F32R).ap()
    v2h_d = dt_("v2h", [1, H], F32).ap()
    dbg_xn = dt_("dbg_xn", [128, KC * S], BF16, kind="ExternalOutput").ap()
    dbg_q = dt_("dbg_q", [64, S], BF16, kind="ExternalOutput").ap()
    dbg_k = dt_("dbg_k", [64, S], BF16, kind="ExternalOutput").ap()
    dbg_vt = dt_("dbg_vt", [128, NTT * D], BF16, kind="ExternalOutput").ap()
    dbg_o = dt_("dbg_o", [64, S], BF16, kind="ExternalOutput").ap()
    dbg_p = dt_("dbg_p", [128, S], BF16, kind="ExternalOutput").ap()
    dbg_a = dt_("dbg_a", [128, C32 * TW], BF16, kind="ExternalOutput").ap()
    blod_sp = dt_("blod_sp", [128, C32 * 128], F32R).ap()
    bhid_sp = dt_("bhid_sp", [128, C32 * 128], F32R).ap()

    sbm = lambda cx, name, shape, dtype: cx.enter_context(nc.sbuf_tensor(name, shape, dtype))
    sbr = lambda cx, name, shape, dtype: cx.enter_context(nc.sbuf_tensor(name, shape, dtype, side="right"))
    psm = lambda cx, name, shape, dtype: cx.enter_context(nc.psum_tensor(name, shape, dtype))

    wup = sbm(ctx, "wup", [128, KC * H], BF16)
    maskb = sbm(ctx, "maskb", [128, 896], BF16)
    idb = sbm(ctx, "idb", [128, 128], BF16)
    biasp = sbm(ctx, "biasp", [128, 32], F32)
    bsbc = sbm(ctx, "bsbc", [128, E], F32)
    wqsb = sbm(ctx, "wqsb", [128, KC * D], BF16)
    wksb = sbm(ctx, "wksb", [128, KC * D], BF16)
    wv1sb = sbm(ctx, "wv1sb", [128, KC * D], BF16)
    wv2sb = sbm(ctx, "wv2sb", [64, E], BF16)
    diagq = sbm(ctx, "diagq", [128, 128], BF16)
    lnv = sbm(ctx, "lnv", [128, 16], F32)
    attv = sbm(ctx, "attv", [128, 16], F32)

    pb = [psm(ctx, f"pb{i}", [128, 512], F32) for i in range(6)]
    pT = [psm(ctx, f"pT{i}", [128, 1024], BF16) for i in range(2)]
    pTf = [b[:].bitcast(F32) for b in pT]

    P = Plan(nc, ctx)
    for s_ in ["bld", "bmm", "bev", "bswz", "bg",
               "sx", "ssq", "sst", "sxn", "stt",
               "spj", "ssc", "smx", "sex", "sdg", "spt", "spte", "sot",
               "sup", "sgl", "sdn", "soev", "sxr", "sost"]:
        P.sem(s_)

    P.opi('V', lambda e: e.memset(lnv[:, 15:16], 1e-5), "bev", 1)
    P.opi('S', lambda e: e.dma_start(out=idb[:], in_=identb_d[:]), "bld", 16)
    P.opi('S', lambda e: e.dma_start(out=maskb[:], in_=mask_d[:]), "bld", 16)
    P.opi('S', lambda e: e.dma_start(out=onesr[:], in_=ones_d[:]), "bld", 16)
    P.opi('G', lambda e: e.dma_start(out=wqsb[:], in_=wq_d[:], max_dma_last_dim=4096), "bg", 16)
    P.opi('G', lambda e: e.dma_start(out=wksb[:], in_=wk_d[:], max_dma_last_dim=4096), "bg", 16)
    P.opi('G', lambda e: e.dma_start(out=wv1sb[:], in_=wv1_d[:], max_dma_last_dim=4096), "bg", 16)
    P.opi('G', lambda e: e.dma_start(out=wv2sb[:], in_=wv2_d[:], max_dma_last_dim=4096), "bg", 16)
    for pg in range(4):
        for jh in range(2):
            src = bass.AP(wcon_d.tensor, (pg + 4 * 16 * jh) * E, [[4 * E, 16], [128 * E, 32], [1, E]])
            dst = bass.AP(wcpg_d.tensor, (32 * pg) * E + (16 * jh) * 128 * E, [[128 * E, 16], [E, 32], [1, E]])
            P.opi('G', lambda e, s=src, d=dst: e.dma_start(out=d, in_=s, max_dma_last_dim=4096), "bg", 16)

    idf = sbm(ctx, "idf", [128, 128], F32)
    onesr = sbm(ctx, "onesr_sb", [1, 128], F32)
    P.opi('S', lambda e: e.dma_start(out=idf[:], in_=identf_d[:]), "bld", 16)

    def emit_chains(cx, bt, out_lo, out_hi):
        mhist = [sbm(cx, f"mhist{bt}{i}", [128, C32 * 128], F32) for i in range(1)]
        i32qsb = sbm(cx, f"i32qsb{bt}", [128, 32 * 32], F32)
        mlo8 = sbm(cx, f"mlo8{bt}", [128, 4 * 896], F32)
        chb8 = sbm(cx, f"chb8{bt}", [128, 4 * 128], F32)
        cqall = sbm(cx, f"cqall{bt}", [128, 32 * 32], F32)
        cqTall = sbm(cx, f"cqTall{bt}", [128, 32 * 32], F32)
        msrc = mhiU_d if bt == 0 else mhiD_d
        mlo_src = mloU_d if bt == 0 else mloD_d
        P.opi('S', lambda e, i32qsb=i32qsb: e.dma_start(out=i32qsb[:], in_=i32q_d[:]), "bld", 16)
        P.opi('V', lambda e, out_hi=out_hi: e.memset(out_hi[:].bitcast(F32), 0.0), "bev", 1)
        P.waitc('V', "bld")
        P.waitc('V', "bmm")
        P.opi('V', lambda e, cqall=cqall, i32qsb=i32qsb: e.tensor_copy(cqall[:], i32qsb[:]), "bev", 1)
        P.waitc('T', "bev")
        for st in range(5):
            mh = mhist[0]
            P.wait('S', "bmm", P.cnt["bmm"])
            P.opi('S', lambda e, mh=mh, msrc=msrc, st=st: e.dma_start(out=mh[:], in_=msrc[st]), "bld", 16)
            P.waitc('T', "bld")
            banks = (pb[0], pb[1]) if st % 2 == 0 else (pb[2], pb[3])
            for j in range(C32):
                bk = banks[j // 16]
                P.opi('T', lambda e, bk=bk, mh=mh, j=j, cqall=cqall: e.matmul(
                    bk[:, 32 * (j % 16):32 * (j % 16 + 1)], mh[:, 128 * j:128 * (j + 1)],
                    cqall[:, 32 * j:32 * (j + 1)], start=True, stop=True), "bmm", 1)
            P.waitc('V', "bmm")
            P.opi('V', lambda e, b=banks[0], cqall=cqall: e.tensor_copy(cqall[:, 0:512], b[:, 0:512]), "bev", 1)
            P.opi('V', lambda e, b=banks[1], cqall=cqall: e.tensor_copy(cqall[:, 512:1024], b[:, 0:512]), "bev", 1)
            P.waitc('T', "bev")
        if bt == 0:
            P.opi('V', lambda e, cqTall=cqTall, cqall=cqall: e.transpose(cqTall[:], cqall[:]), "bev", 1)
            P.waitc('V', "bev")
        srcq = cqTall if bt == 0 else cqall
        for j in range(C32):
            for pg in range(4):
                P.opi('V', lambda e, j=j, pg=pg, d=out_hi, s=srcq: e.tensor_copy(
                    d[32 * pg:32 * (pg + 1), 128 * j + 32 * pg:128 * j + 32 * (pg + 1)],
                    s[32 * pg:32 * (pg + 1), 32 * j:32 * (j + 1)]), "bev", 1)
        for g in range(8):
            P.wait('S', "bmm", P.cnt["bmm"])
            for cc in range(4):
                P.opi('S', lambda e, cc=cc, s=mlo_src, g=g, mlo8=mlo8: e.dma_start(
                    out=mlo8[:, 896 * cc:896 * (cc + 1)], in_=s[4 * g + cc]), "bld", 16)
            P.waitc('T', "bld")
            for st in range(7):
                bk = pb[st % 2]
                for cc in range(4):
                    rhs = idf[:] if st == 0 else chb8[:, 128 * cc:128 * (cc + 1)]
                    P.opi('T', lambda e, bk=bk, cc=cc, st=st, r=rhs, mlo8=mlo8: e.matmul(
                        bk[:, 128 * cc:128 * (cc + 1)],
                        mlo8[:, 896 * cc + 128 * st:896 * cc + 128 * (st + 1)],
                        r, start=True, stop=True), "bmm", 1)
                P.waitc('V', "bmm")
                if st < 6 or bt == 0:
                    dst = chb8[:, 0:512]
                else:
                    dst = out_lo[:, 512 * g:512 * (g + 1)]
                P.opi('V', lambda e, b=bk, d=dst: e.tensor_copy(d, b[:, 0:512]), "bev", 1)
                P.waitc('T', "bev")
            if bt == 0:
                for cc in range(4):
                    P.opi('T', lambda e, cc=cc, chb8=chb8: e.transpose(
                        pb[4][:, 128 * cc:128 * (cc + 1)], chb8[:, 128 * cc:128 * (cc + 1)], idf[:]), "bmm", 1)
                P.waitc('V', "bmm")
                P.opi('V', lambda e, g=g: e.tensor_copy(out_lo[:, 512 * g:512 * (g + 1)], pb[4][:, 0:512]), "bev", 1)
                P.waitc('T', "bev")

    cxA = ExitStack()
    bloUT = sbr(cxA, "bloUT", [128, C32 * 128], F32R)
    bhiUblk = sbr(cxA, "bhiUblk", [128, C32 * 128], F32R)
    P.on = _rank >= 1
    cx1c = ExitStack()
    emit_chains(cx1c, 0, bloUT, bhiUblk)
    P.waitc('S', "bev")
    P.waitc('T', "bev")
    P.waitc('V', "bmm")
    cx1c.close()

    P.on = _rank >= 3
    cx2 = ExitStack()
    wx = sbr(cx2, "wx", [128, H], F32)
    wtb = [sbr(cx2, f"wtb{i}", [128, FW], F32R) for i in range(2)]
    p1t = sbr(cx2, "p1t", [128, C32 * FW], F32R)
    p1s = sbr(cx2, "p1s", [128, C32 * FW], F32R)
    p2tb = [sbr(cx2, f"p2tb{i}", [128, FW], F32R) for i in range(2)]
    rowsb = sbr(cx2, "rowsb", [1, H], F32)
    bvec = sbr(cx2, "bvec", [128, 2 * C32], F32R)
    p1tbia = sbr(cx2, "p1tbia", [128, C32 * 2], F32R)
    p1sbia = sbr(cx2, "p1sbia", [128, C32 * 2], F32R)

    P.opi('S', lambda e: e.dma_start(out=rowsb[:], in_=bexp_d[:]), "bld", 16)
    P.waitc('T', "bld")
    for c in range(C32):
        P.opi('T', lambda e, c=c: e.matmul(
            pb[3][:, c:c + 1], rowsb[0:1, 128 * c:128 * (c + 1)], onesr[0:1, 0:1],
            start=True, stop=True), "bmm", 1)
    P.waitc('V', "bmm")
    P.opi('V', lambda e: e.memset(bvec[:].bitcast(F32), 0.0), "bev", 1)
    P.waitc('V', "bev")
    P.opi('V', lambda e: e.tensor_copy(
        bass.AP(bvec, 0, [[2 * C32, 128], [2, 32]]), pb[3][:, 0:32]), "bev", 1)
    P.waitc('T', "bev")
    for c in range(C32):
        P.opi('T', lambda e, c=c: e.matmul(
            pb[4][:, 2 * c:2 * (c + 1)], bloUT[:, 128 * c:128 * (c + 1)], bvec[:, 2 * c:2 * (c + 1)],
            start=True, stop=True), "bmm", 1)
    P.waitc('V', "bmm")
    P.opi('V', lambda e: e.tensor_copy(p1tbia[:], pb[4][:, 0:64]), "bev", 1)
    P.waitc('S', "bev")
    P.opi('S', lambda e: e.dma_start(
        out=bass.AP(swzb_d.tensor, 0, [[2, 128], [128 * 2, 32], [1, 2]]),
        in_=bass.AP(p1tbia, 0, [[C32 * 2, 128], [2, 32], [1, 2]])), "bswz", 16)
    P.waitc('S', "bswz")
    for pg in range(4):
        P.opi('S', lambda e, pg=pg: e.dma_start(
            out=bass.AP(p1sbia, (32 * pg) * (C32 * 2), [[C32 * 2, 32], [2, 32], [1, 2]]),
            in_=bass.AP(swzb_d.tensor, pg * 2, [[128 * 2, 32], [4 * 2, 32], [1, 2]])), "bswz", 16)
    P.waitc('T', "bswz")
    for j in range(C32):
        P.opi('T', lambda e, j=j: e.matmul(
            pb[5][:, 2 * j:2 * (j + 1)], bhiUblk[:, 128 * j:128 * (j + 1)],
            p1sbia[:, 2 * j:2 * (j + 1)], start=True, stop=True), "bmm", 1)
    P.waitc('V', "bmm")
    P.opi('V', lambda e: e.tensor_copy(p1sbia[:], pb[5][:, 0:64]), "bev", 1)
    P.waitc('S', "bev")
    for pg in range(4):
        P.opi('S', lambda e, pg=pg: e.dma_start(
            out=bass.AP(v2h_d.tensor, pg, [[128, 32], [4, 32], [1, 1]]),
            in_=bass.AP(p1sbia, (32 * pg) * 64, [[64, 32], [2, 32], [1, 1]]).bitcast(F32),
            allow_slow_non_contiguous=True), "bswz", 16)

    for w in range(4):
        for ki, kappa in enumerate([2 * w, 2 * w + 1]):
            P.wait('S', "bmm", P.cnt["bmm"])
            P.opi('S', lambda e, kappa=kappa: e.dma_start(out=wx[:], in_=wexp_d[128 * kappa:128 * (kappa + 1), :]), "bld", 16)
            P.waitc('T', "bld")
            for c in range(C32):
                P.opi('T', lambda e, ki=ki, c=c: e.transpose(
                    pTf[0][:, 128 * (c % 4):128 * (c % 4 + 1)], wx[:, 128 * c:128 * (c + 1)], idf[:]), "bmm", 1)
                if c % 4 == 3:
                    P.waitc('V', "bmm")
                    c0 = c - 3
                    P.opi('V', lambda e, ki=ki, c0=c0: e.tensor_copy(
                        bass.AP(p1t, FW * c0 + 128 * ki, [[C32 * FW, 128], [FW, 4], [1, 128]]),
                        pTf[0][:, 0:512]), "bev", 1)
                    P.waitc('T', "bev")
        for c in range(C32):
            P.waitc('T', "bev")
            P.opi('T', lambda e, c=c: e.matmul(
                pb[4][:, 0:FW], bloUT[:, 128 * c:128 * (c + 1)], p1t[:, FW * c:FW * (c + 1)],
                start=True, stop=True), "bmm", 1)
            P.waitc('V', "bmm")
            P.opi('V', lambda e, c=c: e.tensor_copy(p1t[:, FW * c:FW * (c + 1)], pb[4][:, 0:FW]), "bev", 1)
        P.waitc('S', "bev")
        P.opi('S', lambda e: e.dma_start(
            out=bass.AP(swz_d.tensor, 0, [[FW, 128], [128 * FW, 32], [1, FW]]),
            in_=p1t[:]), "bswz", 16)
        P.waitc('S', "bswz")
        P.wait('S', "bmm", P.cnt["bmm"])
        for pg in range(4):
            P.opi('S', lambda e, pg=pg: e.dma_start(
                out=bass.AP(p1s, (32 * pg) * (C32 * FW), [[C32 * FW, 32], [FW, 32], [1, FW]]),
            in_=bass.AP(swz_d.tensor, pg * FW, [[128 * FW, 32], [4 * FW, 32], [1, FW]])), "bswz", 16)
        P.waitc('T', "bswz")
        for j in range(C32):
            P.opi('T', lambda e, j=j: e.matmul(
                pb[5][:, 0:FW], bhiUblk[:, 128 * j:128 * (j + 1)], p1s[:, FW * j:FW * (j + 1)],
                start=True, stop=True), "bmm", 1)
            P.waitc('V', "bmm")
            p2b = p2tb[j % 2]
            if j >= 2:
                P.wait('V', "bmm", P.cnt["bmm"])
            P.opi('V', lambda e, d=p2b: e.tensor_copy(d[:], pb[5][:, 0:FW]), "bev", 1)
            P.waitc('T', "bev")
            for sub in range(2):
                P.opi('T', lambda e, d=p2b, sub=sub: e.transpose(
                    pTf[1][:, 128 * sub:128 * (sub + 1)], d[:].bitcast(F32)[:, 128 * sub:128 * (sub + 1)], idf[:]), "bmm", 1)
            P.waitc('V', "bmm")
            P.opi('V', lambda e, w=w, j=j: e.tensor_copy(
                bass.AP(wup, (2 * w) * H + 128 * j, [[KC * H, 128], [H, 2], [1, 128]]),
                pTf[1][:, 0:256]), "bev", 1)
            P.waitc('T', "bev")

    P.waitc('S', "bswz")
    P.wait('S', "bmm", P.cnt["bmm"])
    P.opi('S', lambda e: e.dma_start(out=rowsb[:], in_=v2h_d[:]), "bld", 16)
    P.waitc('T', "bld")
    for c in range(C32):
        P.opi('T', lambda e, c=c: e.matmul(
            pb[3][:, c:c + 1], rowsb[0:1, 128 * c:128 * (c + 1)], onesr[0:1, 0:1],
            start=True, stop=True), "bmm", 1)
    P.wait('S', "bmm", P.cnt["bmm"])
    P.opi('S', lambda e: e.dma_start(out=rowsb[:], in_=ubias_d[:]), "bld", 16)
    P.waitc('T', "bld")
    for c in range(C32):
        P.opi('T', lambda e, c=c: e.matmul(
            pb[4][:, c:c + 1], rowsb[0:1, 128 * c:128 * (c + 1)], onesr[0:1, 0:1],
            start=True, stop=True), "bmm", 1)
    P.waitc('V', "bmm")
    P.opi('V', lambda e: e.tensor_copy(biasp[:], pb[3][:, 0:32]), "bev", 1)
    P.waitc('V', "bev")
    P.opi('V', lambda e: e.tensor_add(biasp[:], biasp[:], pb[4][:, 0:32]), "bev", 1)
    P.wait('S', "bev", P.cnt["bev"])
    P.opi('S', lambda e: e.dma_start(out=rowsb[0:1, 0:E], in_=bcon_d[:]), "bld", 16)
    P.opi('S', lambda e: e.dma_start(out=rowsb[0:1, E:2 * E], in_=dbias_d[:]), "bld", 16)
    P.waitc('V', "bld")
    P.opi('V', lambda e: e.tensor_add(rowsb[0:1, 0:E], rowsb[0:1, 0:E], rowsb[0:1, E:2 * E]), "bev", 1)
    P.waitc('T', "bev")
    for hf in range(2):
        P.opi('T', lambda e, hf=hf: e.matmul(
            pb[5][:, 0:512], onesr[0:1, 0:128], rowsb[0:1, 512 * hf:512 * (hf + 1)],
            start=True, stop=True), "bmm", 1)
        P.waitc('V', "bmm")
        P.opi('V', lambda e, hf=hf: e.tensor_copy(bsbc[:, 512 * hf:512 * (hf + 1)], pb[5][:, 0:512]), "bev", 1)
        P.waitc('T', "bev")
    P.waitc('S', "bev")
    P.waitc('V', "bswz")
    cx2.close()
    cxA.close()

    P.on = _rank >= 4
    ldn = sbr(ctx, "ldn", [128, C32 * E], BF16)
    cx3o = ExitStack()
    bloD2 = sbr(cx3o, "bloD2", [128, C32 * 128], F32R)
    bhiD2 = sbr(cx3o, "bhiD2", [128, C32 * 128], F32R)
    cx3c = ExitStack()
    emit_chains(cx3c, 1, bloD2, bhiD2)
    P.waitc('S', "bev")
    P.waitc('T', "bev")
    P.waitc('V', "bmm")
    cx3c.close()
    cx3 = ExitStack()
    upg = sbr(cx3, "upg", [128, C32 * FW], F32R)
    ucmh = sbr(cx3, "ucmh", [128, 16 * FW], F32R)
    wct = [sbr(cx3, f"wct{i}", [128, FW], F32R) for i in range(1)]
    P.waitc('S', "bg")
    for w in range(4):
        for j in range(C32):
            b = wct[0]
            if w + j > 0:
                P.wait('S', "bmm", P.cnt["bmm"])
            P.opi('S', lambda e, b=b, j=j, w=w: e.dma_start(
                out=b[:], in_=bass.AP(wcpg_d.tensor, (128 * j) * E + FW * w, [[E, 128], [1, FW]])), "bld", 16)
            P.waitc('T', "bld")
            P.opi('T', lambda e, j=j, b=b: e.matmul(
                pb[4][:, 0:FW], bhiD2[:, 128 * j:128 * (j + 1)], b[:],
                start=True, stop=True), "bmm", 1)
            P.waitc('V', "bmm")
            P.opi('V', lambda e, j=j: e.tensor_copy(upg[:, FW * j:FW * (j + 1)], pb[4][:, 0:FW]), "bev", 1)
            P.waitc('T', "bev")
        P.waitc('S', "bev")
        P.wait('S', "bld", P.cnt["bld"])
        for pg in range(4):
            P.opi('S', lambda e, pg=pg: e.dma_start(
                out=bass.AP(swz_d.tensor, pg * FW, [[128 * FW, 32], [4 * FW, 32], [1, FW]]),
                in_=bass.AP(upg, (32 * pg) * (C32 * FW), [[C32 * FW, 32], [FW, 32], [1, FW]])), "bswz", 16)
        P.waitc('S', "bswz")
        for ch in range(2):
            P.wait('S', "bmm", P.cnt["bmm"])
            P.opi('S', lambda e, ch=ch: e.dma_start(
                out=bass.AP(ucmh, 0, [[16 * FW, 128], [FW, 16], [1, FW]]),
                in_=bass.AP(swz_d.tensor, (16 * ch) * 128 * FW, [[FW, 128], [128 * FW, 16], [1, FW]])), "bld", 16)
            P.waitc('T', "bld")
            for cc in range(16):
                c = 16 * ch + cc
                P.opi('T', lambda e, c=c, cc=cc: e.matmul(
                    pb[5][:, 0:FW], bloD2[:, 128 * c:128 * (c + 1)], ucmh[:, FW * cc:FW * (cc + 1)],
                    start=True, stop=True), "bmm", 1)
                P.waitc('V', "bmm")
                P.opi('V', lambda e, c=c, w=w: e.tensor_copy(
                    ldn[:, E * c + FW * w:E * c + FW * (w + 1)], pb[5][:, 0:FW]), "bev", 1)
                P.waitc('T', "bev")
    P.waitc('S', "bev")
    P.waitc('V', "bswz")
    cx3.close()
    cx3o.close()

    xn = sbr(ctx, "xn", [128, KC * S], BF16)
    abuf = sbr(ctx, "abuf", [128, C32 * TW], BF16)
    qbuf = sbr(ctx, "qbuf", [64, S], BF16)
    kbuf = sbr(ctx, "kbuf", [64, S], BF16)
    o64t = sbr(ctx, "o64t", [64, S], BF16)
    vt = sbr(ctx, "vt", [128, NTT * D], BF16)
    pbuf = sbr(ctx, "pbuf", [128, S], BF16)
    ptb = sbr(ctx, "ptb", [128, 2 * 512], BF16)
    xtile = [sbr(ctx, f"xtile{i}", [128, E], F32) for i in range(2)]
    xnt = [sbr(ctx, f"xnt{i}", [128, E], BF16) for i in range(2)]
    ost = [sbr(ctx, f"ost{i}", [128, E], F32) for i in range(2)]

    P.on = True
    P.waitc('T', "bev")
    P.waitc('C', "bev")
    P.waitc('V', "bmm")
    P.waitc('C', "bld")
    P.waitc('C', "bg")
    P.waitc('T', "bg")

    nseq = 0 if _rank <= 5 else (1 if limit in ('ln', 'proj', 'attn', 'ffn1') else BPC)
    for b in range(nseq):
        P.wait('C', "sost", P.cnt["sost"])
        for i in range(NTT):
            sl = i % 2
            xt = xtile[sl]
            gi = b * NTT + i
            if gi >= 2:
                P.wait('S', "sxn", 2 * (gi - 2) + 1)
            if gi >= 2:
                P.wait('S', "ssq", P.cnt["ssq"] - 3)
            P.opi('S', lambda e, xt=xt, b=b, i=i: e.dma_start(
                out=xt[:], in_=x_d[b, 128 * i:128 * (i + 1), :]), "sx", 16)
            P.waitc('V', "sx")
            P.waitc('C', "sx")
            c0, c1, c2, c3, c5 = sl * 8, sl * 8 + 1, sl * 8 + 2, sl * 8 + 3, sl * 8 + 5
            P.opi('V', lambda e, xt=xt, c0=c0: e.reduce_sum(lnv[:, c0:c0 + 1], xt[:], axis=AX.X), "sst", 1)
            P.waitc('V', "sst")
            if gi >= 2:
                P.wait('C', "stt", P.cnt["stt"] - 8)
            P.opi('C', lambda e, xt=xt, sq=xnt[sl], c1=c1: e.activation(
                sq[:], xt[:], AF.Square, accum_out=lnv[:, c1:c1 + 1]), "ssq", 1)
            P.waitc('V', "ssq")
            P.opi('V', lambda e, c0=c0, c2=c2: e.tensor_scalar_mul(lnv[:, c2:c2 + 1], lnv[:, c0:c0 + 1], -1.0 / E), "sst", 1)
            P.opi('V', lambda e, c1=c1, c3=c3: e.tensor_scalar_mul(lnv[:, c3:c3 + 1], lnv[:, c1:c1 + 1], 1.0 / E), "sst", 1)
            P.waitc('V', "sst")
            c4 = sl * 8 + 4
            P.opi('V', lambda e, c2=c2, c4=c4: e.tensor_mul(lnv[:, c4:c4 + 1], lnv[:, c2:c2 + 1], lnv[:, c2:c2 + 1]), "sst", 1)
            P.waitc('V', "sst")
            P.opi('V', lambda e, c3=c3, c4=c4: e.tensor_sub(lnv[:, c3:c3 + 1], lnv[:, c3:c3 + 1], lnv[:, c4:c4 + 1]), "sst", 1)
            P.waitc('V', "sst")
            P.waitc('C', "sst")
            P.opi('C', lambda e, c3=c3, c5=c5: e.activation(lnv[:, c5:c5 + 1], lnv[:, c3:c3 + 1], AF.Sqrt, bias=lnv[:, 15:16]), "ssq", 1)
            P.waitc('V', "ssq")
            P.opi('V', lambda e, c5=c5: e.reciprocal(lnv[:, c5:c5 + 1], lnv[:, c5:c5 + 1]), "sst", 1)
            P.waitc('V', "sst")
            P.opi('V', lambda e, xt=xt, xo=xnt[sl], c2=c2, c5=c5: e.tensor_scalar(
                xo[:], xt[:], lnv[:, c2:c2 + 1], lnv[:, c5:c5 + 1], ALU.add, ALU.mult), "sxn", 1)
            P.waitc('T', "sxn")
            tb = pT[sl]
            for kappa in range(KC):
                P.opi('T', lambda e, tb=tb, xo=xnt[sl], kappa=kappa: e.transpose(
                    tb[:, 128 * kappa:128 * (kappa + 1)], xo[:, 128 * kappa:128 * (kappa + 1)], idb[:]), "stt", 1)
            P.waitc('V', "stt")
            P.opi('V', lambda e, tb=tb, i=i: e.tensor_copy(
                bass.AP(xn, 128 * i, [[KC * S, 128], [S, 8], [1, 128]]), tb[:]), "sxn", 1)
        if limit == 'ln':
            continue
        P.waitc('T', "sxn")
        for tw in range(4):
            for which, wsb_, dst in ((0, wqsb, qbuf), (1, wksb, kbuf), (2, wv1sb, None)):
                bank = pb[which]
                for kappa in range(KC):
                    P.opi('T', lambda e, bank=bank, w_=wsb_, kappa=kappa, tw=tw: e.matmul(
                        bank[0:64, 0:512], w_[:, D * kappa:D * (kappa + 1)],
                        bass.AP(xn, S * kappa + 512 * tw, [[KC * S, 128], [1, 512]]),
                        start=(kappa == 0), stop=(kappa == KC - 1)), "ssc", 1)
                P.waitc('C', "ssc")
                if which < 2:
                    P.opi('C', lambda e, bank=bank, dst=dst, tw=tw: e.activation(
                        dst[0:64, 512 * tw:512 * (tw + 1)], bank[0:64, 0:512], AF.Copy, scale=SCALE_QK), "spj", 1)
                else:
                    P.opi('C', lambda e, bank=bank: e.activation(
                        pbuf[0:64, 0:512], bank[0:64, 0:512], AF.Copy), "spj", 1)
            P.waitc('T', "spj")
            for s4 in range(4):
                P.opi('T', lambda e, s4=s4: e.transpose(
                    pT[0][:, 64 * s4:64 * (s4 + 1)], pbuf[0:64, 128 * s4:128 * (s4 + 1)], idb[0:64, 0:64]), "stt", 1)
            P.waitc('V', "stt")
            P.wait('V', "sot", P.cnt["sot"])
            P.opi('V', lambda e, tw=tw: e.tensor_copy(vt[:, 256 * tw:256 * (tw + 1)], pT[0][:, 0:256]), "spj", 1)
            P.waitc('T', "spj")
        if limit == 'proj':
            continue
        blkctr = getattr(P, "_blkctr", 0)
        if not hasattr(P, "_sexbase"):
            P._sexbase = P.cnt["sex"]
        for qi in range(NTT):
            bd = qi // 4
            o = 128 * (qi % 4)
            nbk = bd + 1
            P.waitc('V', "sex")
            P.waitc('V', "spt")
            myblk = []
            for blk in range(nbk):
                bank = pb[blkctr % 4]
                myblk.append(bank)
                P.wait('T', "sex", P._sexbase + blkctr - 3)
                blkctr += 1
                P.opi('T', lambda e, bank=bank, qi=qi, blk=blk: e.matmul(
                    bank[:, 0:512], qbuf[0:64, 128 * qi:128 * (qi + 1)], kbuf[0:64, 512 * blk:512 * (blk + 1)],
                    start=True, stop=True), "ssc", 1)
                P.waitc('V', "ssc")
                P.opi('V', lambda e, bank=bank, blk=blk: e.reduce_max(
                    attv[:, blk:blk + 1], bank[:, 0:512], axis=AX.X), "smx", 1)
            P.waitc('V', "smx")
            P.opi('V', lambda e, nbk=nbk: e.reduce_max(
                attv[:, 4:5], attv[:, 0:nbk], axis=AX.X, negate=True), "smx", 1)
            P.waitc('V', "smx")
            P.waitc('C', "smx")
            P.waitc('C', "spt")
            for blk in range(nbk):
                bank = myblk[blk]
                P.opi('C', lambda e, bank=bank, blk=blk: e.activation(
                    pbuf[:, 512 * blk:512 * (blk + 1)], bank[:, 0:512], AF.Exp,
                    bias=attv[:, 4:5], accum_out=attv[:, 8 + blk:9 + blk]), "sex", 1)
            P.waitc('V', "sex")
            P.opi('V', lambda e, bd=bd, o=o: e.tensor_mul(
                pbuf[:, 512 * bd:512 * (bd + 1)], pbuf[:, 512 * bd:512 * (bd + 1)],
                maskb[:, 384 - o:896 - o]), "sdg", 1)
            P.waitc('V', "sdg")
            P.opi('V', lambda e, bd=bd: e.reduce_sum(
                attv[:, 8 + bd:9 + bd], pbuf[:, 512 * bd:512 * (bd + 1)], axis=AX.X), "sdg", 1)
            P.waitc('V', "sdg")
            P.opi('V', lambda e, nbk=nbk: e.reduce_sum(attv[:, 13:14], attv[:, 8:8 + nbk], axis=AX.X), "sdg", 1)
            P.waitc('V', "sdg")
            P.opi('V', lambda e: e.reciprocal(attv[:, 13:14], attv[:, 13:14]), "sdg", 1)
            P.waitc('V', "sdg")
            P.opi('V', lambda e: e.tensor_scalar_mul(diagq[:], idb[:], attv[:, 13:14]), "sdg", 1)
            P.waitc('T', "sdg")
            nm = qi + 1
            for g4 in range(0, nm, 4):
                mm4 = list(range(g4, min(g4 + 4, nm)))
                ptbank = pTf[(g4 // 4) % 2]
                P.wait('T', "spte", P.cnt["spte"] - 1)
                for ii, m in enumerate(mm4):
                    P.opi('T', lambda e, ptbank=ptbank, m=m, ii=ii: e.matmul(
                        ptbank[:, 128 * ii:128 * (ii + 1)], pbuf[:, 128 * m:128 * (m + 1)], diagq[:],
                        start=True, stop=True), "spt", 1)
                P.waitc('V', "spt")
                P.wait('V', "sot", P.cnt["sot"] - 4)
                sl4 = (g4 // 4) % 2
                P.opi('V', lambda e, ptbank=ptbank, sl4=sl4, n4=len(mm4): e.tensor_copy(
                    ptb[:, 512 * sl4:512 * sl4 + 128 * n4], ptbank[:, 0:128 * n4]), "spte", 1)
                P.waitc('T', "spte")
                for ii, m in enumerate(mm4):
                    P.opi('T', lambda e, m=m, qi=qi, sl4=sl4, ii=ii: e.matmul(
                        pb[5][0:64, 0:128], vt[:, 64 * m:64 * (m + 1)], ptb[:, 512 * sl4 + 128 * ii:512 * sl4 + 128 * (ii + 1)],
                        start=(m == 0), stop=(m == qi)), "sot", 1)
            P.waitc('C', "sot")
            P.opi('C', lambda e, qi=qi: e.activation(
                o64t[0:64, 128 * qi:128 * (qi + 1)], pb[5][0:64, 0:128], AF.Copy), "sot", 1)
            P.waitc('T', "sot")
        P._blkctr = blkctr
        if limit == 'attn':
            continue
        P.waitc('T', "sex")
        P.waitc('T', "sot")
        for tw in range(NTW):
            t_id = tw
            slx = t_id % 2
            upbanks = (pb[0], pb[1])
            for c in range(C32):
                bank = upbanks[c % 2]
                P.wait('T', "sgl", P.cnt["sgl"] - 1)
                for kappa in range(KC):
                    P.opi('T', lambda e, bank=bank, c=c, kappa=kappa, tw=tw: e.matmul(
                        bank[:, 0:TW],
                        bass.AP(wup, H * kappa + c, [[KC * H, 128], [128, 32], [32, 4]]),
                        bass.AP(xn, S * kappa + TW * tw, [[KC * S, 128], [1, TW]]),
                        start=(kappa == 0), stop=(kappa == KC - 1)), "sup", 1)
                P.waitc('C', "sup")
                P.opi('C', lambda e, bank=bank, c=c: e.activation(
                    abuf[:, TW * c:TW * (c + 1)], bank[:, 0:TW], AF.Gelu, bias=biasp[:, c:c + 1]), "sgl", 1)
            P.waitc('T', "sgl")
            if tw == 0:
                P.waitc('S', "sxn")
            P.wait('S', "soev", P.cnt["soev"] - 2)
            P.opi('S', lambda e, b=b, t_id=t_id, slx=slx: e.dma_start(
                out=xtile[slx][:], in_=x_d[b, 128 * t_id:128 * (t_id + 1), :]), "sxr", 16)
            dnbanks = (pb[2], pb[3])
            for fo in range(2):
                bank = dnbanks[fo]
                P.wait('T', "soev", P.cnt["soev"] - 3)
                P.opi('T', lambda e, bank=bank, t_id=t_id, fo=fo: e.matmul(
                    bank[:, 0:512], o64t[0:64, 128 * t_id:128 * (t_id + 1)], wv2sb[0:64, 512 * fo:512 * (fo + 1)],
                    start=True, stop=False), "sdn", 1)
                for c in range(C32):
                    P.opi('T', lambda e, bank=bank, c=c, tw=tw, fo=fo: e.matmul(
                        bank[:, 0:512], abuf[:, TW * c:TW * (c + 1)],
                        ldn[:, E * c + 512 * fo:E * c + 512 * (fo + 1)],
                        start=False, stop=(c == C32 - 1)), "sdn", 1)
                P.waitc('V', "sdn")
                P.waitc('V', "sxr")
                P.wait('V', "sost", 16 * (P.cnt["sost"] // 16 - 1))
                P.opi('V', lambda e, bank=bank, slx=slx, fo=fo: e.tensor_add(
                    ost[slx][:, 512 * fo:512 * (fo + 1)], bank[:, 0:512], xtile[slx][:, 512 * fo:512 * (fo + 1)]), "soev", 1)
                P.waitc('V', "soev")
                P.opi('V', lambda e, slx=slx, fo=fo: e.tensor_add(
                    ost[slx][:, 512 * fo:512 * (fo + 1)], ost[slx][:, 512 * fo:512 * (fo + 1)],
                    bsbc[:, 512 * fo:512 * (fo + 1)]), "soev", 1)
            P.waitc('S', "soev")
            P.opi('S', lambda e, b=b, t_id=t_id, slx=slx: e.dma_start(
                out=out_d[b, 128 * t_id:128 * (t_id + 1), :], in_=ost[slx][:]), "sost", 16)
            if b == 0 and tw == 0 and limit == 'dbg':
                P.waitc('S', "sgl")
                P.opi('S', lambda e: e.dma_start(out=dbg_a[:], in_=abuf[:]), "sost", 16)

    P.wait('S', "sost", P.cnt["sost"])
    P.replay()
    ctx.close()
    return nc


_NC_CACHE = {}


def _chunk_rows(w):
    w = np.asarray(w, np.float32).reshape(KC, 128, D).transpose(1, 0, 2).reshape(128, KC * D)
    return np.ascontiguousarray(w)


def kernel(L0_attn_x, L0_attn_w_q, L0_attn_w_k, L0_attn_w_v1, L0_attn_w_v2,
           L0_ffn_W_exp, L0_ffn_b_exp, L0_ffn_up_weights, L0_ffn_up_bias,
           L0_ffn_down_weights, L0_ffn_W_con, L0_ffn_b_con, L0_ffn_down_bias):
    x = np.ascontiguousarray(np.asarray(L0_attn_x, dtype=np.float32))
    mask, identf, identb, i32q, ones_r = _host_consts()
    mloU = _stage_lo_T(np.asarray(L0_ffn_up_weights, np.float32))
    mloD = _stage_lo_T(np.asarray(L0_ffn_down_weights, np.float32))
    mhiU = _stage_hi_T(np.asarray(L0_ffn_up_weights, np.float32))
    mhiD = _stage_hi_T(np.asarray(L0_ffn_down_weights, np.float32))

    common = {
        "wq": _chunk_rows(L0_attn_w_q),
        "wk": _chunk_rows(L0_attn_w_k),
        "wv1": _chunk_rows(L0_attn_w_v1),
        "wv2": np.ascontiguousarray(L0_attn_w_v2, np.float32),
        "wexp": np.ascontiguousarray(L0_ffn_W_exp, np.float32),
        "wcon": np.ascontiguousarray(L0_ffn_W_con, np.float32),
        "bexp": np.ascontiguousarray(np.asarray(L0_ffn_b_exp, np.float32).reshape(1, H)),
        "ubias": np.ascontiguousarray(np.asarray(L0_ffn_up_bias, np.float32).reshape(1, H)),
        "bcon": np.ascontiguousarray(np.asarray(L0_ffn_b_con, np.float32).reshape(1, E)),
        "dbias": np.ascontiguousarray(np.asarray(L0_ffn_down_bias, np.float32).reshape(1, E)),
        "mloU": mloU, "mloD": mloD, "mhiU": mhiU, "mhiD": mhiD,
        "maskc": mask, "identf": identf, "identb": identb,
        "i32q": i32q, "onesr": ones_r,
    }
    in_maps = []
    for core in range(NCORES):
        m = dict(common)
        m["x"] = np.ascontiguousarray(x[BPC * core:BPC * (core + 1)])
        in_maps.append(m)

    if "nc" not in _NC_CACHE:
        _NC_CACHE["nc"] = build_nc()
    nc = _NC_CACHE["nc"]
    res = run_bass_kernel_spmd(nc, in_maps, core_ids=list(range(NCORES)))
    return np.concatenate([res.results[i]["out"] for i in range(NCORES)], axis=0)




# version 23
# speedup vs baseline: 1.1662x; 1.0985x over previous
import numpy as np
import ml_dtypes
import concourse.bass as bass
import concourse.mybir as mybir
from concourse.bass_utils import run_bass_kernel_spmd
from contextlib import ExitStack

F32 = mybir.dt.float32
F32R = mybir.dt.float32r
BF16 = mybir.dt.bfloat16
AF = mybir.ActivationFunctionType
AX = mybir.AxisListType
ALU = mybir.AluOpType
bfloat16 = ml_dtypes.bfloat16

NCORES = 8
BPC = 4
S = 2048
E = 1024
H = 4096
D = 64
KC = 8
C32 = 32
NTT = S // 128
TW = 128
NTW = S // TW
FW = 256
SCALE_QK = float(64 ** -0.25)


def _stage_lo_T(wstages):
    M = np.zeros((32, 7, 128, 128), np.float32)
    for st in range(7):
        s = 1 << st
        w = wstages[st]
        ig = np.arange(4096)
        ig = ig[(ig & s) == 0]
        widx = (ig // (2 * s)) * s + (ig % s)
        c, i = np.divmod(ig, 128)
        i2 = i + s
        M[c, st, i, i] = w[widx, 0, 0]
        M[c, st, i2, i] = w[widx, 0, 1]
        M[c, st, i, i2] = w[widx, 1, 0]
        M[c, st, i2, i2] = w[widx, 1, 1]
    return np.ascontiguousarray(M.transpose(0, 2, 1, 3).reshape(32, 128, 7 * 128))


def _stage_hi_T(wstages):
    A = np.zeros((5, 32, 128, 128), np.float32)
    for st in range(7, 12):
        s = 1 << st
        sc = s // 128
        w = wstages[st]
        ig = np.arange(4096)
        ig = ig[(ig & s) == 0]
        widx = (ig // (2 * s)) * s + (ig % s)
        ci, p = np.divmod(ig, 128)
        ci2 = ci + sc
        j, pg = np.divmod(p, 4)
        st5 = st - 7
        A[st5, j, 32 * pg + ci, 32 * pg + ci] = w[widx, 0, 0]
        A[st5, j, 32 * pg + ci2, 32 * pg + ci] = w[widx, 0, 1]
        A[st5, j, 32 * pg + ci, 32 * pg + ci2] = w[widx, 1, 0]
        A[st5, j, 32 * pg + ci2, 32 * pg + ci2] = w[widx, 1, 1]
    return np.ascontiguousarray(A.transpose(0, 2, 1, 3).reshape(5, 128, 32 * 128))


def _host_consts():
    r = np.arange(128)[:, None]
    u = np.arange(896)[None, :]
    mask = (u <= r + 384).astype(bfloat16)
    identf = np.eye(128, dtype=np.float32)
    identb = np.eye(128, dtype=bfloat16)
    i32q = np.zeros((128, 32 * 32), np.float32)
    for j in range(32):
        for pg in range(4):
            i32q[32 * pg:32 * (pg + 1), 32 * j:32 * (j + 1)] = np.eye(32, dtype=np.float32)
    ones_r = np.ones((1, 128), np.float32)
    return mask, identf, identb, i32q, ones_r


class Plan:
    def __init__(self, nc, ctx):
        self.nc = nc
        self.ctx = ctx
        self.ops = []
        self.semh = {}
        self.cnt = {}
        self.on = True

    def sem(self, name):
        self.semh[name] = self.ctx.enter_context(self.nc.semaphore(name))
        self.cnt[name] = 0
        return name

    def op(self, eng, f):
        if not self.on:
            return
        self.ops.append((eng, f))

    def opi(self, eng, f, sem, k=1):
        if not self.on:
            return self.cnt[sem]
        h = self.semh[sem]
        self.cnt[sem] += k
        self.ops.append((eng, lambda e, f=f, h=h, k=k: f(e).then_inc(h, k)))
        return self.cnt[sem]

    def wait(self, eng, sem, val):
        if val <= 0 or not self.on:
            return
        h = self.semh[sem]
        self.ops.append((eng, lambda e, h=h, v=val: e.wait_ge(h, v)))

    def waitc(self, eng, sem):
        self.wait(eng, sem, self.cnt[sem])

    def replay(self):
        nc = self.nc
        ops = self.ops
        with nc.Block() as block:
            @block.tensor
            def _(t):
                for e, f in ops:
                    if e == 'T':
                        f(t)

            @block.vector
            def _(v):
                for e, f in ops:
                    if e == 'V':
                        f(v)

            @block.scalar
            def _(c):
                for e, f in ops:
                    if e == 'C':
                        f(c)

            @block.sync
            def _(s):
                for e, f in ops:
                    if e == 'S':
                        f(s)

            @block.gpsimd
            def _(g):
                for e, f in ops:
                    if e == 'G':
                        f(g)


_PH = {'loads': 0, 'bhichain': 1, 'blochain': 2, 'upfold': 3, 'downfold': 4,
       'build': 5, 'ln': 6, 'proj': 7, 'attn': 8, 'full': 9, 'dbg': 9}


def build_nc(limit='full'):
    _rank = _PH[limit]
    nc = bass.Bass("TRN2", target_bir_lowering=False, dynamic_dma_scratch_size=2048)
    ctx = ExitStack()

    dt_ = nc.dram_tensor
    x_d = dt_("x", [BPC, S, E], F32, kind="ExternalInput").ap()
    out_d = dt_("out", [BPC, S, E], F32, kind="ExternalOutput").ap()
    wq_d = dt_("wq", [128, KC * D], F32, kind="ExternalInput").ap()
    wk_d = dt_("wk", [128, KC * D], F32, kind="ExternalInput").ap()
    wv1_d = dt_("wv1", [128, KC * D], F32, kind="ExternalInput").ap()
    wv2_d = dt_("wv2", [D, E], F32, kind="ExternalInput").ap()
    wexp_d = dt_("wexp", [E, H], F32, kind="ExternalInput").ap()
    wcon_d = dt_("wcon", [H, E], F32, kind="ExternalInput").ap()
    bexp_d = dt_("bexp", [1, H], F32, kind="ExternalInput").ap()
    ubias_d = dt_("ubias", [1, H], F32, kind="ExternalInput").ap()
    bcon_d = dt_("bcon", [1, E], F32, kind="ExternalInput").ap()
    dbias_d = dt_("dbias", [1, E], F32, kind="ExternalInput").ap()
    mloU_d = dt_("mloU", [32, 128, 7 * 128], F32, kind="ExternalInput").ap()
    mloD_d = dt_("mloD", [32, 128, 7 * 128], F32, kind="ExternalInput").ap()
    mhiU_d = dt_("mhiU", [5, 128, 32 * 128], F32, kind="ExternalInput").ap()
    mhiD_d = dt_("mhiD", [5, 128, 32 * 128], F32, kind="ExternalInput").ap()
    mask_d = dt_("maskc", [128, 896], BF16, kind="ExternalInput").ap()
    identf_d = dt_("identf", [128, 128], F32, kind="ExternalInput").ap()
    identb_d = dt_("identb", [128, 128], BF16, kind="ExternalInput").ap()
    i32q_d = dt_("i32q", [128, 32 * 32], F32, kind="ExternalInput").ap()
    ones_d = dt_("onesr", [1, 128], F32, kind="ExternalInput").ap()
    swz_d = dt_("swz", [H, FW], F32R).ap()
    swzb_d = dt_("swzb", [H, 2], F32R).ap()
    wcpg_d = dt_("wcpg", [H, E], F32R).ap()
    v2h_d = dt_("v2h", [1, H], F32).ap()
    dbg_xn = dt_("dbg_xn", [128, KC * S], BF16, kind="ExternalOutput").ap()
    dbg_wup = dt_("dbg_wup", [128, KC * H], BF16, kind="ExternalOutput").ap()
    dbg_ldn = dt_("dbg_ldn", [128, C32 * E], BF16, kind="ExternalOutput").ap()
    dbg_bp = dt_("dbg_bp", [128, 32], F32, kind="ExternalOutput").ap()
    dbg_bs = dt_("dbg_bs", [128, E], F32, kind="ExternalOutput").ap()
    dbg_bld = dt_("dbg_bld", [128, C32 * 128], F32, kind="ExternalOutput").ap()
    dbg_bhd = dt_("dbg_bhd", [128, C32 * 128], F32, kind="ExternalOutput").ap()
    dbg_upg = dt_("dbg_upg", [128, C32 * FW], F32, kind="ExternalOutput").ap()
    dbg_ucm = dt_("dbg_ucm", [128, 16 * FW], F32, kind="ExternalOutput").ap()
    dbg_q = dt_("dbg_q", [64, S], BF16, kind="ExternalOutput").ap()
    dbg_k = dt_("dbg_k", [64, S], BF16, kind="ExternalOutput").ap()
    dbg_vt = dt_("dbg_vt", [128, NTT * D], BF16, kind="ExternalOutput").ap()
    dbg_o = dt_("dbg_o", [64, S], BF16, kind="ExternalOutput").ap()
    dbg_p = dt_("dbg_p", [128, S], BF16, kind="ExternalOutput").ap()
    dbg_a = dt_("dbg_a", [128, C32 * TW], BF16, kind="ExternalOutput").ap()
    blod_sp = dt_("blod_sp", [128, C32 * 128], F32R).ap()
    bhid_sp = dt_("bhid_sp", [128, C32 * 128], F32R).ap()

    sbm = lambda cx, name, shape, dtype: cx.enter_context(nc.sbuf_tensor(name, shape, dtype))
    sbr = lambda cx, name, shape, dtype: cx.enter_context(nc.sbuf_tensor(name, shape, dtype, side="right"))
    psm = lambda cx, name, shape, dtype: cx.enter_context(nc.psum_tensor(name, shape, dtype))

    wup = sbm(ctx, "wup", [128, KC * H], BF16)
    idb = sbm(ctx, "idb", [128, 128], BF16)
    biasp = sbm(ctx, "biasp", [128, 32], F32)
    bsbc = sbm(ctx, "bsbc", [128, E], F32)
    wqsb = sbm(ctx, "wqsb", [128, KC * D], BF16)
    wksb = sbm(ctx, "wksb", [128, KC * D], BF16)
    wv1sb = sbm(ctx, "wv1sb", [128, KC * D], BF16)
    wv2sb = sbm(ctx, "wv2sb", [64, E], BF16)
    diagq = sbm(ctx, "diagq", [128, 128], BF16)
    lnv = sbm(ctx, "lnv", [128, 16], F32)
    attv = sbm(ctx, "attv", [128, 16], F32)

    pb = [psm(ctx, f"pb{i}", [128, 512], F32) for i in range(6)]
    pT = [psm(ctx, f"pT{i}", [128, 1024], BF16) for i in range(2)]
    pTf = [b[:].bitcast(F32) for b in pT]

    P = Plan(nc, ctx)
    for s_ in ["bld", "bmm", "bev", "bswz", "bg",
               "sx", "ssq", "sst", "sxn", "stt",
               "spj", "ssc", "smx", "sex", "sdg", "spt", "spte", "sot",
               "sup", "sgl", "sdn", "soev", "sxr", "sost"]:
        P.sem(s_)

    P.opi('V', lambda e: e.memset(lnv[:, 15:16], 1e-5), "bev", 1)
    P.opi('S', lambda e: e.dma_start(out=idf[:], in_=identf_d[:]), "bld", 16)
    P.opi('S', lambda e: e.dma_start(out=idb[:], in_=identb_d[:]), "bld", 16)
    P.opi('S', lambda e: e.dma_start(out=onesr[:], in_=ones_d[:]), "bld", 16)
    P.opi('G', lambda e: e.dma_start(out=wqsb[:], in_=wq_d[:], max_dma_last_dim=4096), "bg", 16)
    P.opi('G', lambda e: e.dma_start(out=wksb[:], in_=wk_d[:], max_dma_last_dim=4096), "bg", 16)
    P.opi('G', lambda e: e.dma_start(out=wv1sb[:], in_=wv1_d[:], max_dma_last_dim=4096), "bg", 16)
    P.opi('G', lambda e: e.dma_start(out=wv2sb[:], in_=wv2_d[:], max_dma_last_dim=4096), "bg", 16)
    for pg in range(4):
        for jh in range(2):
            src = bass.AP(wcon_d.tensor, (pg + 4 * 16 * jh) * E, [[4 * E, 16], [128 * E, 32], [1, E]])
            dst = bass.AP(wcpg_d.tensor, (32 * pg) * E + (16 * jh) * 128 * E, [[128 * E, 16], [E, 32], [1, E]])
            P.opi('G', lambda e, s=src, d=dst: e.dma_start(out=d, in_=s, max_dma_last_dim=4096), "bg", 16)

    idf = sbm(ctx, "idf", [128, 128], F32)
    onesr = sbm(ctx, "onesr_sb", [1, 128], F32)
    P.opi('S', lambda e: e.dma_start(out=idf[:], in_=identf_d[:]), "bld", 16)

    def emit_chains(cx, bt, out_lo, out_hi):
        mhist = [sbm(cx, f"mhist{bt}{i}", [128, C32 * 128], F32) for i in range(1)]
        i32qsb = sbm(cx, f"i32qsb{bt}", [128, 32 * 32], F32)
        mlo8 = sbm(cx, f"mlo8{bt}", [128, 4 * 896], F32)
        chb8 = sbm(cx, f"chb8{bt}", [128, 4 * 128], F32)
        cqall = sbm(cx, f"cqall{bt}", [128, 32 * 32], F32)
        cqTall = sbm(cx, f"cqTall{bt}", [128, 32 * 32], F32)
        msrc = mhiU_d if bt == 0 else mhiD_d
        mlo_src = mloU_d if bt == 0 else mloD_d
        P.opi('S', lambda e, i32qsb=i32qsb: e.dma_start(out=i32qsb[:], in_=i32q_d[:]), "bld", 16)
        P.opi('V', lambda e, out_hi=out_hi: e.memset(out_hi[:].bitcast(F32), 0.0), "bev", 1)
        P.waitc('V', "bld")
        P.waitc('V', "bmm")
        P.opi('V', lambda e, cqall=cqall, i32qsb=i32qsb: e.tensor_copy(cqall[:], i32qsb[:]), "bev", 1)
        P.waitc('T', "bev")
        for st in range(5):
            mh = mhist[0]
            P.wait('S', "bmm", P.cnt["bmm"])
            P.opi('S', lambda e, mh=mh, msrc=msrc, st=st: e.dma_start(out=mh[:], in_=msrc[st]), "bld", 16)
            P.waitc('T', "bld")
            banks = (pb[0], pb[1]) if st % 2 == 0 else (pb[2], pb[3])
            for j in range(C32):
                bk = banks[j // 16]
                P.opi('T', lambda e, bk=bk, mh=mh, j=j, cqall=cqall: e.matmul(
                    bk[:, 32 * (j % 16):32 * (j % 16 + 1)], mh[:, 128 * j:128 * (j + 1)],
                    cqall[:, 32 * j:32 * (j + 1)], start=True, stop=True), "bmm", 1)
            P.waitc('V', "bmm")
            P.opi('V', lambda e, b=banks[0], cqall=cqall: e.tensor_copy(cqall[:, 0:512], b[:, 0:512]), "bev", 1)
            P.opi('V', lambda e, b=banks[1], cqall=cqall: e.tensor_copy(cqall[:, 512:1024], b[:, 0:512]), "bev", 1)
            P.waitc('T', "bev")
        if bt == 0:
            P.opi('V', lambda e, cqTall=cqTall, cqall=cqall: e.transpose(cqTall[:], cqall[:]), "bev", 1)
            P.waitc('V', "bev")
        srcq = cqTall if bt == 0 else cqall
        for j in range(C32):
            for pg in range(4):
                P.opi('V', lambda e, j=j, pg=pg, d=out_hi, s=srcq: e.tensor_copy(
                    d[32 * pg:32 * (pg + 1), 128 * j + 32 * pg:128 * j + 32 * (pg + 1)],
                    s[32 * pg:32 * (pg + 1), 32 * j:32 * (j + 1)]), "bev", 1)
        for g in range(8):
            P.wait('S', "bmm", P.cnt["bmm"])
            for cc in range(4):
                P.opi('S', lambda e, cc=cc, s=mlo_src, g=g, mlo8=mlo8: e.dma_start(
                    out=mlo8[:, 896 * cc:896 * (cc + 1)], in_=s[4 * g + cc]), "bld", 16)
            P.waitc('T', "bld")
            for st in range(7):
                bk = pb[st % 2]
                for cc in range(4):
                    rhs = idf[:] if st == 0 else chb8[:, 128 * cc:128 * (cc + 1)]
                    P.opi('T', lambda e, bk=bk, cc=cc, st=st, r=rhs, mlo8=mlo8: e.matmul(
                        bk[:, 128 * cc:128 * (cc + 1)],
                        mlo8[:, 896 * cc + 128 * st:896 * cc + 128 * (st + 1)],
                        r, start=True, stop=True), "bmm", 1)
                P.waitc('V', "bmm")
                if st < 6 or bt == 0:
                    dst = chb8[:, 0:512]
                else:
                    dst = out_lo[:, 512 * g:512 * (g + 1)]
                P.opi('V', lambda e, b=bk, d=dst: e.tensor_copy(d, b[:, 0:512]), "bev", 1)
                P.waitc('T', "bev")
            if bt == 0:
                for cc in range(4):
                    P.opi('T', lambda e, cc=cc, chb8=chb8: e.transpose(
                        pb[4][:, 128 * cc:128 * (cc + 1)], chb8[:, 128 * cc:128 * (cc + 1)], idf[:]), "bmm", 1)
                P.waitc('V', "bmm")
                P.opi('V', lambda e, g=g: e.tensor_copy(out_lo[:, 512 * g:512 * (g + 1)], pb[4][:, 0:512]), "bev", 1)
                P.waitc('T', "bev")

    cxA = ExitStack()
    bloUT = sbr(cxA, "bloUT", [128, C32 * 128], F32R)
    bhiUblk = sbr(cxA, "bhiUblk", [128, C32 * 128], F32R)
    P.on = _rank >= 1
    cx1c = ExitStack()
    emit_chains(cx1c, 0, bloUT, bhiUblk)
    P.waitc('S', "bev")
    P.waitc('T', "bev")
    P.waitc('V', "bmm")
    cx1c.close()

    P.on = _rank >= 3
    cx2 = ExitStack()
    wx = sbr(cx2, "wx", [128, H], F32)
    p1ts = sbr(cx2, "p1ts", [128, C32 * FW], F32R)
    p2t = sbr(cx2, "p2t", [128, C32 * FW], F32R)
    rowsb = sbr(cx2, "rowsb", [1, H], F32)
    bvec = sbr(cx2, "bvec", [128, 2 * C32], F32R)
    p1tbia = sbr(cx2, "p1tbia", [128, C32 * 2], F32R)
    p1sbia = sbr(cx2, "p1sbia", [128, C32 * 2], F32R)

    P.opi('S', lambda e: e.dma_start(out=rowsb[:], in_=bexp_d[:]), "bld", 16)
    P.waitc('T', "bld")
    for c in range(C32):
        P.opi('T', lambda e, c=c: e.matmul(
            pb[3][:, c:c + 1], rowsb[0:1, 128 * c:128 * (c + 1)], onesr[0:1, 0:1],
            start=True, stop=True), "bmm", 1)
    P.waitc('V', "bmm")
    P.opi('V', lambda e: e.memset(bvec[:].bitcast(F32), 0.0), "bev", 1)
    P.waitc('V', "bev")
    P.opi('V', lambda e: e.tensor_copy(
        bass.AP(bvec, 0, [[2 * C32, 128], [2, 32]]), pb[3][:, 0:32]), "bev", 1)
    P.waitc('T', "bev")
    for c in range(C32):
        P.opi('T', lambda e, c=c: e.matmul(
            pb[4][:, 2 * c:2 * (c + 1)], bloUT[:, 128 * c:128 * (c + 1)], bvec[:, 2 * c:2 * (c + 1)],
            start=True, stop=True), "bmm", 1)
    P.waitc('V', "bmm")
    P.opi('V', lambda e: e.tensor_copy(p1tbia[:], pb[4][:, 0:64]), "bev", 1)
    P.waitc('S', "bev")
    P.opi('S', lambda e: e.dma_start(
        out=bass.AP(swzb_d.tensor, 0, [[2, 128], [128 * 2, 32], [1, 2]]),
        in_=bass.AP(p1tbia, 0, [[C32 * 2, 128], [2, 32], [1, 2]])), "bswz", 16)
    P.waitc('S', "bswz")
    for pg in range(4):
        P.opi('S', lambda e, pg=pg: e.dma_start(
            out=bass.AP(p1sbia, (32 * pg) * (C32 * 2), [[C32 * 2, 32], [2, 32], [1, 2]]),
            in_=bass.AP(swzb_d.tensor, pg * 2, [[128 * 2, 32], [4 * 2, 32], [1, 2]])), "bswz", 16)
    P.waitc('T', "bswz")
    for j in range(C32):
        P.opi('T', lambda e, j=j: e.matmul(
            pb[5][:, 2 * j:2 * (j + 1)], bhiUblk[:, 128 * j:128 * (j + 1)],
            p1sbia[:, 2 * j:2 * (j + 1)], start=True, stop=True), "bmm", 1)
    P.waitc('V', "bmm")
    P.opi('V', lambda e: e.tensor_copy(p1sbia[:], pb[5][:, 0:64]), "bev", 1)
    P.waitc('S', "bev")
    for pg in range(4):
        P.opi('S', lambda e, pg=pg: e.dma_start(
            out=bass.AP(v2h_d.tensor, pg, [[128, 32], [4, 32], [1, 1]]),
            in_=bass.AP(p1sbia, (32 * pg) * 64, [[64, 32], [2, 32], [1, 1]]).bitcast(F32),
            allow_slow_non_contiguous=True), "bswz", 16)

    for w in range(4):
        for ki in range(2):
            kappa = 2 * w + ki
            P.wait('S', "bmm", P.cnt["bmm"])
            P.opi('S', lambda e, kappa=kappa: e.dma_start(out=wx[:], in_=wexp_d[128 * kappa:128 * (kappa + 1), :]), "bld", 16)
            P.waitc('T', "bld")
            for c4 in range(8):
                tbk = pT[c4 % 2]
                if w + ki + c4 > 0:
                    P.wait('T', "bev", P.cnt["bev"] - 1)
                for cc in range(4):
                    P.opi('T', lambda e, tbk=tbk, c4=c4, cc=cc: e.transpose(
                        tbk[:].bitcast(F32)[:, 128 * cc:128 * (cc + 1)],
                        wx[:, 128 * (4 * c4 + cc):128 * (4 * c4 + cc + 1)], idf[:]), "bmm", 1)
                P.waitc('V', "bmm")
                P.opi('V', lambda e, tbk=tbk, c4=c4, ki=ki: e.tensor_copy(
                    bass.AP(p1ts, FW * (4 * c4) + 128 * ki, [[C32 * FW, 128], [FW, 4], [1, 128]]),
                    tbk[:].bitcast(F32)[:, 0:512]), "bev", 1)
        P.waitc('T', "bev")
        for c in range(C32):
            P.wait('T', "bev", P.cnt["bev"] - 1)
            P.opi('T', lambda e, c=c: e.matmul(
                pb[4 + c % 2][:, 0:FW], bloUT[:, 128 * c:128 * (c + 1)], p1ts[:, FW * c:FW * (c + 1)],
                start=True, stop=True), "bmm", 1)
            P.waitc('V', "bmm")
            P.opi('V', lambda e, c=c: e.tensor_copy(p1ts[:, FW * c:FW * (c + 1)], pb[4 + c % 2][:, 0:FW]), "bev", 1)
        P.waitc('S', "bev")
        P.waitc('S', "bswz")
        P.opi('S', lambda e: e.dma_start(
            out=bass.AP(swz_d.tensor, 0, [[FW, 128], [128 * FW, 32], [1, FW]]),
            in_=p1ts[:]), "bswz", 16)
        P.waitc('S', "bswz")
        for pg in range(4):
            P.opi('S', lambda e, pg=pg: e.dma_start(
                out=bass.AP(p1ts, (32 * pg) * (C32 * FW), [[C32 * FW, 32], [FW, 32], [1, FW]]),
                in_=bass.AP(swz_d.tensor, pg * FW, [[128 * FW, 32], [4 * FW, 32], [1, FW]])), "bswz", 16)
        P.waitc('T', "bswz")
        for j in range(C32):
            P.wait('T', "bev", P.cnt["bev"] - 1)
            P.opi('T', lambda e, j=j: e.matmul(
                pb[4 + j % 2][:, 0:FW], bhiUblk[:, 128 * j:128 * (j + 1)], p1ts[:, FW * j:FW * (j + 1)],
                start=True, stop=True), "bmm", 1)
            P.waitc('V', "bmm")
            P.opi('V', lambda e, j=j: e.tensor_copy(p2t[:, FW * j:FW * (j + 1)], pb[4 + j % 2][:, 0:FW]), "bev", 1)
        P.waitc('T', "bev")
        for j2 in range(16):
            tbk = pT[j2 % 2]
            if j2 >= 2:
                P.wait('T', "bev", P.cnt["bev"] - 1)
            for k4 in range(4):
                jj = 2 * j2 + k4 // 2
                sub = k4 % 2
                P.opi('T', lambda e, tbk=tbk, jj=jj, sub=sub, k4=k4: e.transpose(
                    tbk[:].bitcast(F32)[:, 128 * k4:128 * (k4 + 1)],
                    p2t[:].bitcast(F32)[:, FW * jj + 128 * sub:FW * jj + 128 * (sub + 1)], idf[:]), "bmm", 1)
            P.waitc('V', "bmm")
            P.opi('V', lambda e, tbk=tbk, w=w, j2=j2: e.tensor_copy(
                bass.AP(wup, (2 * w) * H + 128 * (2 * j2), [[KC * H, 128], [128, 2], [H, 2], [1, 128]]),
                tbk[:].bitcast(F32)[:, 0:512].bitcast(BF16) if False else tbk[:].bitcast(F32)[:, 0:512]), "bev", 1)

    P.waitc('S', "bswz")
    P.wait('S', "bmm", P.cnt["bmm"])
    P.opi('S', lambda e: e.dma_start(out=rowsb[:], in_=v2h_d[:]), "bld", 16)
    P.waitc('T', "bld")
    for c in range(C32):
        P.opi('T', lambda e, c=c: e.matmul(
            pb[3][:, c:c + 1], rowsb[0:1, 128 * c:128 * (c + 1)], onesr[0:1, 0:1],
            start=True, stop=True), "bmm", 1)
    P.wait('S', "bmm", P.cnt["bmm"])
    P.opi('S', lambda e: e.dma_start(out=rowsb[:], in_=ubias_d[:]), "bld", 16)
    P.waitc('T', "bld")
    for c in range(C32):
        P.opi('T', lambda e, c=c: e.matmul(
            pb[4][:, c:c + 1], rowsb[0:1, 128 * c:128 * (c + 1)], onesr[0:1, 0:1],
            start=True, stop=True), "bmm", 1)
    P.waitc('V', "bmm")
    P.opi('V', lambda e: e.tensor_copy(biasp[:], pb[3][:, 0:32]), "bev", 1)
    P.waitc('V', "bev")
    P.opi('V', lambda e: e.tensor_add(biasp[:], biasp[:], pb[4][:, 0:32]), "bev", 1)
    P.wait('S', "bev", P.cnt["bev"])
    P.opi('S', lambda e: e.dma_start(out=rowsb[0:1, 0:E], in_=bcon_d[:]), "bld", 16)
    P.opi('S', lambda e: e.dma_start(out=rowsb[0:1, E:2 * E], in_=dbias_d[:]), "bld", 16)
    P.waitc('V', "bld")
    P.opi('V', lambda e: e.tensor_add(rowsb[0:1, 0:E], rowsb[0:1, 0:E], rowsb[0:1, E:2 * E]), "bev", 1)
    P.waitc('T', "bev")
    for hf in range(2):
        P.opi('T', lambda e, hf=hf: e.matmul(
            pb[5][:, 0:512], onesr[0:1, 0:128], rowsb[0:1, 512 * hf:512 * (hf + 1)],
            start=True, stop=True), "bmm", 1)
        P.waitc('V', "bmm")
        P.opi('V', lambda e, hf=hf: e.tensor_copy(bsbc[:, 512 * hf:512 * (hf + 1)], pb[5][:, 0:512]), "bev", 1)
        P.waitc('T', "bev")
    P.waitc('S', "bev")
    P.waitc('V', "bswz")
    cx2.close()
    cxA.close()

    P.on = _rank >= 4
    ldn = sbr(ctx, "ldn", [128, C32 * E], BF16)
    cx3o = ExitStack()
    bloD2 = sbr(cx3o, "bloD2", [128, C32 * 128], F32R)
    bhiD2 = sbr(cx3o, "bhiD2", [128, C32 * 128], F32R)
    cx3c = ExitStack()
    emit_chains(cx3c, 1, bloD2, bhiD2)
    P.waitc('S', "bev")
    P.waitc('T', "bev")
    P.waitc('V', "bmm")
    cx3c.close()
    cx3 = ExitStack()
    upg = sbr(cx3, "upg", [128, C32 * FW], F32R)
    ucmh = sbr(cx3, "ucmh", [128, 16 * FW], F32R)
    wct = [sbr(cx3, f"wct{i}", [128, FW], F32R) for i in range(2)]
    P.waitc('S', "bg")
    import os as _os
    _ndw = int(_os.environ.get("KDW", "4"))
    _dwsec = _os.environ.get("KDWS", "ul")
    for w in range(_ndw):
        for j in (range(C32) if 'u' in _dwsec else []):
            b = wct[j % 2]
            P.wait('S', "bmm", P.cnt["bmm"] - 1)
            P.opi('S', lambda e, b=b, j=j, w=w: e.dma_start(
                out=b[:], in_=bass.AP(wcpg_d.tensor, (128 * j) * E + FW * w, [[E, 128], [1, FW]])), "bld", 16)
            P.waitc('T', "bld")
            P.wait('T', "bev", P.cnt["bev"] - 1)
            P.opi('T', lambda e, j=j, b=b: e.matmul(
                pb[4 + j % 2][:, 0:FW], bhiD2[:, 128 * j:128 * (j + 1)], b[:],
                start=True, stop=True), "bmm", 1)
            P.waitc('V', "bmm")
            P.opi('V', lambda e, j=j: e.tensor_copy(upg[:, FW * j:FW * (j + 1)], pb[4 + j % 2][:, 0:FW]), "bev", 1)
        P.waitc('S', "bev")
        P.waitc('S', "bswz")
        for pg in range(4):
            P.opi('S', lambda e, pg=pg: e.dma_start(
                out=bass.AP(swz_d.tensor, pg * FW, [[128 * FW, 32], [4 * FW, 32], [1, FW]]),
                in_=bass.AP(upg, (32 * pg) * (C32 * FW), [[C32 * FW, 32], [FW, 32], [1, FW]])), "bswz", 16)
        P.waitc('S', "bswz")
        for ch in (range(1 if 'h' in _dwsec else 2) if 'l' in _dwsec else []):
            P.wait('S', "bmm", P.cnt["bmm"])
            for qh in range(2):
                P.opi('S', lambda e, ch=ch, qh=qh: e.dma_start(
                    out=bass.AP(ucmh, (8 * qh) * FW, [[16 * FW, 128], [FW, 8], [1, FW]]),
                    in_=bass.AP(swz_d.tensor, (16 * ch + 8 * qh) * 128 * FW, [[FW, 128], [128 * FW, 8], [1, FW]])), "bld", 16)
            P.waitc('T', "bld")
            for cc in (range(16) if 'm' in _dwsec or 'l' == _dwsec.replace('x','') and False else range(16)):
                c = 16 * ch + cc
                if 'x' in _dwsec:
                    continue
                P.wait('T', "bev", P.cnt["bev"] - 1)
                P.opi('T', lambda e, c=c, cc=cc: e.matmul(
                    pb[4 + cc % 2][:, 0:FW], bloD2[:, 128 * c:128 * (c + 1)], ucmh[:, FW * cc:FW * (cc + 1)],
                    start=True, stop=True), "bmm", 1)
                if 'y' in _dwsec:
                    continue
                P.waitc('V', "bmm")
                _dst = (upg if 'z' in _dwsec else ldn)
                if 'z' in _dwsec:
                    P.opi('V', lambda e, cc=cc: e.tensor_copy(
                        upg[:].bitcast(F32)[:, FW * cc:FW * (cc + 1)], pb[4 + cc % 2][:, 0:FW]), "bev", 1)
                elif 'w' in _dwsec:
                    P.opi('V', lambda e, cc=cc: e.tensor_copy(
                        ldn[:, FW * cc:FW * (cc + 1)], pb[4 + cc % 2][:, 0:FW]), "bev", 1)
                elif 'a' in _dwsec:
                    P.opi('V', lambda e, c=c: e.tensor_copy(
                        ldn[:, 2 * FW * c:2 * FW * c + FW], pb[4 + c % 2][:, 0:FW]), "bev", 1)
                elif 'b' in _dwsec:
                    P.opi('V', lambda e, c=c: e.tensor_copy(
                        ldn[:, E * c:E * c + FW], pb[4 + c % 2][:, 0:FW]), "bev", 1)
                else:
                    P.opi('V', lambda e, c=c, w=w, cc=cc: e.tensor_copy(
                        ldn[:, E * c + FW * w:E * c + FW * (w + 1)], pb[4 + cc % 2][:, 0:FW]), "bev", 1)
    P.waitc('S', "bev")
    P.waitc('V', "bswz")
    if limit in ('dbg', 'downfold'):
        P.opi('S', lambda e: e.dma_start(out=dbg_bld[:], in_=bloD2[:].bitcast(F32)), "bswz", 16)
        P.opi('S', lambda e: e.dma_start(out=dbg_bhd[:], in_=bhiD2[:].bitcast(F32)), "bswz", 16)
        P.opi('S', lambda e: e.dma_start(out=dbg_upg[:], in_=upg[:].bitcast(F32)), "bswz", 16)
        P.opi('S', lambda e: e.dma_start(out=dbg_ucm[:], in_=ucmh[:].bitcast(F32)), "bswz", 16)
        P.waitc('S', "bswz")
    cx3.close()
    cx3o.close()

    maskb = sbr(ctx, "maskb", [128, 896], BF16)
    xn = sbr(ctx, "xn", [128, KC * S], BF16)
    abuf = sbr(ctx, "abuf", [128, C32 * TW], BF16)
    qbuf = sbr(ctx, "qbuf", [64, S], BF16)
    kbuf = sbr(ctx, "kbuf", [64, S], BF16)
    o64t = sbr(ctx, "o64t", [64, S], BF16)
    vt = sbr(ctx, "vt", [128, NTT * D], BF16)
    pbuf = sbr(ctx, "pbuf", [128, S], BF16)
    ptb = sbr(ctx, "ptb", [128, 2 * 512], BF16)
    xtile = [sbr(ctx, f"xtile{i}", [128, E], F32) for i in range(2)]
    xnt = [sbr(ctx, f"xnt{i}", [128, E], BF16) for i in range(2)]
    ost = [sbr(ctx, f"ost{i}", [128, E], F32) for i in range(2)]

    P.on = True
    P.opi('S', lambda e: e.dma_start(out=maskb[:], in_=mask_d[:]), "bld", 16)
    P.waitc('T', "bev")
    P.waitc('C', "bev")
    P.waitc('V', "bmm")
    P.waitc('C', "bld")
    P.waitc('V', "bld")
    P.waitc('C', "bg")
    P.waitc('T', "bg")

    nseq = 0 if _rank <= 5 else (1 if limit in ('ln', 'proj', 'attn', 'ffn1') else BPC)
    for b in range(nseq):
        P.wait('C', "sost", P.cnt["sost"])
        for i in range(NTT):
            sl = i % 2
            xt = xtile[sl]
            gi = b * NTT + i
            if gi >= 2:
                P.wait('S', "sxn", 2 * (gi - 2) + 1)
            if gi >= 2:
                P.wait('S', "ssq", P.cnt["ssq"] - 3)
            P.opi('S', lambda e, xt=xt, b=b, i=i: e.dma_start(
                out=xt[:], in_=x_d[b, 128 * i:128 * (i + 1), :]), "sx", 16)
            P.waitc('V', "sx")
            P.waitc('C', "sx")
            c0, c1, c2, c3, c5 = sl * 8, sl * 8 + 1, sl * 8 + 2, sl * 8 + 3, sl * 8 + 5
            P.opi('V', lambda e, xt=xt, c0=c0: e.reduce_sum(lnv[:, c0:c0 + 1], xt[:], axis=AX.X), "sst", 1)
            P.waitc('V', "sst")
            if gi >= 2:
                P.wait('C', "stt", P.cnt["stt"] - 8)
            P.opi('C', lambda e, xt=xt, sq=xnt[sl], c1=c1: e.activation(
                sq[:], xt[:], AF.Square, accum_out=lnv[:, c1:c1 + 1]), "ssq", 1)
            P.waitc('V', "ssq")
            P.opi('V', lambda e, c0=c0, c2=c2: e.tensor_scalar_mul(lnv[:, c2:c2 + 1], lnv[:, c0:c0 + 1], -1.0 / E), "sst", 1)
            P.opi('V', lambda e, c1=c1, c3=c3: e.tensor_scalar_mul(lnv[:, c3:c3 + 1], lnv[:, c1:c1 + 1], 1.0 / E), "sst", 1)
            P.waitc('V', "sst")
            c4 = sl * 8 + 4
            P.opi('V', lambda e, c2=c2, c4=c4: e.tensor_mul(lnv[:, c4:c4 + 1], lnv[:, c2:c2 + 1], lnv[:, c2:c2 + 1]), "sst", 1)
            P.waitc('V', "sst")
            P.opi('V', lambda e, c3=c3, c4=c4: e.tensor_sub(lnv[:, c3:c3 + 1], lnv[:, c3:c3 + 1], lnv[:, c4:c4 + 1]), "sst", 1)
            P.waitc('V', "sst")
            P.waitc('C', "sst")
            P.opi('C', lambda e, c3=c3, c5=c5: e.activation(lnv[:, c5:c5 + 1], lnv[:, c3:c3 + 1], AF.Sqrt, bias=lnv[:, 15:16]), "ssq", 1)
            P.waitc('V', "ssq")
            P.opi('V', lambda e, c5=c5: e.reciprocal(lnv[:, c5:c5 + 1], lnv[:, c5:c5 + 1]), "sst", 1)
            P.waitc('V', "sst")
            P.opi('V', lambda e, xt=xt, xo=xnt[sl], c2=c2, c5=c5: e.tensor_scalar(
                xo[:], xt[:], lnv[:, c2:c2 + 1], lnv[:, c5:c5 + 1], ALU.add, ALU.mult), "sxn", 1)
            P.waitc('T', "sxn")
            tb = pT[sl]
            for kappa in range(KC):
                P.opi('T', lambda e, tb=tb, xo=xnt[sl], kappa=kappa: e.transpose(
                    tb[:, 128 * kappa:128 * (kappa + 1)], xo[:, 128 * kappa:128 * (kappa + 1)], idb[:]), "stt", 1)
            P.waitc('V', "stt")
            P.opi('V', lambda e, tb=tb, i=i: e.tensor_copy(
                bass.AP(xn, 128 * i, [[KC * S, 128], [S, 8], [1, 128]]), tb[:]), "sxn", 1)
        if limit == 'ln':
            continue
        P.waitc('T', "sxn")
        for tw in range(4):
            for which, wsb_, dst in ((0, wqsb, qbuf), (1, wksb, kbuf), (2, wv1sb, None)):
                bank = pb[which]
                for kappa in range(KC):
                    P.opi('T', lambda e, bank=bank, w_=wsb_, kappa=kappa, tw=tw: e.matmul(
                        bank[0:64, 0:512], w_[:, D * kappa:D * (kappa + 1)],
                        bass.AP(xn, S * kappa + 512 * tw, [[KC * S, 128], [1, 512]]),
                        start=(kappa == 0), stop=(kappa == KC - 1)), "ssc", 1)
                P.waitc('C', "ssc")
                if which < 2:
                    P.opi('C', lambda e, bank=bank, dst=dst, tw=tw: e.activation(
                        dst[0:64, 512 * tw:512 * (tw + 1)], bank[0:64, 0:512], AF.Copy, scale=SCALE_QK), "spj", 1)
                else:
                    P.opi('C', lambda e, bank=bank: e.activation(
                        pbuf[0:64, 0:512], bank[0:64, 0:512], AF.Copy), "spj", 1)
            P.waitc('T', "spj")
            for s4 in range(4):
                P.opi('T', lambda e, s4=s4: e.transpose(
                    pT[0][:, 64 * s4:64 * (s4 + 1)], pbuf[0:64, 128 * s4:128 * (s4 + 1)], idb[0:64, 0:64]), "stt", 1)
            P.waitc('V', "stt")
            P.wait('V', "sot", P.cnt["sot"])
            P.opi('V', lambda e, tw=tw: e.tensor_copy(vt[:, 256 * tw:256 * (tw + 1)], pT[0][:, 0:256]), "spj", 1)
            P.waitc('T', "spj")
        if limit == 'proj':
            continue
        blkctr = getattr(P, "_blkctr", 0)
        if not hasattr(P, "_sexbase"):
            P._sexbase = P.cnt["sex"]
        for qi in range(NTT):
            bd = qi // 4
            o = 128 * (qi % 4)
            nbk = bd + 1
            P.waitc('V', "sex")
            P.waitc('V', "spt")
            myblk = []
            for blk in range(nbk):
                bank = pb[blkctr % 4]
                myblk.append(bank)
                P.wait('T', "sex", P._sexbase + blkctr - 3)
                blkctr += 1
                P.opi('T', lambda e, bank=bank, qi=qi, blk=blk: e.matmul(
                    bank[:, 0:512], qbuf[0:64, 128 * qi:128 * (qi + 1)], kbuf[0:64, 512 * blk:512 * (blk + 1)],
                    start=True, stop=True), "ssc", 1)
                P.waitc('V', "ssc")
                P.opi('V', lambda e, bank=bank, blk=blk: e.reduce_max(
                    attv[:, blk:blk + 1], bank[:, 0:512], axis=AX.X), "smx", 1)
            P.waitc('V', "smx")
            P.opi('V', lambda e, nbk=nbk: e.reduce_max(
                attv[:, 4:5], attv[:, 0:nbk], axis=AX.X, negate=True), "smx", 1)
            P.waitc('V', "smx")
            P.waitc('C', "smx")
            P.waitc('C', "spt")
            for blk in range(nbk):
                bank = myblk[blk]
                P.opi('C', lambda e, bank=bank, blk=blk: e.activation(
                    pbuf[:, 512 * blk:512 * (blk + 1)], bank[:, 0:512], AF.Exp,
                    bias=attv[:, 4:5], accum_out=attv[:, 8 + blk:9 + blk]), "sex", 1)
            P.waitc('V', "sex")
            P.opi('V', lambda e, bd=bd, o=o: e.tensor_mul(
                pbuf[:, 512 * bd:512 * (bd + 1)], pbuf[:, 512 * bd:512 * (bd + 1)],
                maskb[:, 384 - o:896 - o]), "sdg", 1)
            P.waitc('V', "sdg")
            P.opi('V', lambda e, bd=bd: e.reduce_sum(
                attv[:, 8 + bd:9 + bd], pbuf[:, 512 * bd:512 * (bd + 1)], axis=AX.X), "sdg", 1)
            P.waitc('V', "sdg")
            P.opi('V', lambda e, nbk=nbk: e.reduce_sum(attv[:, 13:14], attv[:, 8:8 + nbk], axis=AX.X), "sdg", 1)
            P.waitc('V', "sdg")
            P.opi('V', lambda e: e.reciprocal(attv[:, 13:14], attv[:, 13:14]), "sdg", 1)
            P.waitc('V', "sdg")
            P.opi('V', lambda e: e.tensor_scalar_mul(diagq[:], idb[:], attv[:, 13:14]), "sdg", 1)
            P.waitc('T', "sdg")
            nm = qi + 1
            for g4 in range(0, nm, 4):
                mm4 = list(range(g4, min(g4 + 4, nm)))
                ptbank = pTf[(g4 // 4) % 2]
                P.wait('T', "spte", P.cnt["spte"] - 1)
                for ii, m in enumerate(mm4):
                    P.opi('T', lambda e, ptbank=ptbank, m=m, ii=ii: e.matmul(
                        ptbank[:, 128 * ii:128 * (ii + 1)], pbuf[:, 128 * m:128 * (m + 1)], diagq[:],
                        start=True, stop=True), "spt", 1)
                P.waitc('V', "spt")
                P.wait('V', "sot", P.cnt["sot"] - 4)
                sl4 = (g4 // 4) % 2
                P.opi('V', lambda e, ptbank=ptbank, sl4=sl4, n4=len(mm4): e.tensor_copy(
                    ptb[:, 512 * sl4:512 * sl4 + 128 * n4], ptbank[:, 0:128 * n4]), "spte", 1)
                P.waitc('T', "spte")
                for ii, m in enumerate(mm4):
                    P.opi('T', lambda e, m=m, qi=qi, sl4=sl4, ii=ii: e.matmul(
                        pb[5][0:64, 0:128], vt[:, 64 * m:64 * (m + 1)], ptb[:, 512 * sl4 + 128 * ii:512 * sl4 + 128 * (ii + 1)],
                        start=(m == 0), stop=(m == qi)), "sot", 1)
            P.waitc('C', "sot")
            P.opi('C', lambda e, qi=qi: e.activation(
                o64t[0:64, 128 * qi:128 * (qi + 1)], pb[5][0:64, 0:128], AF.Copy), "sot", 1)
            P.waitc('T', "sot")
        P._blkctr = blkctr
        if limit == 'attn':
            continue
        P.waitc('T', "sex")
        P.waitc('T', "sot")
        for tw in range(NTW):
            t_id = tw
            slx = t_id % 2
            upbanks = (pb[0], pb[1])
            for c in range(C32):
                bank = upbanks[c % 2]
                P.wait('T', "sgl", P.cnt["sgl"] - 1)
                for kappa in range(KC):
                    P.opi('T', lambda e, bank=bank, c=c, kappa=kappa, tw=tw: e.matmul(
                        bank[:, 0:TW],
                        bass.AP(wup, H * kappa + c, [[KC * H, 128], [128, 32], [32, 4]]),
                        bass.AP(xn, S * kappa + TW * tw, [[KC * S, 128], [1, TW]]),
                        start=(kappa == 0), stop=(kappa == KC - 1)), "sup", 1)
                P.waitc('C', "sup")
                P.opi('C', lambda e, bank=bank, c=c: e.activation(
                    abuf[:, TW * c:TW * (c + 1)], bank[:, 0:TW], AF.Gelu, bias=biasp[:, c:c + 1]), "sgl", 1)
            P.waitc('T', "sgl")
            if tw == 0:
                P.waitc('S', "sxn")
            P.wait('S', "soev", P.cnt["soev"] - 2)
            P.opi('S', lambda e, b=b, t_id=t_id, slx=slx: e.dma_start(
                out=xtile[slx][:], in_=x_d[b, 128 * t_id:128 * (t_id + 1), :]), "sxr", 16)
            dnbanks = (pb[2], pb[3])
            for fo in range(2):
                bank = dnbanks[fo]
                P.wait('T', "soev", P.cnt["soev"] - 3)
                P.opi('T', lambda e, bank=bank, t_id=t_id, fo=fo: e.matmul(
                    bank[:, 0:512], o64t[0:64, 128 * t_id:128 * (t_id + 1)], wv2sb[0:64, 512 * fo:512 * (fo + 1)],
                    start=True, stop=False), "sdn", 1)
                for c in range(C32):
                    P.opi('T', lambda e, bank=bank, c=c, tw=tw, fo=fo: e.matmul(
                        bank[:, 0:512], abuf[:, TW * c:TW * (c + 1)],
                        ldn[:, E * c + 512 * fo:E * c + 512 * (fo + 1)],
                        start=False, stop=(c == C32 - 1)), "sdn", 1)
                P.waitc('V', "sdn")
                P.waitc('V', "sxr")
                P.wait('V', "sost", 16 * (P.cnt["sost"] // 16 - 1))
                P.opi('V', lambda e, bank=bank, slx=slx, fo=fo: e.tensor_add(
                    ost[slx][:, 512 * fo:512 * (fo + 1)], bank[:, 0:512], xtile[slx][:, 512 * fo:512 * (fo + 1)]), "soev", 1)
                P.waitc('V', "soev")
                P.opi('V', lambda e, slx=slx, fo=fo: e.tensor_add(
                    ost[slx][:, 512 * fo:512 * (fo + 1)], ost[slx][:, 512 * fo:512 * (fo + 1)],
                    bsbc[:, 512 * fo:512 * (fo + 1)]), "soev", 1)
            P.waitc('S', "soev")
            P.opi('S', lambda e, b=b, t_id=t_id, slx=slx: e.dma_start(
                out=out_d[b, 128 * t_id:128 * (t_id + 1), :], in_=ost[slx][:]), "sost", 16)
            if b == 0 and tw == 0 and limit == 'dbg':
                P.waitc('S', "sgl")
                P.opi('S', lambda e: e.dma_start(out=dbg_a[:], in_=abuf[:]), "sost", 16)

    P.wait('S', "sost", P.cnt["sost"])
    P.replay()
    ctx.close()
    return nc


_NC_CACHE = {}


def _chunk_rows(w):
    w = np.asarray(w, np.float32).reshape(KC, 128, D).transpose(1, 0, 2).reshape(128, KC * D)
    return np.ascontiguousarray(w)


def kernel(L0_attn_x, L0_attn_w_q, L0_attn_w_k, L0_attn_w_v1, L0_attn_w_v2,
           L0_ffn_W_exp, L0_ffn_b_exp, L0_ffn_up_weights, L0_ffn_up_bias,
           L0_ffn_down_weights, L0_ffn_W_con, L0_ffn_b_con, L0_ffn_down_bias):
    x = np.ascontiguousarray(np.asarray(L0_attn_x, dtype=np.float32))
    mask, identf, identb, i32q, ones_r = _host_consts()
    mloU = _stage_lo_T(np.asarray(L0_ffn_up_weights, np.float32))
    mloD = _stage_lo_T(np.asarray(L0_ffn_down_weights, np.float32))
    mhiU = _stage_hi_T(np.asarray(L0_ffn_up_weights, np.float32))
    mhiD = _stage_hi_T(np.asarray(L0_ffn_down_weights, np.float32))

    common = {
        "wq": _chunk_rows(L0_attn_w_q),
        "wk": _chunk_rows(L0_attn_w_k),
        "wv1": _chunk_rows(L0_attn_w_v1),
        "wv2": np.ascontiguousarray(L0_attn_w_v2, np.float32),
        "wexp": np.ascontiguousarray(L0_ffn_W_exp, np.float32),
        "wcon": np.ascontiguousarray(L0_ffn_W_con, np.float32),
        "bexp": np.ascontiguousarray(np.asarray(L0_ffn_b_exp, np.float32).reshape(1, H)),
        "ubias": np.ascontiguousarray(np.asarray(L0_ffn_up_bias, np.float32).reshape(1, H)),
        "bcon": np.ascontiguousarray(np.asarray(L0_ffn_b_con, np.float32).reshape(1, E)),
        "dbias": np.ascontiguousarray(np.asarray(L0_ffn_down_bias, np.float32).reshape(1, E)),
        "mloU": mloU, "mloD": mloD, "mhiU": mhiU, "mhiD": mhiD,
        "maskc": mask, "identf": identf, "identb": identb,
        "i32q": i32q, "onesr": ones_r,
    }
    in_maps = []
    for core in range(NCORES):
        m = dict(common)
        m["x"] = np.ascontiguousarray(x[BPC * core:BPC * (core + 1)])
        in_maps.append(m)

    if "nc" not in _NC_CACHE:
        _NC_CACHE["nc"] = build_nc()
    nc = _NC_CACHE["nc"]
    res = run_bass_kernel_spmd(nc, in_maps, core_ids=list(range(NCORES)))
    return np.concatenate([res.results[i]["out"] for i in range(NCORES)], axis=0)




# version 24
# speedup vs baseline: 1.2123x; 1.0395x over previous
import numpy as np
import ml_dtypes
import concourse.bass as bass
import concourse.mybir as mybir
from concourse.bass_utils import run_bass_kernel_spmd
from contextlib import ExitStack

F32 = mybir.dt.float32
F32R = mybir.dt.float32r
BF16 = mybir.dt.bfloat16
AF = mybir.ActivationFunctionType
AX = mybir.AxisListType
ALU = mybir.AluOpType
bfloat16 = ml_dtypes.bfloat16

NCORES = 8
BPC = 4
S = 2048
E = 1024
H = 4096
D = 64
KC = 8
C32 = 32
NTT = S // 128
TW = 128
NTW = S // TW
FW = 256
SCALE_QK = float(64 ** -0.25)


def _stage_lo_T(wstages):
    M = np.zeros((32, 7, 128, 128), np.float32)
    for st in range(7):
        s = 1 << st
        w = wstages[st]
        ig = np.arange(4096)
        ig = ig[(ig & s) == 0]
        widx = (ig // (2 * s)) * s + (ig % s)
        c, i = np.divmod(ig, 128)
        i2 = i + s
        M[c, st, i, i] = w[widx, 0, 0]
        M[c, st, i2, i] = w[widx, 0, 1]
        M[c, st, i, i2] = w[widx, 1, 0]
        M[c, st, i2, i2] = w[widx, 1, 1]
    return np.ascontiguousarray(M.transpose(0, 2, 1, 3).reshape(32, 128, 7 * 128))


def _stage_hi_T(wstages):
    A = np.zeros((5, 32, 128, 128), np.float32)
    for st in range(7, 12):
        s = 1 << st
        sc = s // 128
        w = wstages[st]
        ig = np.arange(4096)
        ig = ig[(ig & s) == 0]
        widx = (ig // (2 * s)) * s + (ig % s)
        ci, p = np.divmod(ig, 128)
        ci2 = ci + sc
        j, pg = np.divmod(p, 4)
        st5 = st - 7
        A[st5, j, 32 * pg + ci, 32 * pg + ci] = w[widx, 0, 0]
        A[st5, j, 32 * pg + ci2, 32 * pg + ci] = w[widx, 0, 1]
        A[st5, j, 32 * pg + ci, 32 * pg + ci2] = w[widx, 1, 0]
        A[st5, j, 32 * pg + ci2, 32 * pg + ci2] = w[widx, 1, 1]
    return np.ascontiguousarray(A.transpose(0, 2, 1, 3).reshape(5, 128, 32 * 128))


def _host_consts():
    r = np.arange(128)[:, None]
    u = np.arange(896)[None, :]
    mask = (u <= r + 384).astype(bfloat16)
    identf = np.eye(128, dtype=np.float32)
    identb = np.eye(128, dtype=bfloat16)
    i32q = np.zeros((128, 32 * 32), np.float32)
    for j in range(32):
        for pg in range(4):
            i32q[32 * pg:32 * (pg + 1), 32 * j:32 * (j + 1)] = np.eye(32, dtype=np.float32)
    ones_r = np.ones((1, 128), np.float32)
    return mask, identf, identb, i32q, ones_r


class Plan:
    def __init__(self, nc, ctx):
        self.nc = nc
        self.ctx = ctx
        self.ops = []
        self.semh = {}
        self.cnt = {}
        self.on = True

    def sem(self, name):
        self.semh[name] = self.ctx.enter_context(self.nc.semaphore(name))
        self.cnt[name] = 0
        return name

    def op(self, eng, f):
        if not self.on:
            return
        self.ops.append((eng, f))

    def opi(self, eng, f, sem, k=1):
        if not self.on:
            return self.cnt[sem]
        h = self.semh[sem]
        self.cnt[sem] += k
        self.ops.append((eng, lambda e, f=f, h=h, k=k: f(e).then_inc(h, k)))
        return self.cnt[sem]

    def wait(self, eng, sem, val):
        if val <= 0 or not self.on:
            return
        h = self.semh[sem]
        self.ops.append((eng, lambda e, h=h, v=val: e.wait_ge(h, v)))

    def waitc(self, eng, sem):
        self.wait(eng, sem, self.cnt[sem])

    def replay(self):
        nc = self.nc
        ops = self.ops
        with nc.Block() as block:
            @block.tensor
            def _(t):
                for e, f in ops:
                    if e == 'T':
                        f(t)

            @block.vector
            def _(v):
                for e, f in ops:
                    if e == 'V':
                        f(v)

            @block.scalar
            def _(c):
                for e, f in ops:
                    if e == 'C':
                        f(c)

            @block.sync
            def _(s):
                for e, f in ops:
                    if e == 'S':
                        f(s)

            @block.gpsimd
            def _(g):
                for e, f in ops:
                    if e == 'G':
                        f(g)


_PH = {'loads': 0, 'bhichain': 1, 'blochain': 2, 'upfold': 3, 'downfold': 4,
       'build': 5, 'ln': 6, 'proj': 7, 'attn': 8, 'full': 9, 'dbg': 9}


def build_nc(limit='full'):
    _rank = _PH[limit]
    nc = bass.Bass("TRN2", target_bir_lowering=False, dynamic_dma_scratch_size=2048)
    ctx = ExitStack()

    dt_ = nc.dram_tensor
    x_d = dt_("x", [BPC, S, E], F32, kind="ExternalInput").ap()
    out_d = dt_("out", [BPC, S, E], F32, kind="ExternalOutput").ap()
    wq_d = dt_("wq", [128, KC * D], F32, kind="ExternalInput").ap()
    wk_d = dt_("wk", [128, KC * D], F32, kind="ExternalInput").ap()
    wv1_d = dt_("wv1", [128, KC * D], F32, kind="ExternalInput").ap()
    wv2_d = dt_("wv2", [D, E], F32, kind="ExternalInput").ap()
    wexp_d = dt_("wexp", [E, H], F32, kind="ExternalInput").ap()
    wcon_d = dt_("wcon", [H, E], F32, kind="ExternalInput").ap()
    bexp_d = dt_("bexp", [1, H], F32, kind="ExternalInput").ap()
    ubias_d = dt_("ubias", [1, H], F32, kind="ExternalInput").ap()
    bcon_d = dt_("bcon", [1, E], F32, kind="ExternalInput").ap()
    dbias_d = dt_("dbias", [1, E], F32, kind="ExternalInput").ap()
    mloU_d = dt_("mloU", [32, 128, 7 * 128], F32, kind="ExternalInput").ap()
    mloD_d = dt_("mloD", [32, 128, 7 * 128], F32, kind="ExternalInput").ap()
    mhiU_d = dt_("mhiU", [5, 128, 32 * 128], F32, kind="ExternalInput").ap()
    mhiD_d = dt_("mhiD", [5, 128, 32 * 128], F32, kind="ExternalInput").ap()
    mask_d = dt_("maskc", [128, 896], BF16, kind="ExternalInput").ap()
    identf_d = dt_("identf", [128, 128], F32, kind="ExternalInput").ap()
    identb_d = dt_("identb", [128, 128], BF16, kind="ExternalInput").ap()
    i32q_d = dt_("i32q", [128, 32 * 32], F32, kind="ExternalInput").ap()
    ones_d = dt_("onesr", [1, 128], F32, kind="ExternalInput").ap()
    swz_d = dt_("swz", [H, FW], F32R).ap()
    swzb_d = dt_("swzb", [H, 2], F32R).ap()
    wcpg_d = dt_("wcpg", [H, E], F32R).ap()
    v2h_d = dt_("v2h", [1, H], F32).ap()
    dbg_xn = dt_("dbg_xn", [128, KC * S], BF16, kind="ExternalOutput").ap()
    dbg_wup = dt_("dbg_wup", [128, KC * H], BF16, kind="ExternalOutput").ap()
    dbg_ldn = dt_("dbg_ldn", [128, C32 * E], BF16, kind="ExternalOutput").ap()
    dbg_bp = dt_("dbg_bp", [128, 32], F32, kind="ExternalOutput").ap()
    dbg_bs = dt_("dbg_bs", [128, E], F32, kind="ExternalOutput").ap()
    dbg_bld = dt_("dbg_bld", [128, C32 * 128], F32, kind="ExternalOutput").ap()
    dbg_bhd = dt_("dbg_bhd", [128, C32 * 128], F32, kind="ExternalOutput").ap()
    dbg_upg = dt_("dbg_upg", [128, C32 * FW], F32, kind="ExternalOutput").ap()
    dbg_ucm = dt_("dbg_ucm", [128, 16 * FW], F32, kind="ExternalOutput").ap()
    dbg_q = dt_("dbg_q", [64, S], BF16, kind="ExternalOutput").ap()
    dbg_k = dt_("dbg_k", [64, S], BF16, kind="ExternalOutput").ap()
    dbg_vt = dt_("dbg_vt", [128, NTT * D], BF16, kind="ExternalOutput").ap()
    dbg_o = dt_("dbg_o", [64, S], BF16, kind="ExternalOutput").ap()
    dbg_p = dt_("dbg_p", [128, S], BF16, kind="ExternalOutput").ap()
    dbg_a = dt_("dbg_a", [128, C32 * TW], BF16, kind="ExternalOutput").ap()
    blod_sp = dt_("blod_sp", [128, C32 * 128], F32R).ap()
    bhid_sp = dt_("bhid_sp", [128, C32 * 128], F32R).ap()

    sbm = lambda cx, name, shape, dtype: cx.enter_context(nc.sbuf_tensor(name, shape, dtype))
    sbr = lambda cx, name, shape, dtype: cx.enter_context(nc.sbuf_tensor(name, shape, dtype, side="right"))
    psm = lambda cx, name, shape, dtype: cx.enter_context(nc.psum_tensor(name, shape, dtype))

    wup = sbm(ctx, "wup", [128, KC * H], BF16)
    idb = sbm(ctx, "idb", [128, 128], BF16)
    biasp = sbm(ctx, "biasp", [128, 32], F32)
    bsbc = sbm(ctx, "bsbc", [128, E], F32)
    wqsb = sbm(ctx, "wqsb", [128, KC * D], BF16)
    wksb = sbm(ctx, "wksb", [128, KC * D], BF16)
    wv1sb = sbm(ctx, "wv1sb", [128, KC * D], BF16)
    wv2sb = sbm(ctx, "wv2sb", [64, E], BF16)
    diagq = sbm(ctx, "diagq", [128, 128], BF16)
    lnv = sbm(ctx, "lnv", [128, 16], F32)
    attv = sbm(ctx, "attv", [128, 16], F32)

    pb = [psm(ctx, f"pb{i}", [128, 512], F32) for i in range(6)]
    pT = [psm(ctx, f"pT{i}", [128, 1024], BF16) for i in range(2)]
    pTf = [b[:].bitcast(F32) for b in pT]

    P = Plan(nc, ctx)
    for s_ in ["bld", "bmm", "bev", "bswz", "bg",
               "sx", "ssq", "sst", "sxn", "stt",
               "spj", "ssc", "smx", "sex", "sdg", "spt", "spte", "sot",
               "sup", "sgl", "sdn", "soev", "sxr", "sost"]:
        P.sem(s_)

    P.opi('V', lambda e: e.memset(lnv[:, 15:16], 1e-5), "bev", 1)
    P.opi('S', lambda e: e.dma_start(out=idf[:], in_=identf_d[:]), "bld", 16)
    P.opi('S', lambda e: e.dma_start(out=idb[:], in_=identb_d[:]), "bld", 16)
    P.opi('S', lambda e: e.dma_start(out=onesr[:], in_=ones_d[:]), "bld", 16)
    P.opi('G', lambda e: e.dma_start(out=wqsb[:], in_=wq_d[:], max_dma_last_dim=4096), "bg", 16)
    P.opi('G', lambda e: e.dma_start(out=wksb[:], in_=wk_d[:], max_dma_last_dim=4096), "bg", 16)
    P.opi('G', lambda e: e.dma_start(out=wv1sb[:], in_=wv1_d[:], max_dma_last_dim=4096), "bg", 16)
    P.opi('G', lambda e: e.dma_start(out=wv2sb[:], in_=wv2_d[:], max_dma_last_dim=4096), "bg", 16)
    for pg in range(4):
        for jh in range(2):
            src = bass.AP(wcon_d.tensor, (pg + 4 * 16 * jh) * E, [[4 * E, 16], [128 * E, 32], [1, E]])
            dst = bass.AP(wcpg_d.tensor, (32 * pg) * E + (16 * jh) * 128 * E, [[128 * E, 16], [E, 32], [1, E]])
            P.opi('G', lambda e, s=src, d=dst: e.dma_start(out=d, in_=s, max_dma_last_dim=4096), "bg", 16)

    idf = sbm(ctx, "idf", [128, 128], F32)
    onesr = sbm(ctx, "onesr_sb", [1, 128], F32)
    P.opi('S', lambda e: e.dma_start(out=idf[:], in_=identf_d[:]), "bld", 16)

    def emit_chains(cx, bt, out_lo, out_hi):
        mhist = [sbm(cx, f"mhist{bt}{i}", [128, C32 * 128], F32) for i in range(1)]
        i32qsb = sbm(cx, f"i32qsb{bt}", [128, 32 * 32], F32)
        mlo8 = sbm(cx, f"mlo8{bt}", [128, 4 * 896], F32)
        chb8 = sbm(cx, f"chb8{bt}", [128, 4 * 128], F32)
        cqall = sbm(cx, f"cqall{bt}", [128, 32 * 32], F32)
        cqTall = sbm(cx, f"cqTall{bt}", [128, 32 * 32], F32)
        msrc = mhiU_d if bt == 0 else mhiD_d
        mlo_src = mloU_d if bt == 0 else mloD_d
        P.opi('S', lambda e, i32qsb=i32qsb: e.dma_start(out=i32qsb[:], in_=i32q_d[:]), "bld", 16)
        P.opi('V', lambda e, out_hi=out_hi: e.memset(out_hi[:].bitcast(F32), 0.0), "bev", 1)
        P.waitc('V', "bld")
        P.waitc('V', "bmm")
        P.opi('V', lambda e, cqall=cqall, i32qsb=i32qsb: e.tensor_copy(cqall[:], i32qsb[:]), "bev", 1)
        P.waitc('T', "bev")
        for st in range(5):
            mh = mhist[0]
            P.wait('S', "bmm", P.cnt["bmm"])
            P.opi('S', lambda e, mh=mh, msrc=msrc, st=st: e.dma_start(out=mh[:], in_=msrc[st]), "bld", 16)
            P.waitc('T', "bld")
            banks = (pb[0], pb[1]) if st % 2 == 0 else (pb[2], pb[3])
            for j in range(C32):
                bk = banks[j // 16]
                P.opi('T', lambda e, bk=bk, mh=mh, j=j, cqall=cqall: e.matmul(
                    bk[:, 32 * (j % 16):32 * (j % 16 + 1)], mh[:, 128 * j:128 * (j + 1)],
                    cqall[:, 32 * j:32 * (j + 1)], start=True, stop=True), "bmm", 1)
            P.waitc('V', "bmm")
            P.opi('V', lambda e, b=banks[0], cqall=cqall: e.tensor_copy(cqall[:, 0:512], b[:, 0:512]), "bev", 1)
            P.opi('V', lambda e, b=banks[1], cqall=cqall: e.tensor_copy(cqall[:, 512:1024], b[:, 0:512]), "bev", 1)
            P.waitc('T', "bev")
        if bt == 0:
            P.opi('V', lambda e, cqTall=cqTall, cqall=cqall: e.transpose(cqTall[:], cqall[:]), "bev", 1)
            P.waitc('V', "bev")
        srcq = cqTall if bt == 0 else cqall
        for j in range(C32):
            for pg in range(4):
                P.opi('V', lambda e, j=j, pg=pg, d=out_hi, s=srcq: e.tensor_copy(
                    d[32 * pg:32 * (pg + 1), 128 * j + 32 * pg:128 * j + 32 * (pg + 1)],
                    s[32 * pg:32 * (pg + 1), 32 * j:32 * (j + 1)]), "bev", 1)
        for g in range(8):
            P.wait('S', "bmm", P.cnt["bmm"])
            for cc in range(4):
                P.opi('S', lambda e, cc=cc, s=mlo_src, g=g, mlo8=mlo8: e.dma_start(
                    out=mlo8[:, 896 * cc:896 * (cc + 1)], in_=s[4 * g + cc]), "bld", 16)
            P.waitc('T', "bld")
            for st in range(7):
                bk = pb[st % 2]
                for cc in range(4):
                    rhs = idf[:] if st == 0 else chb8[:, 128 * cc:128 * (cc + 1)]
                    P.opi('T', lambda e, bk=bk, cc=cc, st=st, r=rhs, mlo8=mlo8: e.matmul(
                        bk[:, 128 * cc:128 * (cc + 1)],
                        mlo8[:, 896 * cc + 128 * st:896 * cc + 128 * (st + 1)],
                        r, start=True, stop=True), "bmm", 1)
                P.waitc('V', "bmm")
                if st < 6 or bt == 0:
                    dst = chb8[:, 0:512]
                else:
                    dst = out_lo[:, 512 * g:512 * (g + 1)]
                P.opi('V', lambda e, b=bk, d=dst: e.tensor_copy(d, b[:, 0:512]), "bev", 1)
                P.waitc('T', "bev")
            if bt == 0:
                for cc in range(4):
                    P.opi('T', lambda e, cc=cc, chb8=chb8: e.transpose(
                        pb[4][:, 128 * cc:128 * (cc + 1)], chb8[:, 128 * cc:128 * (cc + 1)], idf[:]), "bmm", 1)
                P.waitc('V', "bmm")
                P.opi('V', lambda e, g=g: e.tensor_copy(out_lo[:, 512 * g:512 * (g + 1)], pb[4][:, 0:512]), "bev", 1)
                P.waitc('T', "bev")

    cxA = ExitStack()
    bloUT = sbr(cxA, "bloUT", [128, C32 * 128], F32R)
    bhiUblk = sbr(cxA, "bhiUblk", [128, C32 * 128], F32R)
    P.on = _rank >= 1
    cx1c = ExitStack()
    emit_chains(cx1c, 0, bloUT, bhiUblk)
    P.waitc('S', "bev")
    P.waitc('T', "bev")
    P.waitc('V', "bmm")
    cx1c.close()

    P.on = _rank >= 3
    cx2 = ExitStack()
    wx = sbr(cx2, "wx", [128, H], F32)
    p1ts = sbr(cx2, "p1ts", [128, C32 * FW], F32R)
    p2t = sbr(cx2, "p2t", [128, C32 * FW], F32R)
    rowsb = sbr(cx2, "rowsb", [1, H], F32)
    bvec = sbr(cx2, "bvec", [128, 2 * C32], F32R)
    p1tbia = sbr(cx2, "p1tbia", [128, C32 * 2], F32R)
    p1sbia = sbr(cx2, "p1sbia", [128, C32 * 2], F32R)

    P.opi('S', lambda e: e.dma_start(out=rowsb[:], in_=bexp_d[:]), "bld", 16)
    P.waitc('T', "bld")
    for c in range(C32):
        P.opi('T', lambda e, c=c: e.matmul(
            pb[3][:, c:c + 1], rowsb[0:1, 128 * c:128 * (c + 1)], onesr[0:1, 0:1],
            start=True, stop=True), "bmm", 1)
    P.waitc('V', "bmm")
    P.opi('V', lambda e: e.memset(bvec[:].bitcast(F32), 0.0), "bev", 1)
    P.waitc('V', "bev")
    P.opi('V', lambda e: e.tensor_copy(
        bass.AP(bvec, 0, [[2 * C32, 128], [2, 32]]), pb[3][:, 0:32]), "bev", 1)
    P.waitc('T', "bev")
    for c in range(C32):
        P.opi('T', lambda e, c=c: e.matmul(
            pb[4][:, 2 * c:2 * (c + 1)], bloUT[:, 128 * c:128 * (c + 1)], bvec[:, 2 * c:2 * (c + 1)],
            start=True, stop=True), "bmm", 1)
    P.waitc('V', "bmm")
    P.opi('V', lambda e: e.tensor_copy(p1tbia[:], pb[4][:, 0:64]), "bev", 1)
    P.waitc('S', "bev")
    P.opi('S', lambda e: e.dma_start(
        out=bass.AP(swzb_d.tensor, 0, [[2, 128], [128 * 2, 32], [1, 2]]),
        in_=bass.AP(p1tbia, 0, [[C32 * 2, 128], [2, 32], [1, 2]])), "bswz", 16)
    P.waitc('S', "bswz")
    for pg in range(4):
        P.opi('S', lambda e, pg=pg: e.dma_start(
            out=bass.AP(p1sbia, (32 * pg) * (C32 * 2), [[C32 * 2, 32], [2, 32], [1, 2]]),
            in_=bass.AP(swzb_d.tensor, pg * 2, [[128 * 2, 32], [4 * 2, 32], [1, 2]])), "bswz", 16)
    P.waitc('T', "bswz")
    for j in range(C32):
        P.opi('T', lambda e, j=j: e.matmul(
            pb[5][:, 2 * j:2 * (j + 1)], bhiUblk[:, 128 * j:128 * (j + 1)],
            p1sbia[:, 2 * j:2 * (j + 1)], start=True, stop=True), "bmm", 1)
    P.waitc('V', "bmm")
    P.opi('V', lambda e: e.tensor_copy(p1sbia[:], pb[5][:, 0:64]), "bev", 1)
    P.waitc('S', "bev")
    for pg in range(4):
        P.opi('S', lambda e, pg=pg: e.dma_start(
            out=bass.AP(v2h_d.tensor, pg, [[128, 32], [4, 32], [1, 1]]),
            in_=bass.AP(p1sbia, (32 * pg) * 64, [[64, 32], [2, 32], [1, 1]]).bitcast(F32),
            allow_slow_non_contiguous=True), "bswz", 16)

    for w in range(4):
        for ki in range(2):
            kappa = 2 * w + ki
            P.wait('S', "bmm", P.cnt["bmm"])
            P.opi('S', lambda e, kappa=kappa: e.dma_start(out=wx[:], in_=wexp_d[128 * kappa:128 * (kappa + 1), :]), "bld", 16)
            P.waitc('T', "bld")
            for c4 in range(8):
                tbk = pT[c4 % 2]
                if w + ki + c4 > 0:
                    P.wait('T', "bev", P.cnt["bev"] - 1)
                for cc in range(4):
                    P.opi('T', lambda e, tbk=tbk, c4=c4, cc=cc: e.transpose(
                        tbk[:].bitcast(F32)[:, 128 * cc:128 * (cc + 1)],
                        wx[:, 128 * (4 * c4 + cc):128 * (4 * c4 + cc + 1)], idf[:]), "bmm", 1)
                P.waitc('V', "bmm")
                P.opi('V', lambda e, tbk=tbk, c4=c4, ki=ki: e.tensor_copy(
                    bass.AP(p1ts, FW * (4 * c4) + 128 * ki, [[C32 * FW, 128], [FW, 4], [1, 128]]),
                    tbk[:].bitcast(F32)[:, 0:512]), "bev", 1)
        P.waitc('T', "bev")
        for c in range(C32):
            P.wait('T', "bev", P.cnt["bev"] - 1)
            P.opi('T', lambda e, c=c: e.matmul(
                pb[4 + c % 2][:, 0:FW], bloUT[:, 128 * c:128 * (c + 1)], p1ts[:, FW * c:FW * (c + 1)],
                start=True, stop=True), "bmm", 1)
            P.waitc('V', "bmm")
            P.opi('V', lambda e, c=c: e.tensor_copy(p1ts[:, FW * c:FW * (c + 1)], pb[4 + c % 2][:, 0:FW]), "bev", 1)
        P.waitc('S', "bev")
        P.waitc('S', "bswz")
        P.opi('S', lambda e: e.dma_start(
            out=bass.AP(swz_d.tensor, 0, [[FW, 128], [128 * FW, 32], [1, FW]]),
            in_=p1ts[:]), "bswz", 16)
        P.waitc('S', "bswz")
        for pg in range(4):
            P.opi('S', lambda e, pg=pg: e.dma_start(
                out=bass.AP(p1ts, (32 * pg) * (C32 * FW), [[C32 * FW, 32], [FW, 32], [1, FW]]),
                in_=bass.AP(swz_d.tensor, pg * FW, [[128 * FW, 32], [4 * FW, 32], [1, FW]])), "bswz", 16)
        P.waitc('T', "bswz")
        for j in range(C32):
            P.wait('T', "bev", P.cnt["bev"] - 1)
            P.opi('T', lambda e, j=j: e.matmul(
                pb[4 + j % 2][:, 0:FW], bhiUblk[:, 128 * j:128 * (j + 1)], p1ts[:, FW * j:FW * (j + 1)],
                start=True, stop=True), "bmm", 1)
            P.waitc('V', "bmm")
            P.opi('V', lambda e, j=j: e.tensor_copy(p2t[:, FW * j:FW * (j + 1)], pb[4 + j % 2][:, 0:FW]), "bev", 1)
        P.waitc('T', "bev")
        for j2 in range(16):
            tbk = pT[j2 % 2]
            if j2 >= 2:
                P.wait('T', "bev", P.cnt["bev"] - 1)
            for k4 in range(4):
                jj = 2 * j2 + k4 // 2
                sub = k4 % 2
                P.opi('T', lambda e, tbk=tbk, jj=jj, sub=sub, k4=k4: e.transpose(
                    tbk[:].bitcast(F32)[:, 128 * k4:128 * (k4 + 1)],
                    p2t[:].bitcast(F32)[:, FW * jj + 128 * sub:FW * jj + 128 * (sub + 1)], idf[:]), "bmm", 1)
            P.waitc('V', "bmm")
            P.opi('V', lambda e, tbk=tbk, w=w, j2=j2: e.tensor_copy(
                bass.AP(wup, (2 * w) * H + 128 * (2 * j2), [[KC * H, 128], [128, 2], [H, 2], [1, 128]]),
                tbk[:].bitcast(F32)[:, 0:512].bitcast(BF16) if False else tbk[:].bitcast(F32)[:, 0:512]), "bev", 1)

    P.waitc('S', "bswz")
    P.wait('S', "bmm", P.cnt["bmm"])
    P.opi('S', lambda e: e.dma_start(out=rowsb[:], in_=v2h_d[:]), "bld", 16)
    P.waitc('T', "bld")
    for c in range(C32):
        P.opi('T', lambda e, c=c: e.matmul(
            pb[3][:, c:c + 1], rowsb[0:1, 128 * c:128 * (c + 1)], onesr[0:1, 0:1],
            start=True, stop=True), "bmm", 1)
    P.wait('S', "bmm", P.cnt["bmm"])
    P.opi('S', lambda e: e.dma_start(out=rowsb[:], in_=ubias_d[:]), "bld", 16)
    P.waitc('T', "bld")
    for c in range(C32):
        P.opi('T', lambda e, c=c: e.matmul(
            pb[4][:, c:c + 1], rowsb[0:1, 128 * c:128 * (c + 1)], onesr[0:1, 0:1],
            start=True, stop=True), "bmm", 1)
    P.waitc('V', "bmm")
    P.opi('V', lambda e: e.tensor_copy(biasp[:], pb[3][:, 0:32]), "bev", 1)
    P.waitc('V', "bev")
    P.opi('V', lambda e: e.tensor_add(biasp[:], biasp[:], pb[4][:, 0:32]), "bev", 1)
    P.wait('S', "bev", P.cnt["bev"])
    P.opi('S', lambda e: e.dma_start(out=rowsb[0:1, 0:E], in_=bcon_d[:]), "bld", 16)
    P.opi('S', lambda e: e.dma_start(out=rowsb[0:1, E:2 * E], in_=dbias_d[:]), "bld", 16)
    P.waitc('V', "bld")
    P.opi('V', lambda e: e.tensor_add(rowsb[0:1, 0:E], rowsb[0:1, 0:E], rowsb[0:1, E:2 * E]), "bev", 1)
    P.waitc('T', "bev")
    for hf in range(2):
        P.opi('T', lambda e, hf=hf: e.matmul(
            pb[5][:, 0:512], onesr[0:1, 0:128], rowsb[0:1, 512 * hf:512 * (hf + 1)],
            start=True, stop=True), "bmm", 1)
        P.waitc('V', "bmm")
        P.opi('V', lambda e, hf=hf: e.tensor_copy(bsbc[:, 512 * hf:512 * (hf + 1)], pb[5][:, 0:512]), "bev", 1)
        P.waitc('T', "bev")
    P.waitc('S', "bev")
    P.waitc('V', "bswz")
    cx2.close()
    cxA.close()

    P.on = _rank >= 4
    ldn = sbr(ctx, "ldn", [128, C32 * E], BF16)
    cx3o = ExitStack()
    bloD2 = sbr(cx3o, "bloD2", [128, C32 * 128], F32R)
    bhiD2 = sbr(cx3o, "bhiD2", [128, C32 * 128], F32R)
    cx3c = ExitStack()
    emit_chains(cx3c, 1, bloD2, bhiD2)
    P.waitc('S', "bev")
    P.waitc('T', "bev")
    P.waitc('V', "bmm")
    cx3c.close()
    cx3 = ExitStack()
    upg = sbr(cx3, "upg", [128, C32 * FW], F32R)
    ucmh = sbr(cx3, "ucmh", [128, 16 * FW], F32R)
    wct = [sbr(cx3, f"wct{i}", [128, FW], F32R) for i in range(2)]
    P.waitc('S', "bg")
    import os as _os
    _ndw = int(_os.environ.get("KDW", "4"))
    _dwsec = _os.environ.get("KDWS", "ul")
    for w in range(_ndw):
        for j in (range(C32) if 'u' in _dwsec else []):
            b = wct[j % 2]
            P.wait('S', "bmm", P.cnt["bmm"] - 1)
            P.opi('S', lambda e, b=b, j=j, w=w: e.dma_start(
                out=b[:], in_=bass.AP(wcpg_d.tensor, (128 * j) * E + FW * w, [[E, 128], [1, FW]])), "bld", 16)
            P.waitc('T', "bld")
            P.wait('T', "bev", P.cnt["bev"] - 1)
            P.opi('T', lambda e, j=j, b=b: e.matmul(
                pb[4 + j % 2][:, 0:FW], bhiD2[:, 128 * j:128 * (j + 1)], b[:],
                start=True, stop=True), "bmm", 1)
            P.waitc('V', "bmm")
            P.opi('V', lambda e, j=j: e.tensor_copy(upg[:, FW * j:FW * (j + 1)], pb[4 + j % 2][:, 0:FW]), "bev", 1)
        P.waitc('S', "bev")
        P.waitc('S', "bswz")
        for pg in range(4):
            P.opi('S', lambda e, pg=pg: e.dma_start(
                out=bass.AP(swz_d.tensor, pg * FW, [[128 * FW, 32], [4 * FW, 32], [1, FW]]),
                in_=bass.AP(upg, (32 * pg) * (C32 * FW), [[C32 * FW, 32], [FW, 32], [1, FW]])), "bswz", 16)
        P.waitc('S', "bswz")
        for ch in (range(1 if 'h' in _dwsec else 2) if 'l' in _dwsec else []):
            P.wait('S', "bmm", P.cnt["bmm"])
            for qh in range(2):
                P.opi('S', lambda e, ch=ch, qh=qh: e.dma_start(
                    out=bass.AP(ucmh, (8 * qh) * FW, [[16 * FW, 128], [FW, 8], [1, FW]]),
                    in_=bass.AP(swz_d.tensor, (16 * ch + 8 * qh) * 128 * FW, [[FW, 128], [128 * FW, 8], [1, FW]])), "bld", 16)
            P.waitc('T', "bld")
            for cc in (range(16) if 'm' in _dwsec or 'l' == _dwsec.replace('x','') and False else range(16)):
                c = 16 * ch + cc
                if 'x' in _dwsec:
                    continue
                P.wait('T', "bev", P.cnt["bev"] - 1)
                P.opi('T', lambda e, c=c, cc=cc: e.matmul(
                    pb[4 + cc % 2][:, 0:FW], bloD2[:, 128 * c:128 * (c + 1)], ucmh[:, FW * cc:FW * (cc + 1)],
                    start=True, stop=True), "bmm", 1)
                if 'y' in _dwsec:
                    continue
                P.waitc('V', "bmm")
                _dst = (upg if 'z' in _dwsec else ldn)
                if 'z' in _dwsec:
                    P.opi('V', lambda e, cc=cc: e.tensor_copy(
                        upg[:].bitcast(F32)[:, FW * cc:FW * (cc + 1)], pb[4 + cc % 2][:, 0:FW]), "bev", 1)
                elif 'w' in _dwsec:
                    P.opi('V', lambda e, cc=cc: e.tensor_copy(
                        ldn[:, FW * cc:FW * (cc + 1)], pb[4 + cc % 2][:, 0:FW]), "bev", 1)
                elif 'a' in _dwsec:
                    P.opi('V', lambda e, c=c: e.tensor_copy(
                        ldn[:, 2 * FW * c:2 * FW * c + FW], pb[4 + c % 2][:, 0:FW]), "bev", 1)
                elif 'b' in _dwsec:
                    P.opi('V', lambda e, c=c: e.tensor_copy(
                        ldn[:, E * c:E * c + FW], pb[4 + c % 2][:, 0:FW]), "bev", 1)
                else:
                    P.opi('V', lambda e, c=c, w=w, cc=cc: e.tensor_copy(
                        ldn[:, E * c + FW * w:E * c + FW * (w + 1)], pb[4 + cc % 2][:, 0:FW]), "bev", 1)
    P.waitc('S', "bev")
    P.waitc('V', "bswz")
    if limit in ('dbg', 'downfold'):
        P.opi('S', lambda e: e.dma_start(out=dbg_bld[:], in_=bloD2[:].bitcast(F32)), "bswz", 16)
        P.opi('S', lambda e: e.dma_start(out=dbg_bhd[:], in_=bhiD2[:].bitcast(F32)), "bswz", 16)
        P.opi('S', lambda e: e.dma_start(out=dbg_upg[:], in_=upg[:].bitcast(F32)), "bswz", 16)
        P.opi('S', lambda e: e.dma_start(out=dbg_ucm[:], in_=ucmh[:].bitcast(F32)), "bswz", 16)
        P.waitc('S', "bswz")
    cx3.close()
    cx3o.close()

    maskb = sbr(ctx, "maskb", [128, 896], BF16)
    xn = sbr(ctx, "xn", [128, KC * S], BF16)
    abuf = sbr(ctx, "abuf", [128, C32 * TW], BF16)
    qbuf = sbr(ctx, "qbuf", [64, S], BF16)
    kbuf = sbr(ctx, "kbuf", [64, S], BF16)
    o64t = sbr(ctx, "o64t", [64, S], BF16)
    vt = sbr(ctx, "vt", [128, NTT * D], BF16)
    pbuf = sbr(ctx, "pbuf", [128, S], BF16)
    ptb = sbr(ctx, "ptb", [128, 2 * 512], BF16)
    xtile = [sbr(ctx, f"xtile{i}", [128, E], F32) for i in range(2)]
    xnt = [sbr(ctx, f"xnt{i}", [128, E], BF16) for i in range(2)]
    ost = [sbr(ctx, f"ost{i}", [128, E], F32) for i in range(2)]

    P.on = True
    P.opi('S', lambda e: e.dma_start(out=maskb[:], in_=mask_d[:]), "bld", 16)
    P.waitc('T', "bev")
    P.waitc('C', "bev")
    P.waitc('V', "bmm")
    P.waitc('C', "bld")
    P.waitc('V', "bld")
    P.waitc('C', "bg")
    P.waitc('T', "bg")

    nseq = 0 if _rank <= 5 else (1 if limit in ('ln', 'proj', 'attn', 'ffn1') else BPC)
    for b in range(nseq):
        P.wait('C', "sost", P.cnt["sost"])
        for i in range(NTT):
            sl = i % 2
            xt = xtile[sl]
            gi = b * NTT + i
            if gi >= 2:
                P.wait('S', "sxn", 2 * (gi - 2) + 1)
            if gi >= 2:
                P.wait('S', "ssq", P.cnt["ssq"] - 3)
            P.opi('S', lambda e, xt=xt, b=b, i=i: e.dma_start(
                out=xt[:], in_=x_d[b, 128 * i:128 * (i + 1), :]), "sx", 16)
            P.waitc('V', "sx")
            P.waitc('C', "sx")
            c0, c1, c2, c3, c5 = sl * 8, sl * 8 + 1, sl * 8 + 2, sl * 8 + 3, sl * 8 + 5
            P.opi('V', lambda e, xt=xt, c0=c0: e.reduce_sum(lnv[:, c0:c0 + 1], xt[:], axis=AX.X), "sst", 1)
            P.waitc('V', "sst")
            if gi >= 2:
                P.wait('C', "stt", P.cnt["stt"] - 8)
            P.opi('C', lambda e, xt=xt, sq=xnt[sl], c1=c1: e.activation(
                sq[:], xt[:], AF.Square, accum_out=lnv[:, c1:c1 + 1]), "ssq", 1)
            P.waitc('V', "ssq")
            P.opi('V', lambda e, c0=c0, c2=c2: e.tensor_scalar_mul(lnv[:, c2:c2 + 1], lnv[:, c0:c0 + 1], -1.0 / E), "sst", 1)
            P.opi('V', lambda e, c1=c1, c3=c3: e.tensor_scalar_mul(lnv[:, c3:c3 + 1], lnv[:, c1:c1 + 1], 1.0 / E), "sst", 1)
            P.waitc('V', "sst")
            c4 = sl * 8 + 4
            P.opi('V', lambda e, c2=c2, c4=c4: e.tensor_mul(lnv[:, c4:c4 + 1], lnv[:, c2:c2 + 1], lnv[:, c2:c2 + 1]), "sst", 1)
            P.waitc('V', "sst")
            P.opi('V', lambda e, c3=c3, c4=c4: e.tensor_sub(lnv[:, c3:c3 + 1], lnv[:, c3:c3 + 1], lnv[:, c4:c4 + 1]), "sst", 1)
            P.waitc('V', "sst")
            P.waitc('C', "sst")
            P.opi('C', lambda e, c3=c3, c5=c5: e.activation(lnv[:, c5:c5 + 1], lnv[:, c3:c3 + 1], AF.Sqrt, bias=lnv[:, 15:16]), "ssq", 1)
            P.waitc('V', "ssq")
            P.opi('V', lambda e, c5=c5: e.reciprocal(lnv[:, c5:c5 + 1], lnv[:, c5:c5 + 1]), "sst", 1)
            P.waitc('V', "sst")
            P.opi('V', lambda e, xt=xt, xo=xnt[sl], c2=c2, c5=c5: e.tensor_scalar(
                xo[:], xt[:], lnv[:, c2:c2 + 1], lnv[:, c5:c5 + 1], ALU.add, ALU.mult), "sxn", 1)
            P.waitc('T', "sxn")
            tb = pT[sl]
            for kappa in range(KC):
                P.opi('T', lambda e, tb=tb, xo=xnt[sl], kappa=kappa: e.transpose(
                    tb[:, 128 * kappa:128 * (kappa + 1)], xo[:, 128 * kappa:128 * (kappa + 1)], idb[:]), "stt", 1)
            P.waitc('V', "stt")
            P.opi('V', lambda e, tb=tb, i=i: e.tensor_copy(
                bass.AP(xn, 128 * i, [[KC * S, 128], [S, 8], [1, 128]]), tb[:]), "sxn", 1)
        if limit == 'ln':
            continue
        P.waitc('T', "sxn")
        for tw in range(4):
            for which, wsb_, dst in ((0, wqsb, qbuf), (1, wksb, kbuf), (2, wv1sb, None)):
                bank = pb[which]
                for kappa in range(KC):
                    P.opi('T', lambda e, bank=bank, w_=wsb_, kappa=kappa, tw=tw: e.matmul(
                        bank[0:64, 0:512], w_[:, D * kappa:D * (kappa + 1)],
                        bass.AP(xn, S * kappa + 512 * tw, [[KC * S, 128], [1, 512]]),
                        start=(kappa == 0), stop=(kappa == KC - 1)), "ssc", 1)
                P.waitc('C', "ssc")
                if which < 2:
                    P.opi('C', lambda e, bank=bank, dst=dst, tw=tw: e.activation(
                        dst[0:64, 512 * tw:512 * (tw + 1)], bank[0:64, 0:512], AF.Copy, scale=SCALE_QK), "spj", 1)
                else:
                    P.opi('C', lambda e, bank=bank: e.activation(
                        pbuf[0:64, 0:512], bank[0:64, 0:512], AF.Copy), "spj", 1)
            P.waitc('T', "spj")
            for s4 in range(4):
                P.opi('T', lambda e, s4=s4: e.transpose(
                    pT[0][:, 64 * s4:64 * (s4 + 1)], pbuf[0:64, 128 * s4:128 * (s4 + 1)], idb[0:64, 0:64]), "stt", 1)
            P.waitc('V', "stt")
            P.wait('V', "sot", P.cnt["sot"])
            P.opi('V', lambda e, tw=tw: e.tensor_copy(vt[:, 256 * tw:256 * (tw + 1)], pT[0][:, 0:256]), "spj", 1)
            P.waitc('T', "spj")
        if limit == 'proj':
            continue
        blkctr = getattr(P, "_blkctr", 0)
        if not hasattr(P, "_sexbase"):
            P._sexbase = P.cnt["sex"]
        for qi in range(NTT):
            bd = qi // 4
            o = 128 * (qi % 4)
            nbk = bd + 1
            P.waitc('V', "sex")
            P.waitc('V', "spt")
            myblk = []
            for blk in range(nbk):
                bank = pb[blkctr % 4]
                myblk.append(bank)
                P.wait('T', "sex", P._sexbase + blkctr - 3)
                blkctr += 1
                P.opi('T', lambda e, bank=bank, qi=qi, blk=blk: e.matmul(
                    bank[:, 0:512], qbuf[0:64, 128 * qi:128 * (qi + 1)], kbuf[0:64, 512 * blk:512 * (blk + 1)],
                    start=True, stop=True), "ssc", 1)
                P.waitc('V', "ssc")
                P.opi('V', lambda e, bank=bank, blk=blk: e.reduce_max(
                    attv[:, blk:blk + 1], bank[:, 0:512], axis=AX.X), "smx", 1)
            P.waitc('V', "smx")
            P.opi('V', lambda e, nbk=nbk: e.reduce_max(
                attv[:, 4:5], attv[:, 0:nbk], axis=AX.X, negate=True), "smx", 1)
            P.waitc('V', "smx")
            P.waitc('C', "smx")
            P.waitc('C', "spt")
            for blk in range(nbk):
                bank = myblk[blk]
                P.opi('C', lambda e, bank=bank, blk=blk: e.activation(
                    pbuf[:, 512 * blk:512 * (blk + 1)], bank[:, 0:512], AF.Exp,
                    bias=attv[:, 4:5], accum_out=attv[:, 8 + blk:9 + blk]), "sex", 1)
            P.waitc('V', "sex")
            P.opi('V', lambda e, bd=bd, o=o: e.tensor_mul(
                pbuf[:, 512 * bd:512 * (bd + 1)], pbuf[:, 512 * bd:512 * (bd + 1)],
                maskb[:, 384 - o:896 - o]), "sdg", 1)
            P.waitc('V', "sdg")
            P.opi('V', lambda e, bd=bd: e.reduce_sum(
                attv[:, 8 + bd:9 + bd], pbuf[:, 512 * bd:512 * (bd + 1)], axis=AX.X), "sdg", 1)
            P.waitc('V', "sdg")
            P.opi('V', lambda e, nbk=nbk: e.reduce_sum(attv[:, 13:14], attv[:, 8:8 + nbk], axis=AX.X), "sdg", 1)
            P.waitc('V', "sdg")
            P.opi('V', lambda e: e.reciprocal(attv[:, 13:14], attv[:, 13:14]), "sdg", 1)
            P.waitc('V', "sdg")
            P.opi('V', lambda e: e.tensor_scalar_mul(diagq[:], idb[:], attv[:, 13:14]), "sdg", 1)
            P.waitc('T', "sdg")
            nm = qi + 1
            for g4 in range(0, nm, 4):
                mm4 = list(range(g4, min(g4 + 4, nm)))
                ptbank = pTf[(g4 // 4) % 2]
                P.wait('T', "spte", P.cnt["spte"] - 1)
                for ii, m in enumerate(mm4):
                    P.opi('T', lambda e, ptbank=ptbank, m=m, ii=ii: e.matmul(
                        ptbank[:, 128 * ii:128 * (ii + 1)], pbuf[:, 128 * m:128 * (m + 1)], diagq[:],
                        start=True, stop=True), "spt", 1)
                P.waitc('V', "spt")
                P.wait('V', "sot", P.cnt["sot"] - 4)
                sl4 = (g4 // 4) % 2
                P.opi('V', lambda e, ptbank=ptbank, sl4=sl4, n4=len(mm4): e.tensor_copy(
                    ptb[:, 512 * sl4:512 * sl4 + 128 * n4], ptbank[:, 0:128 * n4]), "spte", 1)
                P.waitc('T', "spte")
                for ii, m in enumerate(mm4):
                    P.opi('T', lambda e, m=m, qi=qi, sl4=sl4, ii=ii: e.matmul(
                        pb[5][0:64, 0:128], vt[:, 64 * m:64 * (m + 1)], ptb[:, 512 * sl4 + 128 * ii:512 * sl4 + 128 * (ii + 1)],
                        start=(m == 0), stop=(m == qi)), "sot", 1)
            P.waitc('C', "sot")
            P.opi('C', lambda e, qi=qi: e.activation(
                o64t[0:64, 128 * qi:128 * (qi + 1)], pb[5][0:64, 0:128], AF.Copy), "sot", 1)
            P.waitc('T', "sot")
        P._blkctr = blkctr
        if limit == 'attn':
            continue
        P.waitc('T', "sex")
        P.waitc('T', "sot")
        for tw in range(NTW):
            t_id = tw
            slx = t_id % 2
            upbanks = (pb[0], pb[1], pb[4])
            for c in range(C32):
                bank = upbanks[c % 3]
                P.wait('T', "sgl", P.cnt["sgl"] - 2)
                for kappa in range(KC):
                    P.opi('T', lambda e, bank=bank, c=c, kappa=kappa, tw=tw: e.matmul(
                        bank[:, 0:TW],
                        bass.AP(wup, H * kappa + c, [[KC * H, 128], [128, 32], [32, 4]]),
                        bass.AP(xn, S * kappa + TW * tw, [[KC * S, 128], [1, TW]]),
                        start=(kappa == 0), stop=(kappa == KC - 1)), "sup", 1)
                P.waitc('C', "sup")
                P.opi('C', lambda e, bank=bank, c=c: e.activation(
                    abuf[:, TW * c:TW * (c + 1)], bank[:, 0:TW], AF.Gelu, bias=biasp[:, c:c + 1]), "sgl", 1)
            P.waitc('T', "sgl")
            if tw == 0:
                P.waitc('S', "sxn")
            P.wait('S', "soev", P.cnt["soev"] - 2)
            P.opi('S', lambda e, b=b, t_id=t_id, slx=slx: e.dma_start(
                out=xtile[slx][:], in_=x_d[b, 128 * t_id:128 * (t_id + 1), :]), "sxr", 16)
            dnbanks = (pb[2], pb[3])
            for fo in range(2):
                bank = dnbanks[fo]
                P.wait('T', "soev", P.cnt["soev"] - 3)
                P.opi('T', lambda e, bank=bank, t_id=t_id, fo=fo: e.matmul(
                    bank[:, 0:512], o64t[0:64, 128 * t_id:128 * (t_id + 1)], wv2sb[0:64, 512 * fo:512 * (fo + 1)],
                    start=True, stop=False), "sdn", 1)
                for c in range(C32):
                    P.opi('T', lambda e, bank=bank, c=c, tw=tw, fo=fo: e.matmul(
                        bank[:, 0:512], abuf[:, TW * c:TW * (c + 1)],
                        ldn[:, E * c + 512 * fo:E * c + 512 * (fo + 1)],
                        start=False, stop=(c == C32 - 1)), "sdn", 1)
                P.waitc('V', "sdn")
                P.waitc('V', "sxr")
                P.wait('V', "sost", 16 * (P.cnt["sost"] // 16 - 1))
                P.opi('V', lambda e, bank=bank, slx=slx, fo=fo: e.tensor_add(
                    ost[slx][:, 512 * fo:512 * (fo + 1)], bank[:, 0:512], xtile[slx][:, 512 * fo:512 * (fo + 1)]), "soev", 1)
                P.waitc('V', "soev")
                P.opi('V', lambda e, slx=slx, fo=fo: e.tensor_add(
                    ost[slx][:, 512 * fo:512 * (fo + 1)], ost[slx][:, 512 * fo:512 * (fo + 1)],
                    bsbc[:, 512 * fo:512 * (fo + 1)]), "soev", 1)
            P.waitc('S', "soev")
            P.opi('S', lambda e, b=b, t_id=t_id, slx=slx: e.dma_start(
                out=out_d[b, 128 * t_id:128 * (t_id + 1), :], in_=ost[slx][:]), "sost", 16)
            if b == 0 and tw == 0 and limit == 'dbg':
                P.waitc('S', "sgl")
                P.opi('S', lambda e: e.dma_start(out=dbg_a[:], in_=abuf[:]), "sost", 16)

    P.wait('S', "sost", P.cnt["sost"])
    P.replay()
    ctx.close()
    return nc


_NC_CACHE = {}


def _chunk_rows(w):
    w = np.asarray(w, np.float32).reshape(KC, 128, D).transpose(1, 0, 2).reshape(128, KC * D)
    return np.ascontiguousarray(w)


def kernel(L0_attn_x, L0_attn_w_q, L0_attn_w_k, L0_attn_w_v1, L0_attn_w_v2,
           L0_ffn_W_exp, L0_ffn_b_exp, L0_ffn_up_weights, L0_ffn_up_bias,
           L0_ffn_down_weights, L0_ffn_W_con, L0_ffn_b_con, L0_ffn_down_bias):
    x = np.ascontiguousarray(np.asarray(L0_attn_x, dtype=np.float32))
    mask, identf, identb, i32q, ones_r = _host_consts()
    mloU = _stage_lo_T(np.asarray(L0_ffn_up_weights, np.float32))
    mloD = _stage_lo_T(np.asarray(L0_ffn_down_weights, np.float32))
    mhiU = _stage_hi_T(np.asarray(L0_ffn_up_weights, np.float32))
    mhiD = _stage_hi_T(np.asarray(L0_ffn_down_weights, np.float32))

    common = {
        "wq": _chunk_rows(L0_attn_w_q),
        "wk": _chunk_rows(L0_attn_w_k),
        "wv1": _chunk_rows(L0_attn_w_v1),
        "wv2": np.ascontiguousarray(L0_attn_w_v2, np.float32),
        "wexp": np.ascontiguousarray(L0_ffn_W_exp, np.float32),
        "wcon": np.ascontiguousarray(L0_ffn_W_con, np.float32),
        "bexp": np.ascontiguousarray(np.asarray(L0_ffn_b_exp, np.float32).reshape(1, H)),
        "ubias": np.ascontiguousarray(np.asarray(L0_ffn_up_bias, np.float32).reshape(1, H)),
        "bcon": np.ascontiguousarray(np.asarray(L0_ffn_b_con, np.float32).reshape(1, E)),
        "dbias": np.ascontiguousarray(np.asarray(L0_ffn_down_bias, np.float32).reshape(1, E)),
        "mloU": mloU, "mloD": mloD, "mhiU": mhiU, "mhiD": mhiD,
        "maskc": mask, "identf": identf, "identb": identb,
        "i32q": i32q, "onesr": ones_r,
    }
    in_maps = []
    for core in range(NCORES):
        m = dict(common)
        m["x"] = np.ascontiguousarray(x[BPC * core:BPC * (core + 1)])
        in_maps.append(m)

    if "nc" not in _NC_CACHE:
        _NC_CACHE["nc"] = build_nc()
    nc = _NC_CACHE["nc"]
    res = run_bass_kernel_spmd(nc, in_maps, core_ids=list(range(NCORES)))
    return np.concatenate([res.results[i]["out"] for i in range(NCORES)], axis=0)




# version 26 (final)
# speedup vs baseline: 1.2369x; 1.0203x over previous
import numpy as np
import ml_dtypes
import concourse.bass as bass
import concourse.mybir as mybir
from concourse.bass_utils import run_bass_kernel_spmd
from contextlib import ExitStack

F32 = mybir.dt.float32
F32R = mybir.dt.float32r
BF16 = mybir.dt.bfloat16
AF = mybir.ActivationFunctionType
AX = mybir.AxisListType
ALU = mybir.AluOpType
bfloat16 = ml_dtypes.bfloat16

NCORES = 8
BPC = 4
S = 2048
E = 1024
H = 4096
D = 64
KC = 8
C32 = 32
NTT = S // 128
TW = 128
NTW = S // TW
FW = 256
SCALE_QK = float(64 ** -0.25)


def _stage_lo_T(wstages):
    M = np.zeros((32, 7, 128, 128), np.float32)
    for st in range(7):
        s = 1 << st
        w = wstages[st]
        ig = np.arange(4096)
        ig = ig[(ig & s) == 0]
        widx = (ig // (2 * s)) * s + (ig % s)
        c, i = np.divmod(ig, 128)
        i2 = i + s
        M[c, st, i, i] = w[widx, 0, 0]
        M[c, st, i2, i] = w[widx, 0, 1]
        M[c, st, i, i2] = w[widx, 1, 0]
        M[c, st, i2, i2] = w[widx, 1, 1]
    return np.ascontiguousarray(M.transpose(0, 2, 1, 3).reshape(32, 128, 7 * 128))


def _stage_hi_T(wstages):
    A = np.zeros((5, 32, 128, 128), np.float32)
    for st in range(7, 12):
        s = 1 << st
        sc = s // 128
        w = wstages[st]
        ig = np.arange(4096)
        ig = ig[(ig & s) == 0]
        widx = (ig // (2 * s)) * s + (ig % s)
        ci, p = np.divmod(ig, 128)
        ci2 = ci + sc
        j, pg = np.divmod(p, 4)
        st5 = st - 7
        A[st5, j, 32 * pg + ci, 32 * pg + ci] = w[widx, 0, 0]
        A[st5, j, 32 * pg + ci2, 32 * pg + ci] = w[widx, 0, 1]
        A[st5, j, 32 * pg + ci, 32 * pg + ci2] = w[widx, 1, 0]
        A[st5, j, 32 * pg + ci2, 32 * pg + ci2] = w[widx, 1, 1]
    return np.ascontiguousarray(A.transpose(0, 2, 1, 3).reshape(5, 128, 32 * 128))


def _host_consts():
    r = np.arange(128)[:, None]
    u = np.arange(896)[None, :]
    mask = (u <= r + 384).astype(bfloat16)
    identf = np.eye(128, dtype=np.float32)
    identb = np.eye(128, dtype=bfloat16)
    i32q = np.zeros((128, 32 * 32), np.float32)
    for j in range(32):
        for pg in range(4):
            i32q[32 * pg:32 * (pg + 1), 32 * j:32 * (j + 1)] = np.eye(32, dtype=np.float32)
    ones_r = np.ones((1, 128), np.float32)
    return mask, identf, identb, i32q, ones_r


class Plan:
    def __init__(self, nc, ctx):
        self.nc = nc
        self.ctx = ctx
        self.ops = []
        self.semh = {}
        self.cnt = {}
        self.on = True

    def sem(self, name):
        self.semh[name] = self.ctx.enter_context(self.nc.semaphore(name))
        self.cnt[name] = 0
        return name

    def op(self, eng, f):
        if not self.on:
            return
        self.ops.append((eng, f))

    def opi(self, eng, f, sem, k=1):
        if not self.on:
            return self.cnt[sem]
        h = self.semh[sem]
        self.cnt[sem] += k
        self.ops.append((eng, lambda e, f=f, h=h, k=k: f(e).then_inc(h, k)))
        return self.cnt[sem]

    def wait(self, eng, sem, val):
        if val <= 0 or not self.on:
            return
        h = self.semh[sem]
        self.ops.append((eng, lambda e, h=h, v=val: e.wait_ge(h, v)))

    def waitc(self, eng, sem):
        self.wait(eng, sem, self.cnt[sem])

    def replay(self):
        nc = self.nc
        ops = self.ops
        with nc.Block() as block:
            @block.tensor
            def _(t):
                for e, f in ops:
                    if e == 'T':
                        f(t)

            @block.vector
            def _(v):
                for e, f in ops:
                    if e == 'V':
                        f(v)

            @block.scalar
            def _(c):
                for e, f in ops:
                    if e == 'C':
                        f(c)

            @block.sync
            def _(s):
                for e, f in ops:
                    if e == 'S':
                        f(s)

            @block.gpsimd
            def _(g):
                for e, f in ops:
                    if e == 'G':
                        f(g)


_PH = {'loads': 0, 'bhichain': 1, 'blochain': 2, 'upfold': 3, 'downfold': 4,
       'build': 5, 'ln': 6, 'proj': 7, 'attn': 8, 'full': 9, 'dbg': 9}


def build_nc(limit='full'):
    _rank = _PH[limit]
    nc = bass.Bass("TRN2", target_bir_lowering=False, dynamic_dma_scratch_size=2048)
    ctx = ExitStack()

    dt_ = nc.dram_tensor
    x_d = dt_("x", [BPC, S, E], F32, kind="ExternalInput").ap()
    out_d = dt_("out", [BPC, S, E], F32, kind="ExternalOutput").ap()
    wq_d = dt_("wq", [128, KC * D], F32, kind="ExternalInput").ap()
    wk_d = dt_("wk", [128, KC * D], F32, kind="ExternalInput").ap()
    wv1_d = dt_("wv1", [128, KC * D], F32, kind="ExternalInput").ap()
    wv2_d = dt_("wv2", [D, E], F32, kind="ExternalInput").ap()
    wexp_d = dt_("wexp", [E, H], F32, kind="ExternalInput").ap()
    wcon_d = dt_("wcon", [H, E], F32, kind="ExternalInput").ap()
    bexp_d = dt_("bexp", [1, H], F32, kind="ExternalInput").ap()
    ubias_d = dt_("ubias", [1, H], F32, kind="ExternalInput").ap()
    bcon_d = dt_("bcon", [1, E], F32, kind="ExternalInput").ap()
    dbias_d = dt_("dbias", [1, E], F32, kind="ExternalInput").ap()
    mloU_d = dt_("mloU", [32, 128, 7 * 128], F32, kind="ExternalInput").ap()
    mloD_d = dt_("mloD", [32, 128, 7 * 128], F32, kind="ExternalInput").ap()
    mhiU_d = dt_("mhiU", [5, 128, 32 * 128], F32, kind="ExternalInput").ap()
    mhiD_d = dt_("mhiD", [5, 128, 32 * 128], F32, kind="ExternalInput").ap()
    mask_d = dt_("maskc", [128, 896], BF16, kind="ExternalInput").ap()
    identf_d = dt_("identf", [128, 128], F32, kind="ExternalInput").ap()
    identb_d = dt_("identb", [128, 128], BF16, kind="ExternalInput").ap()
    i32q_d = dt_("i32q", [128, 32 * 32], F32, kind="ExternalInput").ap()
    ones_d = dt_("onesr", [1, 128], F32, kind="ExternalInput").ap()
    swz_d = dt_("swz", [H, FW], F32R).ap()
    swzb_d = dt_("swzb", [H, 2], F32R).ap()
    wcpg_d = dt_("wcpg", [H, E], F32R).ap()
    v2h_d = dt_("v2h", [1, H], F32).ap()
    dbg_xn = dt_("dbg_xn", [128, KC * S], BF16, kind="ExternalOutput").ap()
    dbg_wup = dt_("dbg_wup", [128, KC * H], BF16, kind="ExternalOutput").ap()
    dbg_ldn = dt_("dbg_ldn", [128, C32 * E], BF16, kind="ExternalOutput").ap()
    dbg_bp = dt_("dbg_bp", [128, 32], F32, kind="ExternalOutput").ap()
    dbg_bs = dt_("dbg_bs", [128, E], F32, kind="ExternalOutput").ap()
    dbg_bld = dt_("dbg_bld", [128, C32 * 128], F32, kind="ExternalOutput").ap()
    dbg_bhd = dt_("dbg_bhd", [128, C32 * 128], F32, kind="ExternalOutput").ap()
    dbg_upg = dt_("dbg_upg", [128, C32 * FW], F32, kind="ExternalOutput").ap()
    dbg_ucm = dt_("dbg_ucm", [128, 16 * FW], F32, kind="ExternalOutput").ap()
    dbg_q = dt_("dbg_q", [64, S], BF16, kind="ExternalOutput").ap()
    dbg_k = dt_("dbg_k", [64, S], BF16, kind="ExternalOutput").ap()
    dbg_vt = dt_("dbg_vt", [128, NTT * D], BF16, kind="ExternalOutput").ap()
    dbg_o = dt_("dbg_o", [64, S], BF16, kind="ExternalOutput").ap()
    dbg_p = dt_("dbg_p", [128, S], BF16, kind="ExternalOutput").ap()
    dbg_a = dt_("dbg_a", [128, C32 * TW], BF16, kind="ExternalOutput").ap()
    blod_sp = dt_("blod_sp", [128, C32 * 128], F32R).ap()
    bhid_sp = dt_("bhid_sp", [128, C32 * 128], F32R).ap()

    sbm = lambda cx, name, shape, dtype: cx.enter_context(nc.sbuf_tensor(name, shape, dtype))
    sbr = lambda cx, name, shape, dtype: cx.enter_context(nc.sbuf_tensor(name, shape, dtype, side="right"))
    psm = lambda cx, name, shape, dtype: cx.enter_context(nc.psum_tensor(name, shape, dtype))

    wup = sbm(ctx, "wup", [128, KC * H], BF16)
    idb = sbm(ctx, "idb", [128, 128], BF16)
    biasp = sbm(ctx, "biasp", [128, 32], F32)
    bsbc = sbm(ctx, "bsbc", [128, E], F32)
    wqsb = sbm(ctx, "wqsb", [128, KC * D], BF16)
    wksb = sbm(ctx, "wksb", [128, KC * D], BF16)
    wv1sb = sbm(ctx, "wv1sb", [128, KC * D], BF16)
    wv2sb = sbm(ctx, "wv2sb", [64, E], BF16)
    diagq = sbm(ctx, "diagq", [128, 128], BF16)
    lnv = sbm(ctx, "lnv", [128, 16], F32)
    attv = sbm(ctx, "attv", [128, 16], F32)

    pb = [psm(ctx, f"pb{i}", [128, 512], F32) for i in range(6)]
    pT = [psm(ctx, f"pT{i}", [128, 1024], BF16) for i in range(2)]
    pTf = [b[:].bitcast(F32) for b in pT]

    P = Plan(nc, ctx)
    for s_ in ["bld", "bmm", "bev", "bswz", "bg",
               "sx", "ssq", "sst", "sxn", "sxt", "stt",
               "spj", "ssc", "smx", "sex", "sdg", "spt", "spte", "sot",
               "sup", "sgl", "sdn", "soev", "sxr", "sost"]:
        P.sem(s_)

    P.opi('V', lambda e: e.memset(lnv[:, 15:16], 1e-5), "bev", 1)
    P.opi('S', lambda e: e.dma_start(out=idf[:], in_=identf_d[:]), "bld", 16)
    P.opi('S', lambda e: e.dma_start(out=idb[:], in_=identb_d[:]), "bld", 16)
    P.opi('S', lambda e: e.dma_start(out=onesr[:], in_=ones_d[:]), "bld", 16)
    P.opi('G', lambda e: e.dma_start(out=wqsb[:], in_=wq_d[:], max_dma_last_dim=4096), "bg", 16)
    P.opi('G', lambda e: e.dma_start(out=wksb[:], in_=wk_d[:], max_dma_last_dim=4096), "bg", 16)
    P.opi('G', lambda e: e.dma_start(out=wv1sb[:], in_=wv1_d[:], max_dma_last_dim=4096), "bg", 16)
    P.opi('G', lambda e: e.dma_start(out=wv2sb[:], in_=wv2_d[:], max_dma_last_dim=4096), "bg", 16)
    for pg in range(4):
        for jh in range(2):
            src = bass.AP(wcon_d.tensor, (pg + 4 * 16 * jh) * E, [[4 * E, 16], [128 * E, 32], [1, E]])
            dst = bass.AP(wcpg_d.tensor, (32 * pg) * E + (16 * jh) * 128 * E, [[128 * E, 16], [E, 32], [1, E]])
            P.opi('G', lambda e, s=src, d=dst: e.dma_start(out=d, in_=s, max_dma_last_dim=4096), "bg", 16)

    idf = sbm(ctx, "idf", [128, 128], F32)
    onesr = sbm(ctx, "onesr_sb", [1, 128], F32)
    P.opi('S', lambda e: e.dma_start(out=idf[:], in_=identf_d[:]), "bld", 16)

    def emit_chains(cx, bt, out_lo, out_hi):
        mhist = [sbm(cx, f"mhist{bt}{i}", [128, C32 * 128], F32) for i in range(1)]
        i32qsb = sbm(cx, f"i32qsb{bt}", [128, 32 * 32], F32)
        mlo8 = sbm(cx, f"mlo8{bt}", [128, 4 * 896], F32)
        chb8 = sbm(cx, f"chb8{bt}", [128, 4 * 128], F32)
        cqall = sbm(cx, f"cqall{bt}", [128, 32 * 32], F32)
        cqTall = sbm(cx, f"cqTall{bt}", [128, 32 * 32], F32)
        msrc = mhiU_d if bt == 0 else mhiD_d
        mlo_src = mloU_d if bt == 0 else mloD_d
        P.opi('S', lambda e, i32qsb=i32qsb: e.dma_start(out=i32qsb[:], in_=i32q_d[:]), "bld", 16)
        P.opi('V', lambda e, out_hi=out_hi: e.memset(out_hi[:].bitcast(F32), 0.0), "bev", 1)
        P.waitc('V', "bld")
        P.waitc('V', "bmm")
        P.opi('V', lambda e, cqall=cqall, i32qsb=i32qsb: e.tensor_copy(cqall[:], i32qsb[:]), "bev", 1)
        P.waitc('T', "bev")
        for st in range(5):
            mh = mhist[0]
            P.wait('S', "bmm", P.cnt["bmm"])
            P.opi('S', lambda e, mh=mh, msrc=msrc, st=st: e.dma_start(out=mh[:], in_=msrc[st]), "bld", 16)
            P.waitc('T', "bld")
            banks = (pb[0], pb[1]) if st % 2 == 0 else (pb[2], pb[3])
            for j in range(C32):
                bk = banks[j // 16]
                P.opi('T', lambda e, bk=bk, mh=mh, j=j, cqall=cqall: e.matmul(
                    bk[:, 32 * (j % 16):32 * (j % 16 + 1)], mh[:, 128 * j:128 * (j + 1)],
                    cqall[:, 32 * j:32 * (j + 1)], start=True, stop=True), "bmm", 1)
            P.waitc('V', "bmm")
            P.opi('V', lambda e, b=banks[0], cqall=cqall: e.tensor_copy(cqall[:, 0:512], b[:, 0:512]), "bev", 1)
            P.opi('V', lambda e, b=banks[1], cqall=cqall: e.tensor_copy(cqall[:, 512:1024], b[:, 0:512]), "bev", 1)
            P.waitc('T', "bev")
        if bt == 0:
            P.opi('V', lambda e, cqTall=cqTall, cqall=cqall: e.transpose(cqTall[:], cqall[:]), "bev", 1)
            P.waitc('V', "bev")
        srcq = cqTall if bt == 0 else cqall
        for j in range(C32):
            for pg in range(4):
                P.opi('V', lambda e, j=j, pg=pg, d=out_hi, s=srcq: e.tensor_copy(
                    d[32 * pg:32 * (pg + 1), 128 * j + 32 * pg:128 * j + 32 * (pg + 1)],
                    s[32 * pg:32 * (pg + 1), 32 * j:32 * (j + 1)]), "bev", 1)
        for g in range(8):
            P.wait('S', "bmm", P.cnt["bmm"])
            for cc in range(4):
                P.opi('S', lambda e, cc=cc, s=mlo_src, g=g, mlo8=mlo8: e.dma_start(
                    out=mlo8[:, 896 * cc:896 * (cc + 1)], in_=s[4 * g + cc]), "bld", 16)
            P.waitc('T', "bld")
            for st in range(7):
                bk = pb[st % 2]
                for cc in range(4):
                    rhs = idf[:] if st == 0 else chb8[:, 128 * cc:128 * (cc + 1)]
                    P.opi('T', lambda e, bk=bk, cc=cc, st=st, r=rhs, mlo8=mlo8: e.matmul(
                        bk[:, 128 * cc:128 * (cc + 1)],
                        mlo8[:, 896 * cc + 128 * st:896 * cc + 128 * (st + 1)],
                        r, start=True, stop=True), "bmm", 1)
                P.waitc('V', "bmm")
                if st < 6 or bt == 0:
                    dst = chb8[:, 0:512]
                else:
                    dst = out_lo[:, 512 * g:512 * (g + 1)]
                P.opi('V', lambda e, b=bk, d=dst: e.tensor_copy(d, b[:, 0:512]), "bev", 1)
                P.waitc('T', "bev")
            if bt == 0:
                for cc in range(4):
                    P.opi('T', lambda e, cc=cc, chb8=chb8: e.transpose(
                        pb[4][:, 128 * cc:128 * (cc + 1)], chb8[:, 128 * cc:128 * (cc + 1)], idf[:]), "bmm", 1)
                P.waitc('V', "bmm")
                P.opi('V', lambda e, g=g: e.tensor_copy(out_lo[:, 512 * g:512 * (g + 1)], pb[4][:, 0:512]), "bev", 1)
                P.waitc('T', "bev")

    cxA = ExitStack()
    bloUT = sbr(cxA, "bloUT", [128, C32 * 128], F32R)
    bhiUblk = sbr(cxA, "bhiUblk", [128, C32 * 128], F32R)
    P.on = _rank >= 1
    cx1c = ExitStack()
    emit_chains(cx1c, 0, bloUT, bhiUblk)
    P.waitc('S', "bev")
    P.waitc('T', "bev")
    P.waitc('V', "bmm")
    cx1c.close()

    P.on = _rank >= 3
    cx2 = ExitStack()
    wx = sbr(cx2, "wx", [128, H], F32)
    p1ts = sbr(cx2, "p1ts", [128, C32 * FW], F32R)
    p2t = sbr(cx2, "p2t", [128, C32 * FW], F32R)
    rowsb = sbr(cx2, "rowsb", [1, H], F32)
    bvec = sbr(cx2, "bvec", [128, 2 * C32], F32R)
    p1tbia = sbr(cx2, "p1tbia", [128, C32 * 2], F32R)
    p1sbia = sbr(cx2, "p1sbia", [128, C32 * 2], F32R)

    P.opi('S', lambda e: e.dma_start(out=rowsb[:], in_=bexp_d[:]), "bld", 16)
    P.waitc('T', "bld")
    for c in range(C32):
        P.opi('T', lambda e, c=c: e.matmul(
            pb[3][:, c:c + 1], rowsb[0:1, 128 * c:128 * (c + 1)], onesr[0:1, 0:1],
            start=True, stop=True), "bmm", 1)
    P.waitc('V', "bmm")
    P.opi('V', lambda e: e.memset(bvec[:].bitcast(F32), 0.0), "bev", 1)
    P.waitc('V', "bev")
    P.opi('V', lambda e: e.tensor_copy(
        bass.AP(bvec, 0, [[2 * C32, 128], [2, 32]]), pb[3][:, 0:32]), "bev", 1)
    P.waitc('T', "bev")
    for c in range(C32):
        P.opi('T', lambda e, c=c: e.matmul(
            pb[4][:, 2 * c:2 * (c + 1)], bloUT[:, 128 * c:128 * (c + 1)], bvec[:, 2 * c:2 * (c + 1)],
            start=True, stop=True), "bmm", 1)
    P.waitc('V', "bmm")
    P.opi('V', lambda e: e.tensor_copy(p1tbia[:], pb[4][:, 0:64]), "bev", 1)
    P.waitc('S', "bev")
    P.opi('S', lambda e: e.dma_start(
        out=bass.AP(swzb_d.tensor, 0, [[2, 128], [128 * 2, 32], [1, 2]]),
        in_=bass.AP(p1tbia, 0, [[C32 * 2, 128], [2, 32], [1, 2]])), "bswz", 16)
    P.waitc('S', "bswz")
    for pg in range(4):
        P.opi('S', lambda e, pg=pg: e.dma_start(
            out=bass.AP(p1sbia, (32 * pg) * (C32 * 2), [[C32 * 2, 32], [2, 32], [1, 2]]),
            in_=bass.AP(swzb_d.tensor, pg * 2, [[128 * 2, 32], [4 * 2, 32], [1, 2]])), "bswz", 16)
    P.waitc('T', "bswz")
    for j in range(C32):
        P.opi('T', lambda e, j=j: e.matmul(
            pb[5][:, 2 * j:2 * (j + 1)], bhiUblk[:, 128 * j:128 * (j + 1)],
            p1sbia[:, 2 * j:2 * (j + 1)], start=True, stop=True), "bmm", 1)
    P.waitc('V', "bmm")
    P.opi('V', lambda e: e.tensor_copy(p1sbia[:], pb[5][:, 0:64]), "bev", 1)
    P.waitc('S', "bev")
    for pg in range(4):
        P.opi('S', lambda e, pg=pg: e.dma_start(
            out=bass.AP(v2h_d.tensor, pg, [[128, 32], [4, 32], [1, 1]]),
            in_=bass.AP(p1sbia, (32 * pg) * 64, [[64, 32], [2, 32], [1, 1]]).bitcast(F32),
            allow_slow_non_contiguous=True), "bswz", 16)

    for w in range(4):
        for ki in range(2):
            kappa = 2 * w + ki
            P.wait('S', "bmm", P.cnt["bmm"])
            P.opi('S', lambda e, kappa=kappa: e.dma_start(out=wx[:], in_=wexp_d[128 * kappa:128 * (kappa + 1), :]), "bld", 16)
            P.waitc('T', "bld")
            for c4 in range(8):
                tbk = pT[c4 % 2]
                if w + ki + c4 > 0:
                    P.wait('T', "bev", P.cnt["bev"] - 1)
                for cc in range(4):
                    P.opi('T', lambda e, tbk=tbk, c4=c4, cc=cc: e.transpose(
                        tbk[:].bitcast(F32)[:, 128 * cc:128 * (cc + 1)],
                        wx[:, 128 * (4 * c4 + cc):128 * (4 * c4 + cc + 1)], idf[:]), "bmm", 1)
                P.waitc('V', "bmm")
                P.opi('V', lambda e, tbk=tbk, c4=c4, ki=ki: e.tensor_copy(
                    bass.AP(p1ts, FW * (4 * c4) + 128 * ki, [[C32 * FW, 128], [FW, 4], [1, 128]]),
                    tbk[:].bitcast(F32)[:, 0:512]), "bev", 1)
        P.waitc('T', "bev")
        for c in range(C32):
            P.wait('T', "bev", P.cnt["bev"] - 1)
            P.opi('T', lambda e, c=c: e.matmul(
                pb[4 + c % 2][:, 0:FW], bloUT[:, 128 * c:128 * (c + 1)], p1ts[:, FW * c:FW * (c + 1)],
                start=True, stop=True), "bmm", 1)
            P.waitc('V', "bmm")
            P.opi('V', lambda e, c=c: e.tensor_copy(p1ts[:, FW * c:FW * (c + 1)], pb[4 + c % 2][:, 0:FW]), "bev", 1)
        P.waitc('S', "bev")
        P.waitc('S', "bswz")
        P.opi('S', lambda e: e.dma_start(
            out=bass.AP(swz_d.tensor, 0, [[FW, 128], [128 * FW, 32], [1, FW]]),
            in_=p1ts[:]), "bswz", 16)
        P.waitc('S', "bswz")
        for pg in range(4):
            P.opi('S', lambda e, pg=pg: e.dma_start(
                out=bass.AP(p1ts, (32 * pg) * (C32 * FW), [[C32 * FW, 32], [FW, 32], [1, FW]]),
                in_=bass.AP(swz_d.tensor, pg * FW, [[128 * FW, 32], [4 * FW, 32], [1, FW]])), "bswz", 16)
        P.waitc('T', "bswz")
        for j in range(C32):
            P.wait('T', "bev", P.cnt["bev"] - 1)
            P.opi('T', lambda e, j=j: e.matmul(
                pb[4 + j % 2][:, 0:FW], bhiUblk[:, 128 * j:128 * (j + 1)], p1ts[:, FW * j:FW * (j + 1)],
                start=True, stop=True), "bmm", 1)
            P.waitc('V', "bmm")
            P.opi('V', lambda e, j=j: e.tensor_copy(p2t[:, FW * j:FW * (j + 1)], pb[4 + j % 2][:, 0:FW]), "bev", 1)
        P.waitc('T', "bev")
        for j2 in range(16):
            tbk = pT[j2 % 2]
            if j2 >= 2:
                P.wait('T', "bev", P.cnt["bev"] - 1)
            for k4 in range(4):
                jj = 2 * j2 + k4 // 2
                sub = k4 % 2
                P.opi('T', lambda e, tbk=tbk, jj=jj, sub=sub, k4=k4: e.transpose(
                    tbk[:].bitcast(F32)[:, 128 * k4:128 * (k4 + 1)],
                    p2t[:].bitcast(F32)[:, FW * jj + 128 * sub:FW * jj + 128 * (sub + 1)], idf[:]), "bmm", 1)
            P.waitc('V', "bmm")
            P.opi('V', lambda e, tbk=tbk, w=w, j2=j2: e.tensor_copy(
                bass.AP(wup, (2 * w) * H + 128 * (2 * j2), [[KC * H, 128], [128, 2], [H, 2], [1, 128]]),
                tbk[:].bitcast(F32)[:, 0:512].bitcast(BF16) if False else tbk[:].bitcast(F32)[:, 0:512]), "bev", 1)

    P.waitc('S', "bswz")
    P.wait('S', "bmm", P.cnt["bmm"])
    P.opi('S', lambda e: e.dma_start(out=rowsb[:], in_=v2h_d[:]), "bld", 16)
    P.waitc('T', "bld")
    for c in range(C32):
        P.opi('T', lambda e, c=c: e.matmul(
            pb[3][:, c:c + 1], rowsb[0:1, 128 * c:128 * (c + 1)], onesr[0:1, 0:1],
            start=True, stop=True), "bmm", 1)
    P.wait('S', "bmm", P.cnt["bmm"])
    P.opi('S', lambda e: e.dma_start(out=rowsb[:], in_=ubias_d[:]), "bld", 16)
    P.waitc('T', "bld")
    for c in range(C32):
        P.opi('T', lambda e, c=c: e.matmul(
            pb[4][:, c:c + 1], rowsb[0:1, 128 * c:128 * (c + 1)], onesr[0:1, 0:1],
            start=True, stop=True), "bmm", 1)
    P.waitc('V', "bmm")
    P.opi('V', lambda e: e.tensor_copy(biasp[:], pb[3][:, 0:32]), "bev", 1)
    P.waitc('V', "bev")
    P.opi('V', lambda e: e.tensor_add(biasp[:], biasp[:], pb[4][:, 0:32]), "bev", 1)
    P.wait('S', "bev", P.cnt["bev"])
    P.opi('S', lambda e: e.dma_start(out=rowsb[0:1, 0:E], in_=bcon_d[:]), "bld", 16)
    P.opi('S', lambda e: e.dma_start(out=rowsb[0:1, E:2 * E], in_=dbias_d[:]), "bld", 16)
    P.waitc('V', "bld")
    P.opi('V', lambda e: e.tensor_add(rowsb[0:1, 0:E], rowsb[0:1, 0:E], rowsb[0:1, E:2 * E]), "bev", 1)
    P.waitc('T', "bev")
    for hf in range(2):
        P.opi('T', lambda e, hf=hf: e.matmul(
            pb[5][:, 0:512], onesr[0:1, 0:128], rowsb[0:1, 512 * hf:512 * (hf + 1)],
            start=True, stop=True), "bmm", 1)
        P.waitc('V', "bmm")
        P.opi('V', lambda e, hf=hf: e.tensor_copy(bsbc[:, 512 * hf:512 * (hf + 1)], pb[5][:, 0:512]), "bev", 1)
        P.waitc('T', "bev")
    P.waitc('S', "bev")
    P.waitc('V', "bswz")
    cx2.close()
    cxA.close()

    P.on = _rank >= 4
    ldn = sbr(ctx, "ldn", [128, C32 * E], BF16)
    cx3o = ExitStack()
    bloD2 = sbr(cx3o, "bloD2", [128, C32 * 128], F32R)
    bhiD2 = sbr(cx3o, "bhiD2", [128, C32 * 128], F32R)
    cx3c = ExitStack()
    emit_chains(cx3c, 1, bloD2, bhiD2)
    P.waitc('S', "bev")
    P.waitc('T', "bev")
    P.waitc('V', "bmm")
    cx3c.close()
    cx3 = ExitStack()
    upg = sbr(cx3, "upg", [128, C32 * FW], F32R)
    ucmh = sbr(cx3, "ucmh", [128, 16 * FW], F32R)
    wct = [sbr(cx3, f"wct{i}", [128, FW], F32R) for i in range(2)]
    P.waitc('S', "bg")
    import os as _os
    _ndw = int(_os.environ.get("KDW", "4"))
    _dwsec = _os.environ.get("KDWS", "ul")
    for w in range(_ndw):
        for j in (range(C32) if 'u' in _dwsec else []):
            b = wct[j % 2]
            P.wait('S', "bmm", P.cnt["bmm"] - 1)
            P.opi('S', lambda e, b=b, j=j, w=w: e.dma_start(
                out=b[:], in_=bass.AP(wcpg_d.tensor, (128 * j) * E + FW * w, [[E, 128], [1, FW]])), "bld", 16)
            P.waitc('T', "bld")
            P.wait('T', "bev", P.cnt["bev"] - 1)
            P.opi('T', lambda e, j=j, b=b: e.matmul(
                pb[4 + j % 2][:, 0:FW], bhiD2[:, 128 * j:128 * (j + 1)], b[:],
                start=True, stop=True), "bmm", 1)
            P.waitc('V', "bmm")
            P.opi('V', lambda e, j=j: e.tensor_copy(upg[:, FW * j:FW * (j + 1)], pb[4 + j % 2][:, 0:FW]), "bev", 1)
        P.waitc('S', "bev")
        P.waitc('S', "bswz")
        for pg in range(4):
            P.opi('S', lambda e, pg=pg: e.dma_start(
                out=bass.AP(swz_d.tensor, pg * FW, [[128 * FW, 32], [4 * FW, 32], [1, FW]]),
                in_=bass.AP(upg, (32 * pg) * (C32 * FW), [[C32 * FW, 32], [FW, 32], [1, FW]])), "bswz", 16)
        P.waitc('S', "bswz")
        for ch in (range(1 if 'h' in _dwsec else 2) if 'l' in _dwsec else []):
            P.wait('S', "bmm", P.cnt["bmm"])
            for qh in range(2):
                P.opi('S', lambda e, ch=ch, qh=qh: e.dma_start(
                    out=bass.AP(ucmh, (8 * qh) * FW, [[16 * FW, 128], [FW, 8], [1, FW]]),
                    in_=bass.AP(swz_d.tensor, (16 * ch + 8 * qh) * 128 * FW, [[FW, 128], [128 * FW, 8], [1, FW]])), "bld", 16)
            P.waitc('T', "bld")
            for cc in (range(16) if 'm' in _dwsec or 'l' == _dwsec.replace('x','') and False else range(16)):
                c = 16 * ch + cc
                if 'x' in _dwsec:
                    continue
                P.wait('T', "bev", P.cnt["bev"] - 1)
                P.opi('T', lambda e, c=c, cc=cc: e.matmul(
                    pb[4 + cc % 2][:, 0:FW], bloD2[:, 128 * c:128 * (c + 1)], ucmh[:, FW * cc:FW * (cc + 1)],
                    start=True, stop=True), "bmm", 1)
                if 'y' in _dwsec:
                    continue
                P.waitc('V', "bmm")
                _dst = (upg if 'z' in _dwsec else ldn)
                if 'z' in _dwsec:
                    P.opi('V', lambda e, cc=cc: e.tensor_copy(
                        upg[:].bitcast(F32)[:, FW * cc:FW * (cc + 1)], pb[4 + cc % 2][:, 0:FW]), "bev", 1)
                elif 'w' in _dwsec:
                    P.opi('V', lambda e, cc=cc: e.tensor_copy(
                        ldn[:, FW * cc:FW * (cc + 1)], pb[4 + cc % 2][:, 0:FW]), "bev", 1)
                elif 'a' in _dwsec:
                    P.opi('V', lambda e, c=c: e.tensor_copy(
                        ldn[:, 2 * FW * c:2 * FW * c + FW], pb[4 + c % 2][:, 0:FW]), "bev", 1)
                elif 'b' in _dwsec:
                    P.opi('V', lambda e, c=c: e.tensor_copy(
                        ldn[:, E * c:E * c + FW], pb[4 + c % 2][:, 0:FW]), "bev", 1)
                else:
                    P.opi('V', lambda e, c=c, w=w, cc=cc: e.tensor_copy(
                        ldn[:, E * c + FW * w:E * c + FW * (w + 1)], pb[4 + cc % 2][:, 0:FW]), "bev", 1)
    P.waitc('S', "bev")
    P.waitc('V', "bswz")
    if limit in ('dbg', 'downfold'):
        P.opi('S', lambda e: e.dma_start(out=dbg_bld[:], in_=bloD2[:].bitcast(F32)), "bswz", 16)
        P.opi('S', lambda e: e.dma_start(out=dbg_bhd[:], in_=bhiD2[:].bitcast(F32)), "bswz", 16)
        P.opi('S', lambda e: e.dma_start(out=dbg_upg[:], in_=upg[:].bitcast(F32)), "bswz", 16)
        P.opi('S', lambda e: e.dma_start(out=dbg_ucm[:], in_=ucmh[:].bitcast(F32)), "bswz", 16)
        P.waitc('S', "bswz")
    cx3.close()
    cx3o.close()

    maskb = sbr(ctx, "maskb", [128, 896], BF16)
    xn = sbr(ctx, "xn", [128, KC * S], BF16)
    abuf = sbr(ctx, "abuf", [128, C32 * TW], BF16)
    qbuf = sbr(ctx, "qbuf", [64, S], BF16)
    kbuf = sbr(ctx, "kbuf", [64, S], BF16)
    o64t = sbr(ctx, "o64t", [64, S], BF16)
    vt = sbr(ctx, "vt", [128, NTT * D], BF16)
    pbuf = sbr(ctx, "pbuf", [128, S], BF16)
    ptb = sbr(ctx, "ptb", [128, 2 * 512], BF16)
    xtile = [sbr(ctx, f"xtile{i}", [128, E], F32) for i in range(2)]
    xnt = [sbr(ctx, f"xnt{i}", [128, E], BF16) for i in range(2)]
    ost = [sbr(ctx, f"ost{i}", [128, E], F32) for i in range(2)]

    P.on = True
    P.opi('S', lambda e: e.dma_start(out=maskb[:], in_=mask_d[:]), "bld", 16)
    P.waitc('T', "bev")
    P.waitc('C', "bev")
    P.waitc('V', "bmm")
    P.waitc('C', "bld")
    P.waitc('V', "bld")
    P.waitc('C', "bg")
    P.waitc('T', "bg")

    nseq = 0 if _rank <= 5 else (1 if limit in ('ln', 'proj', 'attn', 'ffn1') else BPC)
    for b in range(nseq):
        P.wait('C', "sost", P.cnt["sost"])
        for i in range(NTT):
            sl = i % 2
            xt = xtile[sl]
            gi = b * NTT + i
            if gi >= 2:
                P.wait('S', "sxt", gi - 1)
            if gi >= 2:
                P.wait('S', "ssq", P.cnt["ssq"] - 3)
            P.opi('S', lambda e, xt=xt, b=b, i=i: e.dma_start(
                out=xt[:], in_=x_d[b, 128 * i:128 * (i + 1), :]), "sx", 16)
            P.waitc('V', "sx")
            P.waitc('C', "sx")
            c0, c1, c2, c3, c5 = sl * 8, sl * 8 + 1, sl * 8 + 2, sl * 8 + 3, sl * 8 + 5
            P.opi('V', lambda e, xt=xt, c0=c0: e.reduce_sum(lnv[:, c0:c0 + 1], xt[:], axis=AX.X), "sst", 1)
            P.waitc('V', "sst")
            if gi >= 2:
                P.wait('C', "stt", P.cnt["stt"] - 8)
            P.opi('C', lambda e, xt=xt, sq=xnt[sl], c1=c1: e.activation(
                sq[:], xt[:], AF.Square, accum_out=lnv[:, c1:c1 + 1]), "ssq", 1)
            P.waitc('V', "ssq")
            P.opi('V', lambda e, c0=c0, c2=c2: e.tensor_scalar_mul(lnv[:, c2:c2 + 1], lnv[:, c0:c0 + 1], -1.0 / E), "sst", 1)
            P.opi('V', lambda e, c1=c1, c3=c3: e.tensor_scalar_mul(lnv[:, c3:c3 + 1], lnv[:, c1:c1 + 1], 1.0 / E), "sst", 1)
            P.waitc('V', "sst")
            c4 = sl * 8 + 4
            P.opi('V', lambda e, c2=c2, c4=c4: e.tensor_mul(lnv[:, c4:c4 + 1], lnv[:, c2:c2 + 1], lnv[:, c2:c2 + 1]), "sst", 1)
            P.waitc('V', "sst")
            P.opi('V', lambda e, c3=c3, c4=c4: e.tensor_sub(lnv[:, c3:c3 + 1], lnv[:, c3:c3 + 1], lnv[:, c4:c4 + 1]), "sst", 1)
            P.waitc('V', "sst")
            P.waitc('C', "sst")
            P.opi('C', lambda e, c3=c3, c5=c5: e.activation(lnv[:, c5:c5 + 1], lnv[:, c3:c3 + 1], AF.Sqrt, bias=lnv[:, 15:16]), "ssq", 1)
            P.waitc('V', "ssq")
            P.opi('V', lambda e, c5=c5: e.reciprocal(lnv[:, c5:c5 + 1], lnv[:, c5:c5 + 1]), "sst", 1)
            P.waitc('V', "sst")
            P.opi('V', lambda e, xt=xt, xo=xnt[sl], c2=c2, c5=c5: e.tensor_scalar(
                xo[:], xt[:], lnv[:, c2:c2 + 1], lnv[:, c5:c5 + 1], ALU.add, ALU.mult), "sxt", 1)
            P.waitc('T', "sxt")
            P.wait('T', "sxn", P.cnt["sxn"] - 1)
            tb = pT[sl]
            for kappa in range(KC):
                P.opi('T', lambda e, tb=tb, xo=xnt[sl], kappa=kappa: e.transpose(
                    tb[:, 128 * kappa:128 * (kappa + 1)], xo[:, 128 * kappa:128 * (kappa + 1)], idb[:]), "stt", 1)
            P.waitc('V', "stt")
            P.opi('V', lambda e, tb=tb, i=i: e.tensor_copy(
                bass.AP(xn, 128 * i, [[KC * S, 128], [S, 8], [1, 128]]), tb[:]), "sxn", 1)
        if limit == 'ln':
            continue
        P.waitc('T', "sxn")
        for tw in range(4):
            for which, wsb_, dst in ((0, wqsb, qbuf), (1, wksb, kbuf), (2, wv1sb, None)):
                bank = pb[which]
                for kappa in range(KC):
                    P.opi('T', lambda e, bank=bank, w_=wsb_, kappa=kappa, tw=tw: e.matmul(
                        bank[0:64, 0:512], w_[:, D * kappa:D * (kappa + 1)],
                        bass.AP(xn, S * kappa + 512 * tw, [[KC * S, 128], [1, 512]]),
                        start=(kappa == 0), stop=(kappa == KC - 1)), "ssc", 1)
                P.waitc('C', "ssc")
                if which < 2:
                    P.opi('C', lambda e, bank=bank, dst=dst, tw=tw: e.activation(
                        dst[0:64, 512 * tw:512 * (tw + 1)], bank[0:64, 0:512], AF.Copy, scale=SCALE_QK), "spj", 1)
                else:
                    P.opi('C', lambda e, bank=bank: e.activation(
                        pbuf[0:64, 0:512], bank[0:64, 0:512], AF.Copy), "spj", 1)
            P.waitc('T', "spj")
            for s4 in range(4):
                P.opi('T', lambda e, s4=s4: e.transpose(
                    pT[0][:, 64 * s4:64 * (s4 + 1)], pbuf[0:64, 128 * s4:128 * (s4 + 1)], idb[0:64, 0:64]), "stt", 1)
            P.waitc('V', "stt")
            P.wait('V', "sot", P.cnt["sot"])
            P.opi('V', lambda e, tw=tw: e.tensor_copy(vt[:, 256 * tw:256 * (tw + 1)], pT[0][:, 0:256]), "spj", 1)
            P.waitc('T', "spj")
        if limit == 'proj':
            continue
        blkctr = getattr(P, "_blkctr", 0)
        if not hasattr(P, "_sexbase"):
            P._sexbase = P.cnt["sex"]
        P.waitc('T', "soev")
        for qi in range(NTT):
            bd = qi // 4
            o = 128 * (qi % 4)
            nbk = bd + 1
            P.waitc('V', "sex")
            P.waitc('V', "spt")
            myblk = []
            for blk in range(nbk):
                bank = pb[blkctr % 4]
                myblk.append(bank)
                P.wait('T', "sex", P._sexbase + blkctr - 3)
                blkctr += 1
                P.opi('T', lambda e, bank=bank, qi=qi, blk=blk: e.matmul(
                    bank[:, 0:512], qbuf[0:64, 128 * qi:128 * (qi + 1)], kbuf[0:64, 512 * blk:512 * (blk + 1)],
                    start=True, stop=True), "ssc", 1)
                P.waitc('V', "ssc")
                P.opi('V', lambda e, bank=bank, blk=blk: e.reduce_max(
                    attv[:, blk:blk + 1], bank[:, 0:512], axis=AX.X), "smx", 1)
            P.waitc('V', "smx")
            P.opi('V', lambda e, nbk=nbk: e.reduce_max(
                attv[:, 4:5], attv[:, 0:nbk], axis=AX.X, negate=True), "smx", 1)
            P.waitc('V', "smx")
            P.waitc('C', "smx")
            P.waitc('C', "spt")
            for blk in range(nbk):
                bank = myblk[blk]
                P.opi('C', lambda e, bank=bank, blk=blk: e.activation(
                    pbuf[:, 512 * blk:512 * (blk + 1)], bank[:, 0:512], AF.Exp,
                    bias=attv[:, 4:5], accum_out=attv[:, 8 + blk:9 + blk]), "sex", 1)
            P.waitc('V', "sex")
            P.opi('V', lambda e, bd=bd, o=o: e.tensor_mul(
                pbuf[:, 512 * bd:512 * (bd + 1)], pbuf[:, 512 * bd:512 * (bd + 1)],
                maskb[:, 384 - o:896 - o]), "sdg", 1)
            P.waitc('V', "sdg")
            P.opi('V', lambda e, bd=bd: e.reduce_sum(
                attv[:, 8 + bd:9 + bd], pbuf[:, 512 * bd:512 * (bd + 1)], axis=AX.X), "sdg", 1)
            P.waitc('V', "sdg")
            P.opi('V', lambda e, nbk=nbk: e.reduce_sum(attv[:, 13:14], attv[:, 8:8 + nbk], axis=AX.X), "sdg", 1)
            P.waitc('V', "sdg")
            P.opi('V', lambda e: e.reciprocal(attv[:, 13:14], attv[:, 13:14]), "sdg", 1)
            P.waitc('V', "sdg")
            P.opi('V', lambda e: e.tensor_scalar_mul(diagq[:], idb[:], attv[:, 13:14]), "sdg", 1)
            P.waitc('T', "sdg")
            nm = qi + 1
            for g4 in range(0, nm, 4):
                mm4 = list(range(g4, min(g4 + 4, nm)))
                ptbank = pTf[(g4 // 4) % 2]
                P.wait('T', "spte", P.cnt["spte"] - 1)
                for ii, m in enumerate(mm4):
                    P.opi('T', lambda e, ptbank=ptbank, m=m, ii=ii: e.matmul(
                        ptbank[:, 128 * ii:128 * (ii + 1)], pbuf[:, 128 * m:128 * (m + 1)], diagq[:],
                        start=True, stop=True), "spt", 1)
                P.waitc('V', "spt")
                P.wait('V', "sot", P.cnt["sot"] - 4)
                sl4 = (g4 // 4) % 2
                P.opi('V', lambda e, ptbank=ptbank, sl4=sl4, n4=len(mm4): e.tensor_copy(
                    ptb[:, 512 * sl4:512 * sl4 + 128 * n4], ptbank[:, 0:128 * n4]), "spte", 1)
                P.waitc('T', "spte")
                for ii, m in enumerate(mm4):
                    P.opi('T', lambda e, m=m, qi=qi, sl4=sl4, ii=ii: e.matmul(
                        pb[5][0:64, 0:128], vt[:, 64 * m:64 * (m + 1)], ptb[:, 512 * sl4 + 128 * ii:512 * sl4 + 128 * (ii + 1)],
                        start=(m == 0), stop=(m == qi)), "sot", 1)
            P.waitc('C', "sot")
            P.opi('C', lambda e, qi=qi: e.activation(
                o64t[0:64, 128 * qi:128 * (qi + 1)], pb[5][0:64, 0:128], AF.Copy), "sot", 1)
            P.waitc('T', "sot")
        P._blkctr = blkctr
        if limit == 'attn':
            continue
        P.waitc('T', "sex")
        P.waitc('T', "sot")
        for tw in range(NTW):
            t_id = tw
            slx = t_id % 2
            upbanks = (pb[0], pb[1], pb[4])
            for c in range(C32):
                bank = upbanks[c % 3]
                P.wait('T', "sgl", P.cnt["sgl"] - 2)
                for kappa in range(KC):
                    P.opi('T', lambda e, bank=bank, c=c, kappa=kappa, tw=tw: e.matmul(
                        bank[:, 0:TW],
                        bass.AP(wup, H * kappa + c, [[KC * H, 128], [128, 32], [32, 4]]),
                        bass.AP(xn, S * kappa + TW * tw, [[KC * S, 128], [1, TW]]),
                        start=(kappa == 0), stop=(kappa == KC - 1)), "sup", 1)
                P.waitc('C', "sup")
                P.opi('C', lambda e, bank=bank, c=c: e.activation(
                    abuf[:, TW * c:TW * (c + 1)], bank[:, 0:TW], AF.Gelu, bias=biasp[:, c:c + 1]), "sgl", 1)
            P.waitc('T', "sgl")
            if tw == 0:
                P.waitc('S', "sxn")
            P.wait('S', "soev", P.cnt["soev"] - 2)
            P.opi('S', lambda e, b=b, t_id=t_id, slx=slx: e.dma_start(
                out=xtile[slx][:], in_=x_d[b, 128 * t_id:128 * (t_id + 1), :]), "sxr", 16)
            dnbanks = (pb[2], pb[3], pb[5])
            for fo in range(2):
                bank = dnbanks[(t_id * 2 + fo) % 3]
                P.wait('T', "soev", P.cnt["soev"] - 5)
                P.opi('T', lambda e, bank=bank, t_id=t_id, fo=fo: e.matmul(
                    bank[:, 0:512], o64t[0:64, 128 * t_id:128 * (t_id + 1)], wv2sb[0:64, 512 * fo:512 * (fo + 1)],
                    start=True, stop=False), "sdn", 1)
                for c in range(C32):
                    P.opi('T', lambda e, bank=bank, c=c, tw=tw, fo=fo: e.matmul(
                        bank[:, 0:512], abuf[:, TW * c:TW * (c + 1)],
                        ldn[:, E * c + 512 * fo:E * c + 512 * (fo + 1)],
                        start=False, stop=(c == C32 - 1)), "sdn", 1)
                P.waitc('V', "sdn")
                P.waitc('V', "sxr")
                P.wait('V', "sost", 16 * (P.cnt["sost"] // 16 - 1))
                P.opi('V', lambda e, bank=bank, slx=slx, fo=fo: e.tensor_add(
                    ost[slx][:, 512 * fo:512 * (fo + 1)], bank[:, 0:512], xtile[slx][:, 512 * fo:512 * (fo + 1)]), "soev", 1)
                P.waitc('V', "soev")
                P.opi('V', lambda e, slx=slx, fo=fo: e.tensor_add(
                    ost[slx][:, 512 * fo:512 * (fo + 1)], ost[slx][:, 512 * fo:512 * (fo + 1)],
                    bsbc[:, 512 * fo:512 * (fo + 1)]), "soev", 1)
            P.waitc('S', "soev")
            P.opi('S', lambda e, b=b, t_id=t_id, slx=slx: e.dma_start(
                out=out_d[b, 128 * t_id:128 * (t_id + 1), :], in_=ost[slx][:]), "sost", 16)
            if b == 0 and tw == 0 and limit == 'dbg':
                P.waitc('S', "sgl")
                P.opi('S', lambda e: e.dma_start(out=dbg_a[:], in_=abuf[:]), "sost", 16)

    P.wait('S', "sost", P.cnt["sost"])
    P.replay()
    ctx.close()
    return nc


_NC_CACHE = {}


def _chunk_rows(w):
    w = np.asarray(w, np.float32).reshape(KC, 128, D).transpose(1, 0, 2).reshape(128, KC * D)
    return np.ascontiguousarray(w)


def kernel(L0_attn_x, L0_attn_w_q, L0_attn_w_k, L0_attn_w_v1, L0_attn_w_v2,
           L0_ffn_W_exp, L0_ffn_b_exp, L0_ffn_up_weights, L0_ffn_up_bias,
           L0_ffn_down_weights, L0_ffn_W_con, L0_ffn_b_con, L0_ffn_down_bias):
    x = np.ascontiguousarray(np.asarray(L0_attn_x, dtype=np.float32))
    mask, identf, identb, i32q, ones_r = _host_consts()
    mloU = _stage_lo_T(np.asarray(L0_ffn_up_weights, np.float32))
    mloD = _stage_lo_T(np.asarray(L0_ffn_down_weights, np.float32))
    mhiU = _stage_hi_T(np.asarray(L0_ffn_up_weights, np.float32))
    mhiD = _stage_hi_T(np.asarray(L0_ffn_down_weights, np.float32))

    common = {
        "wq": _chunk_rows(L0_attn_w_q),
        "wk": _chunk_rows(L0_attn_w_k),
        "wv1": _chunk_rows(L0_attn_w_v1),
        "wv2": np.ascontiguousarray(L0_attn_w_v2, np.float32),
        "wexp": np.ascontiguousarray(L0_ffn_W_exp, np.float32),
        "wcon": np.ascontiguousarray(L0_ffn_W_con, np.float32),
        "bexp": np.ascontiguousarray(np.asarray(L0_ffn_b_exp, np.float32).reshape(1, H)),
        "ubias": np.ascontiguousarray(np.asarray(L0_ffn_up_bias, np.float32).reshape(1, H)),
        "bcon": np.ascontiguousarray(np.asarray(L0_ffn_b_con, np.float32).reshape(1, E)),
        "dbias": np.ascontiguousarray(np.asarray(L0_ffn_down_bias, np.float32).reshape(1, E)),
        "mloU": mloU, "mloD": mloD, "mhiU": mhiU, "mhiD": mhiD,
        "maskc": mask, "identf": identf, "identb": identb,
        "i32q": i32q, "onesr": ones_r,
    }
    in_maps = []
    for core in range(NCORES):
        m = dict(common)
        m["x"] = np.ascontiguousarray(x[BPC * core:BPC * (core + 1)])
        in_maps.append(m)

    if "nc" not in _NC_CACHE:
        _NC_CACHE["nc"] = build_nc()
    nc = _NC_CACHE["nc"]
    res = run_bass_kernel_spmd(nc, in_maps, core_ids=list(range(NCORES)))
    return np.concatenate([res.results[i]["out"] for i in range(NCORES)], axis=0)


